# version 01
# baseline (speedup 1.0000x reference)
DEBUG_LN = False

import math
from contextlib import ExitStack

import numpy as np

import concourse.bass as bass
import concourse.mybir as mybir
from concourse.bass_utils import run_bass_kernel_spmd

F32 = mybir.dt.float32
BF16 = mybir.dt.bfloat16
I32 = mybir.dt.int32
AF = mybir.ActivationFunctionType
ALU = mybir.AluOpType
AX = mybir.AxisListType

D = 4096
B = 4
S = 4096
T = 2048
NC = 32
NTT = 4
NH = 32
NG = 8
LN_EPS = 1e-5
CLIP = 8.0
SCALE = 1.0 / math.sqrt(128.0)
SWEEPS = [(0, 3), (1, 2)]
RG = [[0, 1], [2, 3], [4, 5], [6, 7]]

ENGS = ("tensor", "scalar", "vector", "sync", "gpsimd")


class PB:

    def __init__(self, nc):
        self.nc = nc
        self.sems = {}
        self.ct = {}
        self.st = {}
        self.cur = None
        self.eng = None
        names = ["pe", "act", "dve", "pool", "lx", "lw", "st", "cc"] + [f"d{i}" for i in range(10)]
        for name in names:
            self.sems[name] = nc.alloc_semaphore("sem_" + name)
            self.ct[name] = 0

    def phase(self, fn, name=None):
        snap_ct = dict(self.ct)
        snap_st = dict(self.st)
        final = [None, None]
        with self.nc.Block(name) as blk:
            for engname in ENGS:
                self.ct = dict(snap_ct)
                self.st = dict(snap_st)
                self.cur = engname

                def body(eng):
                    self.eng = eng
                    for s_ in ("st", "d2", "d3", "d8", "d9", "cc"):
                        if self.ct[s_]:
                            eng.wait_ge(self.sems[s_], self.ct[s_])
                    fn(self)
                    if final[0] is None:
                        final[0] = dict(self.ct)
                        final[1] = dict(self.st)
                    else:
                        assert final[0] == self.ct, (
                            f"phase {name}: ct divergence on {self.cur}: {final[0]} vs {self.ct}")

                getattr(blk, engname)(body)
        self.ct = final[0]
        self.st = final[1]

    def emit(self, engine, fn, inc=None, n=1):
        if inc is not None and n == 16:
            self.wait(engine, inc, self.ct[inc])
        if self.cur == engine:
            inst = fn(self.eng)
            if inc is not None:
                inst.then_inc(self.sems[inc], n)
        if inc is not None:
            self.ct[inc] += n
            return self.ct[inc]
        return None

    def wait(self, engine, sem, val):
        if not val:
            return
        if self.cur == engine:
            self.eng.wait_ge(self.sems[sem], val)

    def tag(self, sem):
        return self.ct[sem]


def build_program(upto=99, dump=None, debug_ln=False):
    global DEBUG_LN
    DEBUG_LN = debug_ln
    nc = bass.Bass("TRN2", target_bir_lowering=False)

    def inp(name, shape, dt=F32):
        return nc.dram_tensor(name, list(shape), dt, kind="ExternalInput").ap()

    xT = inp("xT", [D, T])
    wq_t = inp("wq_t", [32, 128, 4096])
    wk_t = inp("wk_t", [8, 128, 4096])
    wv_t = inp("wv_t", [8, 128, 4096])
    wo_t = inp("wo_t", [32, 128, 4096])
    wqkv_t = inp("wqkv_t", [96, 128, 4096])
    wout_t = inp("wout_t", [32, 128, 4096])
    wbase_t = inp("wbase_t", [32, 128, 4096])
    lora_a_t = inp("lora_a_t", [128, 32, 128])
    lora_b_f = inp("lora_b_f", [128, 4096])
    router_t = inp("router_t", [128, 32, 16])
    bq_i = inp("bq", [1, 4096])
    bk_i = inp("bk", [1, 1024])
    bv_i = inp("bv", [1, 1024])
    bo_i = inp("bo", [1, 4096])
    bbase_i = inp("b_base", [1, 4096])
    qlnw_i = inp("qlnw_col", [128, 32])
    qlnb_i = inp("qlnb_col", [128, 32])
    klnw_i = inp("klnw_col", [128, 32])
    klnb_i = inp("klnb_col", [128, 32])
    rb_i = inp("router_b", [1, 16])
    prior_i = inp("prior", [1, 16])
    active_i = inp("active", [1, 4], I32)
    iota16_i = inp("iota16", [1, 16])
    eye_i = inp("eye128", [128, 128])
    emat_i = inp("emat", [16, 128])
    ropeq_cos_i = inp("ropeq_cos", [64, T])
    ropeq_sin_i = inp("ropeq_sin", [64, T])
    ropek_cos_i = inp("ropek_cos", [64, T])
    ropek_sin_i = inp("ropek_sin", [64, T])
    alibi_q_i = inp("alibi_q", [32, T])
    alibi_k_i = inp("alibi_k", [128, 1024])
    masks_i = inp("masks", [128, 8 * 512], F32)

    yT = nc.dram_tensor("yT", [D, T], F32, kind="ExternalOutput").ap()
    dump_out = None
    if dump is not None:
        dump_out = nc.dram_tensor("dump", list(dump[1]), dump[2], kind="ExternalOutput").ap()

    def internal(name, shape, dt):
        return nc.dram_tensor(name, list(shape), dt, kind="Internal").ap()

    r1 = internal("r1", [D, T], F32)
    r2 = internal("r2", [D, T], F32)
    r3 = internal("r3", [D, T], F32)
    q_buf = internal("q_buf", [D, T], BF16)
    kstage = internal("kstage", [D, T], BF16)
    qstage = internal("qstage", [D, T], BF16)
    vstage = internal("vstage", [D, T], BF16)
    cka_in = internal("cka_in", [1024, T], BF16)
    cka_out = internal("cka_out", [2048, T], BF16)
    cva_in = internal("cva_in", [NG * 16 * 128, 128], BF16)
    cva_out = internal("cva_out", [2 * NG * 16 * 128, 128], BF16)
    ckb_in = internal("ckb_in", [4096, T], BF16)
    ckb_out = internal("ckb_out", [8192, T], BF16)
    cvb_in = internal("cvb_in", [NH * 16 * 128, 128], BF16)
    cvb_out = internal("cvb_out", [2 * NH * 16 * 128, 128], BF16)
    dram_map = {"r1": r1, "r2": r2, "r3": r3, "q_buf": q_buf, "kstage": kstage,
                "qstage": qstage, "vstage": vstage, "cka_in": cka_in,
                "cka_out": cka_out, "cva_in": cva_in, "cva_out": cva_out,
                "ckb_in": ckb_in, "ckb_out": ckb_out, "cvb_in": cvb_in,
                "cvb_out": cvb_out, "xT": xT, "yT": yT, "ln": None}

    sbt = nc.alloc_sbuf_tensor
    big_guard = nc.sbuf_tensor("big_sb", [128, NC, T], BF16)
    big_sb = big_guard.__enter__()
    zp_sb = sbt("zp_sb", [128, T], BF16)
    ones_b = sbt("ones_b", [1, 512], BF16)
    ones_f = sbt("ones_f", [1, 128], F32)
    ones_col_b = sbt("ones_col_b", [128, 1], BF16)
    eps5 = sbt("eps5", [1, 1], F32)
    eps12 = sbt("eps12", [1, 1], F32)
    eye_b = sbt("eye_b", [128, 128], BF16)
    emat_b = sbt("emat_b", [16, 128], BF16)
    qlnw_c = sbt("qlnw_c", [128, 32], F32)
    qlnb_c = sbt("qlnb_c", [128, 32], F32)
    klnw_c = sbt("klnw_c", [128, 32], F32)
    klnb_c = sbt("klnb_c", [128, 32], F32)
    c16_b = sbt("c16_b", [1, 16], BF16)
    cnt_bc = sbt("cnt_bc", [128, 16], F32)
    actf = sbt("actf", [1, 4], F32)
    acti = sbt("acti", [1, 4], I32)
    iota16 = sbt("iota16_s", [1, 16], F32)
    msk_sb = sbt("msk_sb", [128, 8 * 512], BF16)
    alibi_k_sb = sbt("alibi_k_sb", [128, 1024], F32)
    tmp16a = sbt("tmp16a", [1, 16], F32)
    tmp16b = sbt("tmp16b", [1, 16], F32)
    cnt16 = sbt("cnt16", [1, 16], F32)
    rbf = sbt("rbf", [1, 16], F32)
    priorf = sbt("priorf", [1, 16], F32)

    pb = PB(nc)

    def ph_setup_fn(ps):
        def fn(p):
            t_lx = None
            for dst, src in ((qlnw_c, qlnw_i), (qlnb_c, qlnb_i), (klnw_c, klnw_i),
                             (klnb_c, klnb_i), (iota16, iota16_i), (alibi_k_sb, alibi_k_i),
                             (acti, active_i), (rbf, rb_i), (priorf, prior_i)):
                t_lx = p.emit("sync", lambda e, d=dst, s=src: e.dma_start(out=d[...], in_=s[...]), "lx", 16)
            for dst, src in ((eye_b, eye_i), (emat_b, emat_i), (msk_sb, masks_i)):
                p.emit("gpsimd", lambda e, d=dst, s=src: e.dma_start(out=d[...], in_=s[...]), "lw", 16)
            for tile, v in ((ones_b, 1.0), (ones_f, 1.0), (ones_col_b, 1.0),
                            (eps5, LN_EPS), (eps12, 1e-12), (cnt16, 0.0)):
                t_ms = p.emit("vector", lambda e, t=tile, v=v: e.memset(t[...], v), "dve")
            p.st["consts_done"] = t_ms
            def _preload(e):
                from concourse.hw_specs import get_activation_tables
                tables = get_activation_tables(nc.m.arch)
                idx = list(tables.keys()).index("natural_log_exp_and_others")
                return e.add_instruction(mybir.InstLoadActFuncSet(
                    name=nc.get_next_instruction_name(), act_func_set_id=idx,
                    ins=[], outs=[]))
            p.emit("scalar", _preload)
            p.wait("vector", "lx", t_lx)
            p.emit("vector", lambda e: e.tensor_scalar(qlnw_c[...], qlnw_c[...], SCALE, None, ALU.mult), "dve")
            p.emit("vector", lambda e: e.tensor_scalar(qlnb_c[...], qlnb_c[...], SCALE, None, ALU.mult), "dve")
            t3 = p.emit("vector", lambda e: e.tensor_copy(actf[...], acti[...]), "dve")
            tj = t3
            for j in range(4):
                p.wait("vector", "dve", tj)
                ta = p.emit("vector", lambda e, j=j: e.tensor_scalar(
                    tmp16a[...], iota16[...], actf[0:1, j:j + 1], None, ALU.is_equal), "dve")
                p.wait("vector", "dve", ta)
                tj = p.emit("vector", lambda e: e.tensor_tensor(cnt16[...], cnt16[...], tmp16a[...], ALU.add), "dve")
            p.wait("scalar", "lx", t_lx)
            p.wait("scalar", "dve", p.st["consts_done"])
            t4 = p.emit("scalar", lambda e: e.activation(tmp16b[...], priorf[...], AF.Ln, bias=eps12[0:1, 0:1]), "act")
            p.wait("vector", "act", t4)
            t5 = p.emit("vector", lambda e: e.tensor_tensor(tmp16b[...], tmp16b[...], rbf[...], ALU.add), "dve")
            p.wait("vector", "dve", t5)
            p.emit("vector", lambda e: e.tensor_copy(c16_b[...], tmp16b[...]), "dve")
            p.wait("tensor", "dve", tj)
            t7 = p.emit("tensor", lambda e: e.matmul(ps[...], ones_f[0:1, :], cnt16[...], start=True, stop=True), "pe")
            p.wait("scalar", "pe", t7)
            p.st["setup_done"] = p.emit("scalar", lambda e: e.activation(cnt_bc[...], ps[...], AF.Copy), "act")
            p.wait("gpsimd", "lw", p.tag("lw"))
            p.wait("sync", "lx", p.tag("lx"))
        return fn

    def ph_ln(p, bufs, src, src_f32, affine, out_sb, out_dram):
        (xc, xb, xq, rows_m, rows_r, rows_n, tmpn, tmp2, ps_sx, ps_sq, ps_b1, ps_b2) = bufs
        NS = 4
        for s in range(NS):
            cols = slice(s * 512, (s + 1) * 512)
            t_mm = [0] * NC
            for c in range(NC):
                cp = c % 2
                prev_mm = t_mm[c - 2] if c >= 2 else p.st.get("ln_mm_prev", 0)
                p.wait("sync", "pe", prev_mm)
                p.wait("sync", "dve", p.st.get(f"ln_xcuse{cp}", 0))
                t_ld = p.emit("sync", lambda e, c=c, cp=cp, cols=cols: e.dma_start(
                    out=xc[cp][...], in_=src[c * 128:(c + 1) * 128, cols]), f"d{cp}", 16)
                if src_f32:
                    p.wait("vector", f"d{cp}", t_ld)
                    p.wait("vector", "pe", prev_mm)
                    t_xb = p.emit("vector", lambda e, cp=cp: e.tensor_copy(xb[cp][...], xc[cp][...]), "dve")
                    p.st[f"ln_xcuse{cp}"] = t_xb
                else:
                    t_xb = 0
                p.wait("scalar", f"d{cp}", t_ld)
                p.wait("scalar", "pe", prev_mm)
                if src_f32:
                    p.wait("scalar", "dve", t_xb)
                t_xq = p.emit("scalar", lambda e, cp=cp: e.activation(
                    xq[cp][...], (xb[cp] if src_f32 else xc[cp])[...], AF.Square), "act")
                if src_f32:
                    p.wait("tensor", "dve", t_xb)
                p.wait("tensor", "act", t_xq)
                p.emit("tensor", lambda e, c=c, cp=cp: e.matmul(
                    ps_sx[0:1, :], ones_col_b[...], (xb[cp] if src_f32 else xc[cp])[...],
                    start=(c == 0), stop=(c == NC - 1)), None)
                t_mm[c] = p.emit("tensor", lambda e, c=c, cp=cp: e.matmul(
                    ps_sq[0:1, :], ones_col_b[...], xq[cp][...],
                    start=(c == 0), stop=(c == NC - 1)), "pe")
            p.st["ln_mm_prev"] = t_mm[NC - 1]
            m = rows_m[0:1, cols]
            rstd = rows_r[0:1, cols]
            nmr = rows_n[0:1, cols]
            p.wait("vector", "pe", t_mm[NC - 1])
            ta = p.emit("vector", lambda e, m=m: e.tensor_scalar(m, ps_sx[0:1, :], 1.0 / D, None, ALU.mult), "dve")
            tb = p.emit("vector", lambda e, nmr=nmr: e.tensor_scalar(nmr, ps_sq[0:1, :], 1.0 / D, None, ALU.mult), "dve")
            p.wait("vector", "dve", ta)
            tc = p.emit("vector", lambda e, m=m, rstd=rstd: e.tensor_tensor(rstd, m, m, ALU.mult), "dve")
            p.wait("vector", "dve", tc)
            td = p.emit("vector", lambda e, rstd=rstd, nmr=nmr: e.tensor_tensor(rstd, nmr, rstd, ALU.subtract), "dve")
            p.wait("scalar", "dve", td)
            te = p.emit("scalar", lambda e, rstd=rstd: e.activation(rstd, rstd, AF.Ln, bias=eps5[0:1, 0:1]), "act")
            p.wait("scalar", "act", te)
            tf = p.emit("scalar", lambda e, rstd=rstd: e.activation(rstd, rstd, AF.Exp, scale=-0.5), "act")
            p.wait("vector", "act", tf)
            tg = p.emit("vector", lambda e, m=m, rstd=rstd, nmr=nmr: e.scalar_tensor_tensor(
                nmr, m, -1.0, rstd, ALU.mult, ALU.mult), "dve")
            p.st[f"ln_rows{s}"] = (tf, tg)
            if DEBUG_LN:
                p.wait("scalar", "dve", tg)
                p.emit("scalar", lambda e, cols=cols: e.dma_start(
                    out=r3[0:1, cols], in_=rows_m[0:1, cols]), "st", 16)
                p.emit("scalar", lambda e, cols=cols: e.dma_start(
                    out=r3[1:2, cols], in_=rows_r[0:1, cols]), "st", 16)
                p.emit("scalar", lambda e, cols=cols: e.dma_start(
                    out=r3[2:3, cols], in_=rows_n[0:1, cols]), "st", 16)
        for s in range(NS):
            cols = slice(s * 512, (s + 1) * 512)
            tf, tg = p.st[f"ln_rows{s}"]
            p.wait("tensor", "act", tf)
            p.wait("tensor", "dve", tg)
            p.wait("tensor", "dve", p.st.get("ln_bc_free", 0))
            p.emit("tensor", lambda e, cols=cols: e.matmul(
                ps_b1[...], ones_f[0:1, :], rows_r[0:1, cols], start=True, stop=True), None)
            t_bc = p.emit("tensor", lambda e, cols=cols: e.matmul(
                ps_b2[...], ones_f[0:1, :], rows_n[0:1, cols], start=True, stop=True), "pe")
            t_t1 = [0] * NC
            t_t2 = [0] * NC
            for c in range(NC):
                cp = c % 2
                p.wait("sync", "pe", p.st.get("ln_mm_prev", 0))
                p.wait("sync", "dve", max(t_t1[c - 2] if c >= 2 else 0,
                                          p.st.get(f"ln_xcuse{cp}", 0)))
                t_ld = p.emit("sync", lambda e, c=c, cp=cp, cols=cols: e.dma_start(
                    out=xc[cp][...], in_=src[c * 128:(c + 1) * 128, cols]), f"d{cp}", 16)
                p.wait("vector", f"d{cp}", t_ld)
                p.wait("vector", "pe", t_bc)
                p.wait("vector", "dve", t_t2[c - 2] if c >= 2 else 0)
                t_t1[c] = p.emit("vector", lambda e, cp=cp: e.tensor_tensor(
                    tmpn[cp][...], xc[cp][...], ps_b1[...], ALU.mult), "dve")
                p.st[f"ln_xcuse{cp}"] = t_t1[c]
                p.wait("vector", "dve", t_t1[c])
                if affine is None:
                    t_t2[c] = p.emit("vector", lambda e, cp=cp, c=c, cols=cols: e.tensor_tensor(
                        out_sb[:, c, cols], tmpn[cp][...], ps_b2[...], ALU.add), "dve")
                else:
                    t2a = p.emit("vector", lambda e, cp=cp: e.tensor_tensor(
                        tmp2[cp][...], tmpn[cp][...], ps_b2[...], ALU.add), "dve")
                    p.wait("vector", "dve", t2a)
                    p.wait("vector", f"d{2 + cp}", p.st.get(f"ln_st{cp}", 0))
                    w_col, b_col = affine
                    t_t2[c] = p.emit("vector", lambda e, cp=cp, c=c: e.tensor_scalar(
                        xq[cp][...], tmp2[cp][...],
                        w_col[:, c:c + 1], b_col[:, c:c + 1], ALU.mult, ALU.add), "dve")
                if out_dram is not None:
                    p.wait("scalar", "dve", t_t2[c])
                    tl = p.emit("scalar", lambda e, c=c, cols=cols, cp=cp: e.dma_start(
                        out=out_dram[c * 128:(c + 1) * 128, cols], in_=xq[cp][...]), f"d{2 + cp}", 16)
                    p.st[f"ln_st{cp}"] = tl
            p.st["ln_bc_free"] = t_t2[NC - 1]

    def make_ln(src_name, src, tag, src_f32=True, affine=None, out_sb=None, out_dram=None):
        def run():
            with ExitStack() as es:
                xdt = F32 if src_f32 else BF16
                xc = [es.enter_context(nc.sbuf_tensor(f"xc{tag}{i}", [128, 512], xdt)) for i in range(2)]
                xb = [es.enter_context(nc.sbuf_tensor(f"xbb{tag}{i}", [128, 512], BF16)) for i in range(2)]
                xq = [es.enter_context(nc.sbuf_tensor(f"xqq{tag}{i}", [128, 512], BF16)) for i in range(2)]
                rows_m = es.enter_context(nc.sbuf_tensor(f"rm{tag}", [1, T], F32))
                rows_r = es.enter_context(nc.sbuf_tensor(f"rr{tag}", [1, T], F32))
                rows_n = es.enter_context(nc.sbuf_tensor(f"rn{tag}", [1, T], F32))
                tmpn = [es.enter_context(nc.sbuf_tensor(f"tn{tag}{i}", [128, 512], F32)) for i in range(2)]
                tmp2 = [es.enter_context(nc.sbuf_tensor(f"t2{tag}{i}", [128, 512], F32)) for i in range(2)]
                ps_sx = es.enter_context(nc.psum_tensor(f"px{tag}", [1, 512], F32))
                ps_sq = es.enter_context(nc.psum_tensor(f"pq{tag}", [1, 512], F32))
                ps_b1 = es.enter_context(nc.psum_tensor(f"p1{tag}", [128, 512], F32))
                ps_b2 = es.enter_context(nc.psum_tensor(f"p2{tag}", [128, 512], F32))
                bufs = (xc, xb, xq, rows_m, rows_r, rows_n, tmpn, tmp2, ps_sx, ps_sq, ps_b1, ps_b2)
                pb.phase(lambda p: ph_ln(p, bufs, src, src_f32, affine, out_sb, out_dram), tag)
        return run

    def ph_proj(p, wbufs, ps_t, wsrc, ob0, nob, bias_row, consume):
        t_mm = [0] * nob
        t_cons = [None] * nob
        for ob in range(nob):
            par = ob % 2
            if ob >= 2:
                p.wait("gpsimd", "pe", t_mm[ob - 2])
            t_w = p.emit("gpsimd", lambda e, ob=ob, par=par: e.dma_start(
                out=wbufs[par][...], in_=wsrc[ob0 + ob]), f"d{4 + par}", 16)
            p.wait("tensor", f"d{4 + par}", t_w)
            if ob >= 2:
                c2 = t_cons[ob - 2]
                p.wait("tensor", "dve", c2.get("dve", 0))
            for c in range(NC):
                for tt in range(NTT):
                    stop = (bias_row is None and c == NC - 1)
                    last = (bias_row is None and c == NC - 1 and tt == NTT - 1)
                    t_mm[ob] = p.emit("tensor", lambda e, c=c, tt=tt, par=par, stop=stop: e.matmul(
                        ps_t[par][tt][...], wbufs[par][:, c * 128:(c + 1) * 128],
                        big_sb[:, c, tt * 512:(tt + 1) * 512],
                        start=(c == 0), stop=stop), "pe" if last else None) or t_mm[ob]
            if bias_row is not None:
                for tt in range(NTT):
                    t_mm[ob] = p.emit("tensor", lambda e, tt=tt, par=par, ob=ob: e.matmul(
                        ps_t[par][tt][...], bias_row[0:1, ob * 128:(ob + 1) * 128],
                        ones_b[0:1, 0:512], start=False, stop=True),
                        "pe" if tt == NTT - 1 else None) or t_mm[ob]
            t_cons[ob] = consume(p, ob, par, t_mm[ob])

    def rope_consume(tag, ropc, rops, tmpr, rot, t_rope_ld, out_dram, ps_t):
        def consume(p, ob, par, t_mm):
            tl_act = None
            tl_dve = None
            for tt in range(NTT):
                kp = tt % 2
                ps = ps_t[par][tt]
                cols = slice(tt * 512, (tt + 1) * 512)
                p.wait("vector", "pe", t_mm)
                p.wait("vector", "d7", t_rope_ld)
                p.wait("vector", f"d{8 + kp}", p.st.get(f"{tag}_rst{kp}", 0))
                p.wait("vector", "dve", p.st.get(f"{tag}_ropeprev", 0))
                p.emit("vector", lambda e, ps=ps, cols=cols: e.tensor_tensor(
                    tmpr[0][...], ps[0:64, :], ropc[:, cols], ALU.mult), "dve")
                tb = p.emit("vector", lambda e, ps=ps, cols=cols: e.tensor_tensor(
                    tmpr[1][...], ps[64:128, :], rops[:, cols], ALU.mult), "dve")
                p.emit("vector", lambda e, ps=ps, cols=cols: e.tensor_tensor(
                    tmpr[2][...], ps[0:64, :], rops[:, cols], ALU.mult), "dve")
                td = p.emit("vector", lambda e, ps=ps, cols=cols: e.tensor_tensor(
                    tmpr[3][...], ps[64:128, :], ropc[:, cols], ALU.mult), "dve")
                p.wait("vector", "dve", tb)
                p.emit("vector", lambda e, kp=kp: e.tensor_tensor(
                    rot[kp][0:64, :], tmpr[0][...], tmpr[1][...], ALU.subtract), "dve")
                p.wait("vector", "dve", td)
                tf = p.emit("vector", lambda e, kp=kp: e.tensor_tensor(
                    rot[kp][64:128, :], tmpr[2][...], tmpr[3][...], ALU.add), "dve")
                p.st[f"{tag}_ropeprev"] = tf
                tl_dve = tf
                p.wait("scalar", "dve", tf)
                tl_act = p.emit("scalar", lambda e, ob=ob, cols=cols, kp=kp: e.dma_start(
                    out=out_dram[ob * 128:(ob + 1) * 128, cols], in_=rot[kp][...]), f"d{8 + kp}", 16)
                p.st[f"{tag}_rst{kp}"] = tl_act
            return {"dve": tl_dve, "act": tl_act}
        return consume

    def clip_consume(tag, kt, out_dram, ps_t, do_clip):
        def consume(p, ob, par, t_mm):
            tl_act = None
            tl_dve = None
            for tt in range(NTT):
                kp = tt % 2
                cols = slice(tt * 512, (tt + 1) * 512)
                p.wait("vector", "pe", t_mm)
                p.wait("vector", f"d{8 + kp}", p.st.get(f"{tag}_cst{kp}", 0))
                if do_clip:
                    ta = p.emit("vector", lambda e, kp=kp, par=par, tt=tt: e.tensor_scalar(
                        kt[kp][...], ps_t[par][tt][...], CLIP, -CLIP, ALU.min, ALU.max), "dve")
                else:
                    ta = p.emit("vector", lambda e, kp=kp, par=par, tt=tt: e.tensor_copy(
                        kt[kp][...], ps_t[par][tt][...]), "dve")
                tl_dve = ta
                p.wait("scalar", "dve", ta)
                tl_act = p.emit("scalar", lambda e, ob=ob, cols=cols, kp=kp: e.dma_start(
                    out=out_dram[ob * 128:(ob + 1) * 128, cols], in_=kt[kp][...]), f"d{8 + kp}", 16)
                p.st[f"{tag}_cst{kp}"] = tl_act
            return {"dve": tl_dve, "act": tl_act}
        return consume

    def resid_consume(tag, rt, ot, resid_in, resid_out, ps_t):
        def consume(p, ob, par, t_mm):
            tl_act = None
            tl_dve = None
            for tt in range(NTT):
                kp = tt % 2
                cols = slice(tt * 512, (tt + 1) * 512)
                p.wait("sync", "dve", p.st.get(f"{tag}_rr{kp}", 0))
                t_r = p.emit("sync", lambda e, ob=ob, cols=cols, kp=kp: e.dma_start(
                    out=rt[kp][...], in_=resid_in[ob * 128:(ob + 1) * 128, cols]), f"d{kp}", 16)
                p.wait("vector", "pe", t_mm)
                p.wait("vector", f"d{kp}", t_r)
                p.wait("vector", f"d{8 + kp}", p.st.get(f"{tag}_ro{kp}", 0))
                ta = p.emit("vector", lambda e, kp=kp, par=par, tt=tt: e.tensor_tensor(
                    ot[kp][...], ps_t[par][tt][...], rt[kp][...], ALU.add), "dve")
                p.st[f"{tag}_rr{kp}"] = ta
                tl_dve = ta
                p.wait("scalar", "dve", ta)
                tl_act = p.emit("scalar", lambda e, ob=ob, cols=cols, kp=kp: e.dma_start(
                    out=resid_out[ob * 128:(ob + 1) * 128, cols], in_=ot[kp][...]), f"d{8 + kp}", 16)
                p.st[f"{tag}_ro{kp}"] = tl_act
            return {"dve": tl_dve, "act": tl_act}
        return consume

    def make_proj(tag, wsrc, ob0, nob, kind, **kw):
        def run():
            with ExitStack() as es:
                wbufs = [es.enter_context(nc.sbuf_tensor(f"wb{tag}{i}", [128, 4096], BF16)) for i in range(2)]
                ps_t = [[es.enter_context(nc.psum_tensor(f"ps{tag}{i}_{t}", [128, 512], F32)) for t in range(NTT)]
                        for i in range(2)]
                bias_i = kw.get("bias_i")
                bias_row = None
                if bias_i is not None:
                    bias_row = es.enter_context(nc.sbuf_tensor(f"br{tag}", [1, kw["bias_n"]], BF16))
                if kind == "rope":
                    ropc = es.enter_context(nc.sbuf_tensor(f"rc{tag}", [64, T], F32))
                    rops = es.enter_context(nc.sbuf_tensor(f"rs{tag}", [64, T], F32))
                    tmpr = [es.enter_context(nc.sbuf_tensor(f"tr{tag}{i}", [64, 512], F32)) for i in range(4)]
                    rot = [es.enter_context(nc.sbuf_tensor(f"ro{tag}{i}", [128, 512], BF16)) for i in range(2)]
                elif kind == "clip":
                    kt = [es.enter_context(nc.sbuf_tensor(f"kt{tag}{i}", [128, 512], BF16)) for i in range(2)]
                else:
                    rt = [es.enter_context(nc.sbuf_tensor(f"rt{tag}{i}", [128, 512], F32)) for i in range(2)]
                    ot = [es.enter_context(nc.sbuf_tensor(f"ot{tag}{i}", [128, 512], F32)) for i in range(2)]

                def fn(p):
                    if bias_i is not None:
                        t_b = p.emit("gpsimd", lambda e: e.dma_start(out=bias_row[...], in_=bias_i[...]), "d6", 16)
                        p.wait("tensor", "d6", t_b)
                    if kind == "rope":
                        p.emit("sync", lambda e: e.dma_start(out=ropc[...], in_=kw["cos_i"][...]), "d7", 16)
                        t_rl = p.emit("sync", lambda e: e.dma_start(out=rops[...], in_=kw["sin_i"][...]), "d7", 16)
                        consume = rope_consume(tag, ropc, rops, tmpr, rot, t_rl, kw["out_dram"], ps_t)
                    elif kind == "clip":
                        consume = clip_consume(tag, kt, kw["out_dram"], ps_t, kw.get("do_clip", True))
                    else:
                        consume = resid_consume(tag, rt, ot, kw["resid_in"], kw["resid_out"], ps_t)
                    ph_proj(p, wbufs, ps_t, wsrc, ob0, nob, bias_row, consume)

                pb.phase(fn, tag)
        return run

    def make_vtr(nheads, v_in, v_stage_out, ccs, tag):
        def run():
            with ExitStack() as es:
                vin = [es.enter_context(nc.sbuf_tensor(f"vi{tag}{i}", [128, T], BF16)) for i in range(2)]
                vot = [es.enter_context(nc.sbuf_tensor(f"vo{tag}{i}", [128, 128], BF16)) for i in range(2)]
                ps_tr = [es.enter_context(nc.psum_tensor(f"pt{tag}{i}", [128, 128], BF16)) for i in range(2)]

                def fn(p):
                    for h in range(nheads):
                        hp = h % 2
                        p.wait("sync", "pe", p.st.get(f"{tag}_pe{hp}", 0))
                        t_ld = p.emit("sync", lambda e, h=h, hp=hp: e.dma_start(
                            out=vin[hp][...], in_=v_in[h * 128:(h + 1) * 128, :]), f"d{hp}", 16)
                        for kt_ in range(16):
                            kp = kt_ % 2
                            p.wait("tensor", f"d{hp}", t_ld)
                            p.wait("tensor", "dve", p.st.get(f"{tag}_cp{kp}", 0))
                            t_tr = p.emit("tensor", lambda e, kt_=kt_, hp=hp, kp=kp: e.transpose(
                                ps_tr[kp][...], vin[hp][:, kt_ * 128:(kt_ + 1) * 128], eye_b[...]), "pe")
                            p.wait("vector", "pe", t_tr)
                            p.wait("vector", f"d{8 + kp}", p.st.get(f"{tag}_st{kp}", 0))
                            t_cp = p.emit("vector", lambda e, kp=kp: e.tensor_copy(vot[kp][...], ps_tr[kp][...]), "dve")
                            p.st[f"{tag}_cp{kp}"] = t_cp
                            p.wait("scalar", "dve", t_cp)
                            t_st = p.emit("scalar", lambda e, h=h, kt_=kt_, kp=kp: e.dma_start(
                                out=v_stage_out[(h * 16 + kt_) * 128:(h * 16 + kt_ + 1) * 128, :],
                                in_=vot[kp][...]), f"d{8 + kp}", 16)
                            p.st[f"{tag}_st{kp}"] = t_st
                        p.st[f"{tag}_pe{hp}"] = p.tag("pe")
                    for s_ in ("st", "d2", "d3", "d8", "d9"):
                        p.wait("gpsimd", s_, p.tag(s_))
                    for cin, cout in ccs:
                        p.emit("gpsimd", lambda e, cin=cin, cout=cout: e.collective_compute(
                            "AllGather", ALU.bypass, replica_groups=RG,
                            ins=[cin[...]], outs=[cout[...]]), "cc")

                pb.phase(fn, tag)
        return run

    def make_attn(variant):
        n_outer = NG if variant == "a" else NH
        heads_per = 4 if variant == "a" else 1
        k_out = cka_out if variant == "a" else ckb_out
        v_out = cva_out if variant == "a" else cvb_out
        krows = 1024 if variant == "a" else 4096

        def run():
            with ExitStack() as es:
                K_sb = es.enter_context(nc.sbuf_tensor(f"Kx{variant}", [128, 4096], BF16))
                V_sb = es.enter_context(nc.sbuf_tensor(f"Vx{variant}", [128, 4096], BF16))
                Q_sb = es.enter_context(nc.sbuf_tensor(f"Qx{variant}", [128, T], BF16))
                probs = [es.enter_context(nc.sbuf_tensor(f"pr{variant}{i}", [128, 1024], BF16)) for i in range(2)]
                sq2 = [es.enter_context(nc.sbuf_tensor(f"sq{variant}{i}", [128, 512], BF16)) for i in range(2)]
                nrow = es.enter_context(nc.sbuf_tensor(f"nr{variant}", [1, 512], F32))
                mrow_b = es.enter_context(nc.sbuf_tensor(f"mb{variant}", [1, T], BF16))
                aqrow = es.enter_context(nc.sbuf_tensor(f"aq{variant}", [1, T], F32))
                sc11 = es.enter_context(nc.sbuf_tensor(f"sc{variant}", [1, 8], F32))
                rl = es.enter_context(nc.sbuf_tensor(f"rl{variant}", [1, 1024], F32))
                rbx = es.enter_context(nc.sbuf_tensor(f"rb{variant}", [128, 1024], F32))
                ps_sc = [es.enter_context(nc.psum_tensor(f"pc{variant}{i}", [128, 1024], F32)) for i in range(2)]
                ps_ctx = es.enter_context(nc.psum_tensor(f"px{variant}", [128, 1024], F32))
                ps_l = es.enter_context(nc.psum_tensor(f"pl{variant}", [1, 1024], F32))

                def fn(p):
                    p.wait("sync", "cc", p.tag("cc"))
                    for go in range(n_outer):
                        p.wait("sync", "pe", p.st.get("kv_pe", 0))
                        p.wait("sync", "act", p.st.get("kv_act", 0))
                        kv_tags = []
                        for r in range(2):
                            kv_tags.append(("lx" if r == 0 else "d0", p.emit(
                                "sync", lambda e, go=go, r=r: e.dma_start(
                                    out=K_sb[:, r * 2048:(r + 1) * 2048],
                                    in_=k_out[r * krows + go * 128: r * krows + go * 128 + 128, :]),
                                "lx" if r == 0 else "d0", 16)))
                            kv_tags.append(("d1" if r == 0 else "d3", p.emit(
                                "sync", lambda e, go=go, r=r: e.dma_start(
                                    out=V_sb.ap().rearrange("p (kt c) -> p kt c", c=128)[:, r * 16:(r + 1) * 16, :],
                                    in_=v_out[(r * n_outer + go) * 2048:(r * n_outer + go + 1) * 2048, :]
                                    .rearrange("(kt p) c -> p kt c", p=128)),
                                "d1" if r == 0 else "d3", 16)))
                        t_ln_free = p.st.get("nrow_free", 0)
                        for i8 in range(8):
                            sp = i8 % 2
                            for s_, t_ in kv_tags:
                                p.wait("scalar", s_, t_)
                            p.wait("scalar", "pe", p.st.get(f"sqf{sp}", 0))
                            tsq = p.emit("scalar", lambda e, i8=i8, sp=sp: e.activation(
                                sq2[sp][...], K_sb[:, i8 * 512:(i8 + 1) * 512], AF.Square), "act")
                            p.wait("tensor", "act", tsq)
                            p.wait("tensor", "dve", p.st.get("ps_l_free", 0))
                            p.wait("tensor", "act", p.st.get("nrow_ev", 0))
                            t_pe = p.emit("tensor", lambda e, sp=sp: e.matmul(
                                ps_l[0:1, 0:512], ones_col_b[...], sq2[sp][...],
                                start=True, stop=True), "pe")
                            p.st[f"sqf{sp}"] = t_pe
                            p.wait("scalar", "pe", t_pe)
                            p.wait("scalar", "dve", p.st.get("nrow_used", 0))
                            t_ev = p.emit("scalar", lambda e: e.activation(nrow[...], ps_l[0:1, 0:512], AF.Copy), "act")
                            p.st["nrow_ev"] = t_ev
                            p.wait("vector", "act", t_ev)
                            if i8 == 0:
                                t_red = p.emit("vector", lambda e: e.tensor_reduce(
                                    sc11[0:1, 0:1], nrow[...], AX.X, ALU.max), "dve")
                            else:
                                t_red = p.emit("vector", lambda e: e.tensor_reduce(
                                    sc11[0:1, 4:5], nrow[...], AX.X, ALU.max), "dve")
                                p.wait("vector", "dve", t_red)
                                t_red = p.emit("vector", lambda e: e.tensor_tensor(
                                    sc11[0:1, 0:1], sc11[0:1, 0:1], sc11[0:1, 4:5], ALU.max), "dve")
                            p.st["nrow_used"] = t_red
                        p.wait("scalar", "dve", p.st["nrow_used"])
                        tkl = p.emit("scalar", lambda e: e.activation(sc11[0:1, 1:2], sc11[0:1, 0:1], AF.Ln), "act")
                        p.wait("vector", "act", tkl)
                        tch = p.emit("vector", lambda e: e.tensor_scalar(
                            sc11[0:1, 2:3], sc11[0:1, 1:2], 0.5, 0.00995, ALU.mult, ALU.add), "dve")
                        for rh in range(heads_per):
                            h = go * heads_per + rh
                            p.wait("sync", "pe", p.st.get("q_pe", 0))
                            p.wait("sync", "act", p.st.get("q_act", 0))
                            t_q = p.emit("sync", lambda e, h=h: e.dma_start(
                                out=Q_sb[...], in_=q_buf[h * 128:(h + 1) * 128, :]), "d7", 16)
                            if variant == "b":
                                p.wait("sync", "dve", p.st.get("aq_used", 0))
                                t_aq = p.emit("sync", lambda e, h=h: e.dma_start(
                                    out=aqrow[...], in_=alibi_q_i[h:h + 1, :]), "lx", 16)
                            for i4 in range(4):
                                sp = i4 % 2
                                p.wait("scalar", "d7", t_q)
                                p.wait("scalar", "pe", p.st.get(f"sqf{sp}", 0))
                                tsq = p.emit("scalar", lambda e, i4=i4, sp=sp: e.activation(
                                    sq2[sp][...], Q_sb[:, i4 * 512:(i4 + 1) * 512], AF.Square), "act")
                                p.wait("tensor", "act", tsq)
                                p.wait("tensor", "dve", p.st.get("ps_l_free", 0))
                                p.wait("tensor", "act", p.st.get("nrow_ev", 0))
                                t_pe = p.emit("tensor", lambda e, sp=sp: e.matmul(
                                    ps_l[0:1, 0:512], ones_col_b[...], sq2[sp][...],
                                    start=True, stop=True), "pe")
                                p.st[f"sqf{sp}"] = t_pe
                                p.wait("scalar", "pe", t_pe)
                                p.wait("scalar", "dve", p.st.get("nrow_used", 0))
                                t_ev = p.emit("scalar", lambda e: e.activation(
                                    nrow[...], ps_l[0:1, 0:512], AF.Copy), "act")
                                p.st["nrow_ev"] = t_ev
                                p.wait("scalar", "act", t_ev)
                                t_l1 = p.emit("scalar", lambda e: e.activation(nrow[...], nrow[...], AF.Ln), "act")
                                p.wait("vector", "act", t_l1)
                                p.wait("vector", "dve", tch)
                                t_l2 = p.emit("vector", lambda e: e.tensor_scalar(
                                    nrow[...], nrow[...], 0.5, sc11[0:1, 2:3], ALU.mult, ALU.add), "dve")
                                p.wait("scalar", "dve", t_l2)
                                t_l3 = p.emit("scalar", lambda e: e.activation(nrow[...], nrow[...], AF.Exp), "act")
                                cols = slice(i4 * 512, (i4 + 1) * 512)
                                p.wait("vector", "act", t_l3)
                                p.wait("vector", "pe", p.st.get("mrow_used", 0))
                                if variant == "b":
                                    p.wait("vector", "lx", t_aq)
                                    t_mr = p.emit("vector", lambda e, cols=cols: e.scalar_tensor_tensor(
                                        mrow_b[0:1, cols], nrow[...], -1.0, aqrow[0:1, cols],
                                        ALU.mult, ALU.subtract), "dve")
                                    p.st["aq_used"] = t_mr
                                else:
                                    t_mr = p.emit("vector", lambda e, cols=cols: e.tensor_scalar(
                                        mrow_b[0:1, cols], nrow[...], -1.0, None, ALU.mult), "dve")
                                p.st["nrow_used"] = t_mr
                                p.st[f"mrow{i4}"] = t_mr
                            t_mrow = [p.st[f"mrow{i}"] for i in range(4)]

                            for sw, (i0, i1) in enumerate(SWEEPS):
                                halves = [(0, i0), (1, i1)]
                                visits = []
                                for j in range(i1 + 1):
                                    for r_k in range(2):
                                        for dd in range(4):
                                            hs = [hf for hf, ii in halves if j <= ii]
                                            visits.append((j, r_k, dd, hs))
                                nv = len(visits)
                                t_exp = [0] * nv
                                t_msk = [0] * nv
                                t_pv = [0] * nv
                                first_pv = {0: True, 1: True}
                                last_v = {}
                                for v, (j, r_k, dd, hs) in enumerate(visits):
                                    for hf in hs:
                                        last_v[hf] = v
                                for v, (j, r_k, dd, hs) in enumerate(visits):
                                    vp = v % 2
                                    kidx = r_k * 16 + j * 4 + dd
                                    kcol = r_k * 2048 + (j * 4 + dd) * 128
                                    if v >= 2:
                                        p.wait("tensor", "act", t_exp[v - 2])
                                        if t_msk[v - 2]:
                                            p.wait("tensor", "dve", t_msk[v - 2])
                                    else:
                                        p.wait("tensor", "act", p.st.get(f"scfree{vp}", 0))
                                    last_mm = None
                                    for hf in hs:
                                        own_i = halves[hf][1]
                                        p.wait("tensor", "dve", t_mrow[own_i])
                                        p.emit("tensor", lambda e, hf=hf, own_i=own_i, kcol=kcol, vp=vp: e.matmul(
                                            ps_sc[vp][:, hf * 512:(hf + 1) * 512],
                                            K_sb[:, kcol:kcol + 128],
                                            Q_sb[:, own_i * 512:(own_i + 1) * 512],
                                            start=True, stop=False), None)
                                        last_mm = (hf, own_i)
                                    t_sc = 0
                                    for hf in hs:
                                        own_i = halves[hf][1]
                                        t_sc = p.emit("tensor", lambda e, hf=hf, own_i=own_i, vp=vp: e.matmul(
                                            ps_sc[vp][:, hf * 512:(hf + 1) * 512],
                                            ones_b[0:1, 0:128],
                                            mrow_b[0:1, own_i * 512:(own_i + 1) * 512],
                                            start=False, stop=True), "pe")
                                    p.st["mrow_used"] = t_sc
                                    for hf in hs:
                                        own_i = halves[hf][1]
                                        if j == own_i:
                                            p.wait("vector", "pe", t_sc)
                                            if v >= 2:
                                                p.wait("vector", "act", t_exp[v - 2])
                                            t_msk[v] = p.emit("vector", lambda e, hf=hf, vp=vp, r_k=r_k, dd=dd: e.tensor_tensor(
                                                ps_sc[vp][:, hf * 512:(hf + 1) * 512],
                                                ps_sc[vp][:, hf * 512:(hf + 1) * 512],
                                                msk_sb[:, (r_k * 4 + dd) * 512:(r_k * 4 + dd + 1) * 512],
                                                ALU.add), "dve")
                                    p.wait("scalar", "pe", t_sc)
                                    if t_msk[v]:
                                        p.wait("scalar", "dve", t_msk[v])
                                    if v >= 2:
                                        p.wait("scalar", "pe", t_pv[v - 2])
                                    for hf in hs:
                                        if variant == "b":
                                            ki = h * 32 + (j * 2 + r_k) * 4 + dd
                                            t_exp[v] = p.emit("scalar", lambda e, hf=hf, vp=vp, ki=ki: e.activation(
                                                probs[vp][:, hf * 512:(hf + 1) * 512],
                                                ps_sc[vp][:, hf * 512:(hf + 1) * 512], AF.Exp,
                                                bias=alibi_k_sb[:, ki:ki + 1]), "act")
                                        else:
                                            t_exp[v] = p.emit("scalar", lambda e, hf=hf, vp=vp: e.activation(
                                                probs[vp][:, hf * 512:(hf + 1) * 512],
                                                ps_sc[vp][:, hf * 512:(hf + 1) * 512], AF.Exp), "act")
                                    p.wait("tensor", "act", t_exp[v])
                                    if first_pv[hs[0]]:
                                        p.wait("tensor", "dve", p.st.get("ctx_l_free", 0))
                                    for hf in hs:
                                        fin_ = (v == last_v[hf])
                                        p.emit("tensor", lambda e, hf=hf, vp=vp, kidx=kidx, fin_=fin_: e.matmul(
                                            ps_ctx[:, hf * 512:(hf + 1) * 512],
                                            V_sb[:, kidx * 128:(kidx + 1) * 128],
                                            probs[vp][:, hf * 512:(hf + 1) * 512],
                                            start=first_pv[hf], stop=fin_), None)
                                        t_pv[v] = p.emit("tensor", lambda e, hf=hf, vp=vp, fin_=fin_: e.matmul(
                                            ps_l[0:1, hf * 512:(hf + 1) * 512],
                                            ones_col_b[...],
                                            probs[vp][:, hf * 512:(hf + 1) * 512],
                                            start=first_pv[hf], stop=fin_), "pe")
                                        first_pv[hf] = False
                                p.wait("vector", "pe", t_pv[nv - 1])
                                tr1 = p.emit("vector", lambda e: e.reciprocal(rl[...], ps_l[...]), "dve")
                                p.st["ps_l_free"] = tr1
                                p.wait("tensor", "dve", tr1)
                                p.wait("tensor", "act", t_exp[nv - 1])
                                t_rb = 0
                                for hf in range(2):
                                    t_rb = p.emit("tensor", lambda e, hf=hf: e.matmul(
                                        ps_sc[0][:, hf * 512:(hf + 1) * 512], ones_f[0:1, :],
                                        rl[0:1, hf * 512:(hf + 1) * 512], start=True, stop=True), "pe")
                                p.wait("scalar", "pe", t_rb)
                                tr3 = p.emit("scalar", lambda e: e.activation(rbx[...], ps_sc[0][...], AF.Copy), "act")
                                p.st["scfree0"] = tr3
                                p.st["scfree1"] = t_exp[nv - 1]
                                p.wait("vector", "act", tr3)
                                tr4 = 0
                                for hf, own_i in halves:
                                    tr4 = p.emit("vector", lambda e, hf=hf, own_i=own_i, h=h: e.tensor_tensor(
                                        big_sb[:, h, own_i * 512:(own_i + 1) * 512],
                                        ps_ctx[:, hf * 512:(hf + 1) * 512],
                                        rbx[:, hf * 512:(hf + 1) * 512], ALU.mult), "dve")
                                p.st["ctx_l_free"] = tr4
                                p.st["ps_l_free"] = tr4
                            p.st["q_pe"] = p.tag("pe")
                            p.st["q_act"] = p.tag("act")
                        p.st["kv_pe"] = p.tag("pe")
                        p.st["kv_act"] = p.tag("act")

                pb.phase(fn, f"attn_{variant}")
        return run

    def make_ffn_a(tag):
        def run():
            with ExitStack() as es:
                la_sb = es.enter_context(nc.sbuf_tensor(f"la{tag}", [128, NC, 128], BF16))
                rt_sb = es.enter_context(nc.sbuf_tensor(f"rt{tag}", [128, NC, 16], BF16))
                z_sb = es.enter_context(nc.sbuf_tensor(f"zs{tag}", [128, 512], BF16))
                lg_b = es.enter_context(nc.sbuf_tensor(f"lg{tag}", [16, 512], BF16))
                lg_tm = es.enter_context(nc.sbuf_tensor(f"lt{tag}", [128, 64], F32))
                mx4 = es.enter_context(nc.sbuf_tensor(f"mx{tag}", [128, 4], F32))
                et = es.enter_context(nc.sbuf_tensor(f"et{tag}", [128, 64], F32))
                sm4 = es.enter_context(nc.sbuf_tensor(f"sm{tag}", [128, 4], F32))
                al_b = es.enter_context(nc.sbuf_tensor(f"ab{tag}", [128, 64], BF16))
                al_em = es.enter_context(nc.sbuf_tensor(f"ae{tag}", [16, 512], BF16))
                ps_z = es.enter_context(nc.psum_tensor(f"pz{tag}", [128, 512], F32))
                ps_lg = es.enter_context(nc.psum_tensor(f"pg{tag}", [16, 512], F32))
                ps_tp = es.enter_context(nc.psum_tensor(f"pp{tag}", [128, 64], BF16))
                ps_tb = es.enter_context(nc.psum_tensor(f"pb{tag}", [16, 512], BF16))
                ps_ax = es.enter_context(nc.psum_tensor(f"pa{tag}", [128, 512], F32))

                def fn(p):
                    p.emit("gpsimd", lambda e: e.dma_start(
                        out=la_sb.ap().rearrange("p c e -> p (c e)"),
                        in_=lora_a_t.rearrange("p c e -> p (c e)")), "lw", 16)
                    t_la = p.emit("gpsimd", lambda e: e.dma_start(
                        out=rt_sb.ap().rearrange("p c e -> p (c e)"),
                        in_=router_t.rearrange("p c e -> p (c e)")), "lw", 16)
                    for tt in range(NTT):
                        cols = slice(tt * 512, (tt + 1) * 512)
                        p.wait("tensor", "lw", t_la)
                        p.wait("tensor", "act", p.st.get("zev", 0))
                        for c in range(NC):
                            p.emit("tensor", lambda e, c=c, cols=cols: e.matmul(
                                ps_z[...], la_sb[:, c, :], big_sb[:, c, cols],
                                start=(c == 0), stop=(c == NC - 1)),
                                "pe" if c == NC - 1 else None)
                        t_z = p.tag("pe")
                        p.wait("tensor", "act", p.st.get("lgev", 0))
                        for c in range(NC):
                            p.emit("tensor", lambda e, c=c, cols=cols: e.matmul(
                                ps_lg[...], rt_sb[:, c, :], big_sb[:, c, cols],
                                start=(c == 0), stop=False), None)
                        p.wait("tensor", "act", p.st["setup_done"])
                        t_lg = p.emit("tensor", lambda e: e.matmul(
                            ps_lg[...], c16_b[...], ones_b[0:1, 0:512], start=False, stop=True), "pe")
                        p.wait("scalar", "pe", t_z)
                        p.wait("scalar", "dve", p.st.get("zpd", 0))
                        t_zev = p.emit("scalar", lambda e: e.activation(z_sb[...], ps_z[...], AF.Copy), "act")
                        p.st["zev"] = t_zev
                        p.wait("scalar", "pe", t_lg)
                        t_lgev = p.emit("scalar", lambda e: e.activation(lg_b[...], ps_lg[...], AF.Copy), "act")
                        p.st["lgev"] = t_lgev
                        p.wait("tensor", "act", t_lgev)
                        p.wait("tensor", "dve", p.st.get("tpc", 0))
                        t_tp = 0
                        for s4 in range(4):
                            t_tp = p.emit("tensor", lambda e, s4=s4: e.transpose(
                                ps_tp[:, s4 * 16:(s4 + 1) * 16], lg_b[:, s4 * 128:(s4 + 1) * 128],
                                eye_b[0:16, 0:16]), "pe")
                        p.wait("vector", "pe", t_tp)
                        t_c = p.emit("vector", lambda e: e.tensor_copy(lg_tm[...], ps_tp[...]), "dve")
                        p.st["tpc"] = t_c
                        p.wait("vector", "dve", t_c)
                        t_mx = p.emit("vector", lambda e: e.tensor_reduce(
                            mx4[...], lg_tm.ap().rearrange("p (s e) -> p s e", e=16),
                            AX.X, ALU.max, negate=True), "dve")
                        t_ex = [0] * 4
                        for s4 in range(4):
                            p.wait("scalar", "dve", t_mx)
                            t_ex[s4] = p.emit("scalar", lambda e, s4=s4: e.activation(
                                et[:, s4 * 16:(s4 + 1) * 16], lg_tm[:, s4 * 16:(s4 + 1) * 16],
                                AF.Exp, bias=mx4[:, s4:s4 + 1]), "act")
                        t_ct = [0] * 4
                        for s4 in range(4):
                            p.wait("vector", "act", t_ex[s4])
                            t_ct[s4] = p.emit("vector", lambda e, s4=s4: e.tensor_tensor(
                                et[:, s4 * 16:(s4 + 1) * 16], et[:, s4 * 16:(s4 + 1) * 16],
                                cnt_bc[...], ALU.mult), "dve")
                        p.wait("vector", "dve", t_ct[3])
                        t_sm = p.emit("vector", lambda e: e.tensor_reduce(
                            sm4[...], et.ap().rearrange("p (s e) -> p s e", e=16), AX.X, ALU.add), "dve")
                        p.wait("vector", "dve", t_sm)
                        t_rc = p.emit("vector", lambda e: e.reciprocal(sm4[...], sm4[...]), "dve")
                        t_al = [0] * 4
                        for s4 in range(4):
                            p.wait("vector", "dve", t_rc)
                            t_al[s4] = p.emit("vector", lambda e, s4=s4: e.tensor_scalar(
                                al_b[:, s4 * 16:(s4 + 1) * 16], et[:, s4 * 16:(s4 + 1) * 16],
                                sm4[:, s4:s4 + 1], 2.0, ALU.mult, ALU.mult), "dve")
                        p.wait("tensor", "dve", t_al[3])
                        p.wait("tensor", "dve", p.st.get("emc", 0))
                        t_tb = 0
                        for s4 in range(4):
                            t_tb = p.emit("tensor", lambda e, s4=s4: e.transpose(
                                ps_tb[:, s4 * 128:(s4 + 1) * 128], al_b[:, s4 * 16:(s4 + 1) * 16],
                                eye_b[...]), "pe")
                        p.wait("vector", "pe", t_tb)
                        t_em = p.emit("vector", lambda e: e.tensor_copy(al_em[...], ps_tb[...]), "dve")
                        p.st["emc"] = t_em
                        p.wait("tensor", "dve", t_em)
                        p.wait("tensor", "dve", p.st.get("zpd", 0))
                        t_ax = p.emit("tensor", lambda e: e.matmul(
                            ps_ax[...], emat_b[...], al_em[...], start=True, stop=True), "pe")
                        p.wait("vector", "pe", t_ax)
                        p.wait("vector", "act", t_zev)
                        t_zp = p.emit("vector", lambda e, cols=cols: e.tensor_tensor(
                            zp_sb[:, cols], ps_ax[...], z_sb[...], ALU.mult), "dve")
                        p.st["zpd"] = t_zp
                    p.st["zp_done"] = p.st["zpd"]

                pb.phase(fn, tag)
        return run

    def make_ffn_b(resid_in, resid_out, tag):
        def run():
            with ExitStack() as es:
                lb_sb = es.enter_context(nc.sbuf_tensor(f"lb{tag}", [128, 4096], BF16))
                bias_row = es.enter_context(nc.sbuf_tensor(f"bb{tag}", [1, 4096], BF16))
                wbufs = [es.enter_context(nc.sbuf_tensor(f"wb{tag}{i}", [128, 4096], BF16)) for i in range(2)]
                ps_t = [[es.enter_context(nc.psum_tensor(f"ps{tag}{i}_{t}", [128, 512], F32)) for t in range(NTT)]
                        for i in range(2)]
                rt = [es.enter_context(nc.sbuf_tensor(f"rt{tag}{i}", [128, 512], F32)) for i in range(2)]
                ot = [es.enter_context(nc.sbuf_tensor(f"ot{tag}{i}", [128, 512], F32)) for i in range(2)]

                def fn(p):
                    p.emit("gpsimd", lambda e: e.dma_start(out=lb_sb[...], in_=lora_b_f[...]), "d6", 16)
                    t_lb = p.emit("gpsimd", lambda e: e.dma_start(out=bias_row[...], in_=bbase_i[...]), "d6", 16)
                    consume = resid_consume(tag, rt, ot, resid_in, resid_out, ps_t)
                    t_mm = [0] * 32
                    t_cons = [None] * 32
                    for ob in range(32):
                        par = ob % 2
                        if ob >= 2:
                            p.wait("gpsimd", "pe", t_mm[ob - 2])
                        t_w = p.emit("gpsimd", lambda e, ob=ob, par=par: e.dma_start(
                            out=wbufs[par][...], in_=wbase_t[ob]), f"d{4 + par}", 16)
                        p.wait("tensor", f"d{4 + par}", t_w)
                        p.wait("tensor", "d6", t_lb)
                        p.wait("tensor", "dve", p.st["zp_done"])
                        if ob >= 2:
                            c2 = t_cons[ob - 2]
                            p.wait("tensor", "dve", c2.get("dve", 0))
                        for c in range(NC):
                            for tt in range(NTT):
                                p.emit("tensor", lambda e, c=c, tt=tt, par=par: e.matmul(
                                    ps_t[par][tt][...], wbufs[par][:, c * 128:(c + 1) * 128],
                                    big_sb[:, c, tt * 512:(tt + 1) * 512],
                                    start=(c == 0), stop=False), None)
                        for tt in range(NTT):
                            p.emit("tensor", lambda e, tt=tt, par=par, ob=ob: e.matmul(
                                ps_t[par][tt][...], lb_sb[:, ob * 128:(ob + 1) * 128],
                                zp_sb[:, tt * 512:(tt + 1) * 512], start=False, stop=False), None)
                            t_mm[ob] = p.emit("tensor", lambda e, tt=tt, par=par, ob=ob: e.matmul(
                                ps_t[par][tt][...], bias_row[0:1, ob * 128:(ob + 1) * 128],
                                ones_b[0:1, 0:512], start=False, stop=True),
                                "pe" if tt == NTT - 1 else None) or t_mm[ob]
                        t_cons[ob] = consume(p, ob, par, t_mm[ob])

                pb.phase(fn, tag)
        return run

    def make_copy(src_fn, dst, tag, rows=D, dt=F32, width=T):
        def run():
            with ExitStack() as es:
                buf = [es.enter_context(nc.sbuf_tensor(f"cp{tag}{i}", [128, width], dt)) for i in range(2)]
                src = src_fn() if callable(src_fn) else src_fn

                def fn(p):
                    for c in range(rows // 128):
                        cp = c % 2
                        p.wait("sync", f"d{8 + cp}", p.st.get(f"{tag}_s{cp}", 0))
                        t_l = p.emit("sync", lambda e, c=c, cp=cp: e.dma_start(
                            out=buf[cp][...], in_=src[c * 128:(c + 1) * 128, :]), f"d{cp}", 16)
                        p.wait("scalar", f"d{cp}", t_l)
                        t_s = p.emit("scalar", lambda e, c=c, cp=cp: e.dma_start(
                            out=dst[c * 128:(c + 1) * 128, :], in_=buf[cp][...]), f"d{8 + cp}", 16)
                        p.st[f"{tag}_s{cp}"] = t_s

                pb.phase(fn, tag)
        return run

    phases = [
        ("setup", None),
        ("ln1", make_ln("xT", xT, "ln1", out_sb=big_sb)),
        ("ka", make_proj("ka", wk_t, 0, NG, "rope", cos_i=ropek_cos_i, sin_i=ropek_sin_i,
                         out_dram=cka_in, bias_i=bk_i, bias_n=1024)),
        ("va", make_proj("va", wv_t, 0, NG, "clip", out_dram=vstage, do_clip=False,
                         bias_i=bv_i, bias_n=1024)),
        ("vta", make_vtr(NG, vstage, cva_in, [(cka_in, cka_out), (cva_in, cva_out)], "vta")),
        ("qa", make_proj("qa", wq_t, 0, NH, "rope", cos_i=ropeq_cos_i, sin_i=ropeq_sin_i,
                         out_dram=q_buf, bias_i=bq_i, bias_n=4096)),
        ("attn_a", make_attn("a")),
        ("out_a", make_proj("out_a", wo_t, 0, 32, "resid", resid_in=xT, resid_out=r1,
                            bias_i=bo_i, bias_n=4096)),
        ("ln2", make_ln("r1", r1, "ln2", out_sb=big_sb)),
        ("ffn1a", make_ffn_a("f1")),
        ("ffn1b", make_ffn_b(r1, r2, "f1b")),
        ("ln3", make_ln("r2", r2, "ln3", out_sb=big_sb)),
        ("kb", make_proj("kb", wqkv_t, 32, 32, "clip", out_dram=kstage)),
        ("klnb", make_ln("kstage", kstage, "klnb", src_f32=False,
                         affine=(klnw_c, klnb_c), out_dram=ckb_in)),
        ("vb", make_proj("vb", wqkv_t, 64, 32, "clip", out_dram=vstage)),
        ("vtb", make_vtr(NH, vstage, cvb_in, [(ckb_in, ckb_out), (cvb_in, cvb_out)], "vtb")),
        ("qb", make_proj("qb", wqkv_t, 0, 32, "clip", out_dram=qstage)),
        ("qlnb", make_ln("qstage", qstage, "qlnb", src_f32=False,
                         affine=(qlnw_c, qlnb_c), out_dram=q_buf)),
        ("attn_b", make_attn("b")),
        ("out_b", make_proj("out_b", wout_t, 0, 32, "resid", resid_in=r2, resid_out=r3)),
        ("ln4", make_ln("r3", r3, "ln4", out_sb=big_sb)),
        ("ffn2a", make_ffn_a("f2")),
        ("ffn2b", make_ffn_b(r3, yT, "f2b")),
    ]

    resid_after = [xT, xT, xT, xT, xT, xT, xT, xT, r1, r1, r1, r2, r2, r2, r2,
                   r2, r2, r2, r2, r2, r3, r3, r3]

    nrun = min(upto, len(phases))
    for i in range(nrun):
        name, fn = phases[i]
        if name == "setup":
            with nc.psum_tensor("ps_set", [128, 16], F32) as ps:
                pb.phase(ph_setup_fn(ps), "setup")
        else:
            fn()
    if nrun < len(phases):
        make_copy(resid_after[nrun], yT, "fincp")()
    if dump is not None:
        if dump[0] == "big_sb":
            def dump_big(p):
                for c in range(NC):
                    t_s = p.emit("scalar", lambda e, c=c: e.dma_start(
                        out=dump_out[c * 128:(c + 1) * 128, :], in_=big_sb[:, c, :]), "st", 16)
                p.wait("scalar", "st", p.tag("st"))
            pb.phase(dump_big, "dump_big")
        elif dump[0] == "zp_sb":
            def dump_zp(p):
                t_s = p.emit("scalar", lambda e: e.dma_start(out=dump_out[...], in_=zp_sb[...]), "st", 16)
                p.wait("scalar", "st", p.tag("st"))
            pb.phase(dump_zp, "dump_zp")
        else:
            src = dram_map[dump[0]]
            make_copy(src, dump_out, "dmp", rows=dump[1][0], dt=dump[2], width=dump[1][1])()

    def fin(p):
        for s_ in ("st", "lx", "lw", "cc") + tuple(f"d{i}" for i in range(10)):
            p.wait("sync", s_, p.tag(s_))
    pb.phase(fin, "fin_wait")
    return nc



def _tile_w(w_in_out):
    i_, o_ = w_in_out.shape
    t = w_in_out.reshape(i_ // 128, 128, o_ // 128, 128)
    t = t.transpose(2, 1, 0, 3).reshape(o_ // 128, 128, (i_ // 128) * 128)
    return np.ascontiguousarray(t)


def prepare_in_maps(inputs):
    x = np.asarray(inputs["L0_attn_a_hidden_states"])
    f32 = np.float32

    wq_t = _tile_w(np.asarray(inputs["L0_attn_a_Wq"]))
    wk_t = _tile_w(np.asarray(inputs["L0_attn_a_Wk"]))
    wv_t = _tile_w(np.asarray(inputs["L0_attn_a_Wv"]))
    wo_t = _tile_w(np.asarray(inputs["L0_attn_a_Wo"]))
    wqkv_t = _tile_w(np.ascontiguousarray(np.asarray(inputs["L0_attn_b_Wqkv"]).T))
    wout_t = _tile_w(np.ascontiguousarray(np.asarray(inputs["L0_attn_b_Wout"]).T))
    wbase_t = _tile_w(np.ascontiguousarray(np.asarray(inputs["L0_ffn_W_base"]).T))
    lora_a = np.asarray(inputs["L0_ffn_lora_A"])
    lora_a_t = np.ascontiguousarray(
        lora_a.reshape(128, 4096).T.reshape(32, 128, 128).transpose(1, 0, 2))
    lora_b = np.asarray(inputs["L0_ffn_lora_B"])
    lora_b_f = np.ascontiguousarray(lora_b.transpose(0, 2, 1).reshape(128, 4096))
    router_w = np.asarray(inputs["L0_ffn_router_W"])
    router_t = np.ascontiguousarray(
        router_w.T.reshape(32, 128, 16).transpose(1, 0, 2))

    def row(v, n):
        return np.ascontiguousarray(np.asarray(v, dtype=f32).reshape(1, n))

    def col32(v):
        return np.ascontiguousarray(np.asarray(v, dtype=f32).reshape(32, 128).T)

    shared = dict(
        wq_t=wq_t, wk_t=wk_t, wv_t=wv_t, wo_t=wo_t, wqkv_t=wqkv_t,
        wout_t=wout_t, wbase_t=wbase_t, lora_a_t=lora_a_t, lora_b_f=lora_b_f,
        router_t=router_t,
        bq=row(inputs["L0_attn_a_bq"], 4096), bk=row(inputs["L0_attn_a_bk"], 1024),
        bv=row(inputs["L0_attn_a_bv"], 1024), bo=row(inputs["L0_attn_a_bo"], 4096),
        b_base=row(inputs["L0_ffn_b_base"], 4096),
        qlnw_col=col32(inputs["L0_attn_b_q_ln_w"]), qlnb_col=col32(inputs["L0_attn_b_q_ln_b"]),
        klnw_col=col32(inputs["L0_attn_b_k_ln_w"]), klnb_col=col32(inputs["L0_attn_b_k_ln_b"]),
        router_b=row(inputs["L0_ffn_router_b"], 16),
        prior=row(inputs["L0_ffn_domain_prior"], 16),
        active=np.ascontiguousarray(np.asarray(inputs["L0_ffn_active"], dtype=np.int32).reshape(1, 4)),
        iota16=np.arange(16, dtype=f32).reshape(1, 16),
        eye128=np.eye(128, dtype=f32),
    )
    emat = np.zeros((16, 128), dtype=f32)
    for e_ in range(16):
        emat[e_, e_ * 8:(e_ + 1) * 8] = 1.0
    shared["emat"] = emat

    half = 64
    inv_freq = 1.0 / (10000.0 ** (np.arange(half, dtype=np.float64) / half))
    m_ = np.arange(1, 33, dtype=np.float64) * (8.0 / 32)
    sl = 1.0 / (2.0 ** m_)

    al_k = np.zeros((128, 32, 8, 4), dtype=np.float64)
    pp = np.arange(128)
    for h in range(32):
        for j in range(4):
            for r in range(2):
                for d_ in range(4):
                    kpos = (2 * j + r) * 512 + d_ * 128 + pp
                    al_k[:, h, j * 2 + r, d_] = sl[h] * kpos
    shared["alibi_k"] = np.ascontiguousarray(al_k.reshape(128, 1024).astype(f32))

    NEG = -1.0e30

    def masks_for(parity):
        mk = np.zeros((128, 8, 512), dtype=f32)
        pcol = np.arange(128).reshape(128, 1)
        ccol = np.arange(512).reshape(1, 512)
        for r in range(2):
            for d_ in range(4):
                if r == parity:
                    mk[:, r * 4 + d_, :] = np.where(d_ * 128 + pcol <= ccol, 0.0, NEG)
                else:
                    mk[:, r * 4 + d_, :] = 0.0 if parity == 1 else NEG
        return np.ascontiguousarray(mk.reshape(128, 8 * 512))

    in_maps = []
    for core in range(8):
        b = core // 2
        parity = core % 2
        blocks = [2 * i + parity for i in range(4)]
        own_pos = np.concatenate([np.arange(g * 512, (g + 1) * 512) for g in blocks])
        xT_core = np.ascontiguousarray(x[b][own_pos, :].T.astype(f32))
        ang = own_pos.reshape(1, T).astype(np.float64) * inv_freq.reshape(half, 1)
        cos_t = np.cos(ang)
        sin_t = np.sin(ang)
        m = dict(shared)
        m.update(
            xT=xT_core,
            ropeq_cos=np.ascontiguousarray((cos_t * SCALE).astype(f32)),
            ropeq_sin=np.ascontiguousarray((sin_t * SCALE).astype(f32)),
            ropek_cos=np.ascontiguousarray(cos_t.astype(f32)),
            ropek_sin=np.ascontiguousarray(sin_t.astype(f32)),
            alibi_q=np.ascontiguousarray((sl.reshape(32, 1) * own_pos.reshape(1, T)).astype(f32)),
            masks=masks_for(parity),
        )
        in_maps.append(m)
    return in_maps


def assemble_output(results, key="yT"):
    out = np.zeros((B, S, D), dtype=np.float32)
    for core in range(8):
        b = core // 2
        parity = core % 2
        yc = results[core][key]
        for i in range(4):
            g = 2 * i + parity
            out[b, g * 512:(g + 1) * 512, :] = yc[:, i * 512:(i + 1) * 512].T
    return out


_BUILD_CFG = {"upto": 99, "dump": None, "debug_ln": False}


def kernel(**inputs):
    nc = build_program(upto=_BUILD_CFG["upto"], dump=_BUILD_CFG["dump"],
                       debug_ln=_BUILD_CFG.get("debug_ln", False))
    in_maps = prepare_in_maps(inputs)
    res = run_bass_kernel_spmd(nc, in_maps, core_ids=list(range(8)))
    if _BUILD_CFG["dump"] is not None:
        return res.results
    return assemble_output(res.results)


# version 02
# speedup vs baseline: 1.1532x; 1.1532x over previous
DEBUG_LN = False

import math
from contextlib import ExitStack

import numpy as np

import concourse.bass as bass
import concourse.mybir as mybir
from concourse.bass_utils import run_bass_kernel_spmd

F32 = mybir.dt.float32
BF16 = mybir.dt.bfloat16
I32 = mybir.dt.int32
AF = mybir.ActivationFunctionType
ALU = mybir.AluOpType
AX = mybir.AxisListType

D = 4096
B = 4
S = 4096
T = 2048
NC = 32
NTT = 4
NH = 32
NG = 8
LN_EPS = 1e-5
CLIP = 8.0
SCALE = 1.0 / math.sqrt(128.0)
SWEEPS = [(0, 3), (1, 2)]
NBLK = 4
RG = [[0, 1], [2, 3], [4, 5], [6, 7]]

ENGS = ("tensor", "scalar", "vector", "sync", "gpsimd")


class PB:

    def __init__(self, nc):
        self.nc = nc
        self.sems = {}
        self.ct = {}
        self.st = {}
        self.cur = None
        self.eng = None
        names = ["pe", "act", "dve", "pool", "lx", "lw", "st", "cc"] + [f"d{i}" for i in range(10)]
        for name in names:
            self.sems[name] = nc.alloc_semaphore("sem_" + name)
            self.ct[name] = 0

    def phase(self, fn, name=None):
        snap_ct = dict(self.ct)
        snap_st = dict(self.st)
        final = [None, None]
        with self.nc.Block(name) as blk:
            for engname in ENGS:
                self.ct = dict(snap_ct)
                self.st = dict(snap_st)
                self.cur = engname

                def body(eng):
                    self.eng = eng
                    for s_ in ("st", "d2", "d3", "d8", "d9", "cc"):
                        if self.ct[s_]:
                            eng.wait_ge(self.sems[s_], self.ct[s_])
                    fn(self)
                    if final[0] is None:
                        final[0] = dict(self.ct)
                        final[1] = dict(self.st)
                    else:
                        assert final[0] == self.ct, (
                            f"phase {name}: ct divergence on {self.cur}: {final[0]} vs {self.ct}")

                getattr(blk, engname)(body)
        self.ct = final[0]
        self.st = final[1]

    def emit(self, engine, fn, inc=None, n=1):
        if inc is not None and n == 16:
            self.wait(engine, inc, self.ct[inc])
        if self.cur == engine:
            inst = fn(self.eng)
            if inc is not None:
                inst.then_inc(self.sems[inc], n)
        if inc is not None:
            self.ct[inc] += n
            return self.ct[inc]
        return None

    def wait(self, engine, sem, val):
        if not val:
            return
        if self.cur == engine:
            self.eng.wait_ge(self.sems[sem], val)

    def tag(self, sem):
        return self.ct[sem]


def build_program(upto=99, dump=None, debug_ln=False):
    global DEBUG_LN
    DEBUG_LN = debug_ln
    nc = bass.Bass("TRN2", target_bir_lowering=False)

    def inp(name, shape, dt=F32):
        return nc.dram_tensor(name, list(shape), dt, kind="ExternalInput").ap()

    xT = inp("xT", [D, T])
    wq_t = inp("wq_t", [32, 128, 4096])
    wk_t = inp("wk_t", [8, 128, 4096])
    wv_t = inp("wv_t", [8, 128, 4096])
    wo_t = inp("wo_t", [32, 128, 4096])
    wqkv_t = inp("wqkv_t", [96, 128, 4096])
    wout_t = inp("wout_t", [32, 128, 4096])
    wbase_t = inp("wbase_t", [32, 128, 4096])
    lora_a_t = inp("lora_a_t", [128, 32, 128])
    lora_b_f = inp("lora_b_f", [128, 4096])
    router_t = inp("router_t", [128, 32, 16])
    bq_i = inp("bq", [1, 4096])
    bk_i = inp("bk", [1, 1024])
    bv_i = inp("bv", [1, 1024])
    bo_i = inp("bo", [1, 4096])
    bbase_i = inp("b_base", [1, 4096])
    qlnw_i = inp("qlnw_col", [128, 32])
    qlnb_i = inp("qlnb_col", [128, 32])
    klnw_i = inp("klnw_col", [128, 32])
    klnb_i = inp("klnb_col", [128, 32])
    rb_i = inp("router_b", [1, 16])
    prior_i = inp("prior", [1, 16])
    active_i = inp("active", [1, 4], I32)
    iota16_i = inp("iota16", [1, 16])
    eye_i = inp("eye128", [128, 128])
    emat_i = inp("emat", [16, 128])
    ropeq_cos_i = inp("ropeq_cos", [64, T])
    ropeq_sin_i = inp("ropeq_sin", [64, T])
    ropek_cos_i = inp("ropek_cos", [64, T])
    ropek_sin_i = inp("ropek_sin", [64, T])
    alibi_q_i = inp("alibi_q", [32, T])
    alibi_k_i = inp("alibi_k", [128, 1024])
    masks_i = inp("masks", [128, 8 * 512], F32)

    yT = nc.dram_tensor("yT", [D, T], F32, kind="ExternalOutput").ap()
    dump_out = None
    if dump is not None:
        dump_out = nc.dram_tensor("dump", list(dump[1]), dump[2], kind="ExternalOutput").ap()

    def internal(name, shape, dt):
        return nc.dram_tensor(name, list(shape), dt, kind="Internal").ap()

    r1 = internal("r1", [D, T], F32)
    r2 = internal("r2", [D, T], F32)
    r3 = internal("r3", [D, T], F32)
    q_buf = internal("q_buf", [D, T], BF16)
    kstage = internal("kstage", [D, T], BF16)
    qstage = internal("qstage", [D, T], BF16)
    vstage = internal("vstage", [D, T], BF16)
    cka_in = internal("cka_in", [1024, T], BF16)
    cka_out = internal("cka_out", [2048, T], BF16)
    cva_in = internal("cva_in", [NG * 16 * 128, 128], BF16)
    cva_out = internal("cva_out", [2 * NG * 16 * 128, 128], BF16)
    ckb_in = internal("ckb_in", [4096, T], BF16)
    ckb_out = internal("ckb_out", [8192, T], BF16)
    cvb_in = internal("cvb_in", [NH * 16 * 128, 128], BF16)
    cvb_out = internal("cvb_out", [2 * NH * 16 * 128, 128], BF16)
    dram_map = {"r1": r1, "r2": r2, "r3": r3, "q_buf": q_buf, "kstage": kstage,
                "qstage": qstage, "vstage": vstage, "cka_in": cka_in,
                "cka_out": cka_out, "cva_in": cva_in, "cva_out": cva_out,
                "ckb_in": ckb_in, "ckb_out": ckb_out, "cvb_in": cvb_in,
                "cvb_out": cvb_out, "xT": xT, "yT": yT, "ln": None}

    sbt = nc.alloc_sbuf_tensor
    big_guard = nc.sbuf_tensor("big_sb", [128, NC, T], BF16)
    big_sb = big_guard.__enter__()
    zp_sb = sbt("zp_sb", [128, T], BF16)
    ones_b = sbt("ones_b", [1, 512], BF16)
    ones_f = sbt("ones_f", [1, 128], F32)
    ones_col_b = sbt("ones_col_b", [128, 1], BF16)
    eps5 = sbt("eps5", [1, 1], F32)
    eps12 = sbt("eps12", [1, 1], F32)
    eye_b = sbt("eye_b", [128, 128], BF16)
    emat_b = sbt("emat_b", [16, 128], BF16)
    qlnw_c = sbt("qlnw_c", [128, 32], F32)
    qlnb_c = sbt("qlnb_c", [128, 32], F32)
    klnw_c = sbt("klnw_c", [128, 32], F32)
    klnb_c = sbt("klnb_c", [128, 32], F32)
    c16_b = sbt("c16_b", [1, 16], BF16)
    cnt_bc = sbt("cnt_bc", [128, 16], F32)
    actf = sbt("actf", [1, 4], F32)
    acti = sbt("acti", [1, 4], I32)
    iota16 = sbt("iota16_s", [1, 16], F32)
    msk_sb = sbt("msk_sb", [128, 8 * 512], BF16)
    alibi_k_sb = sbt("alibi_k_sb", [128, 1024], F32)
    tmp16a = sbt("tmp16a", [1, 16], F32)
    tmp16b = sbt("tmp16b", [1, 16], F32)
    cnt16 = sbt("cnt16", [1, 16], F32)
    rbf = sbt("rbf", [1, 16], F32)
    priorf = sbt("priorf", [1, 16], F32)

    pb = PB(nc)

    def ph_setup_fn(ps):
        def fn(p):
            t_lx = None
            for dst, src in ((qlnw_c, qlnw_i), (qlnb_c, qlnb_i), (klnw_c, klnw_i),
                             (klnb_c, klnb_i), (iota16, iota16_i), (alibi_k_sb, alibi_k_i),
                             (acti, active_i), (rbf, rb_i), (priorf, prior_i)):
                t_lx = p.emit("sync", lambda e, d=dst, s=src: e.dma_start(out=d[...], in_=s[...]), "lx", 16)
            for dst, src in ((eye_b, eye_i), (emat_b, emat_i), (msk_sb, masks_i)):
                p.emit("gpsimd", lambda e, d=dst, s=src: e.dma_start(out=d[...], in_=s[...]), "lw", 16)
            for tile, v in ((ones_b, 1.0), (ones_f, 1.0), (ones_col_b, 1.0),
                            (eps5, LN_EPS), (eps12, 1e-12), (cnt16, 0.0)):
                t_ms = p.emit("vector", lambda e, t=tile, v=v: e.memset(t[...], v), "dve")
            p.st["consts_done"] = t_ms
            def _preload(e):
                from concourse.hw_specs import get_activation_tables
                tables = get_activation_tables(nc.m.arch)
                idx = list(tables.keys()).index("natural_log_exp_and_others")
                return e.add_instruction(mybir.InstLoadActFuncSet(
                    name=nc.get_next_instruction_name(), act_func_set_id=idx,
                    ins=[], outs=[]))
            p.emit("scalar", _preload)
            p.wait("vector", "lx", t_lx)
            p.emit("vector", lambda e: e.tensor_scalar(qlnw_c[...], qlnw_c[...], SCALE, None, ALU.mult), "dve")
            p.emit("vector", lambda e: e.tensor_scalar(qlnb_c[...], qlnb_c[...], SCALE, None, ALU.mult), "dve")
            t3 = p.emit("vector", lambda e: e.tensor_copy(actf[...], acti[...]), "dve")
            tj = t3
            for j in range(4):
                p.wait("vector", "dve", tj)
                ta = p.emit("vector", lambda e, j=j: e.tensor_scalar(
                    tmp16a[...], iota16[...], actf[0:1, j:j + 1], None, ALU.is_equal), "dve")
                p.wait("vector", "dve", ta)
                tj = p.emit("vector", lambda e: e.tensor_tensor(cnt16[...], cnt16[...], tmp16a[...], ALU.add), "dve")
            p.wait("scalar", "lx", t_lx)
            p.wait("scalar", "dve", p.st["consts_done"])
            t4 = p.emit("scalar", lambda e: e.activation(tmp16b[...], priorf[...], AF.Ln, bias=eps12[0:1, 0:1]), "act")
            p.wait("vector", "act", t4)
            t5 = p.emit("vector", lambda e: e.tensor_tensor(tmp16b[...], tmp16b[...], rbf[...], ALU.add), "dve")
            p.wait("vector", "dve", t5)
            p.emit("vector", lambda e: e.tensor_copy(c16_b[...], tmp16b[...]), "dve")
            p.wait("tensor", "dve", tj)
            t7 = p.emit("tensor", lambda e: e.matmul(ps[...], ones_f[0:1, :], cnt16[...], start=True, stop=True), "pe")
            p.wait("scalar", "pe", t7)
            p.st["setup_done"] = p.emit("scalar", lambda e: e.activation(cnt_bc[...], ps[...], AF.Copy), "act")
            p.wait("gpsimd", "lw", p.tag("lw"))
            p.wait("sync", "lx", p.tag("lx"))
        return fn

    def ph_ln(p, bufs, src, src_f32, affine, out_sb, out_dram):
        (xc, xb, xq, rows_m, rows_r, rows_n, tmpn, tmp2, ps_sx, ps_sq, ps_b1, ps_b2) = bufs
        NS = 4
        for s in range(NS):
            cols = slice(s * 512, (s + 1) * 512)
            t_mm = [0] * NC
            for c in range(NC):
                cp = c % 2
                prev_mm = t_mm[c - 2] if c >= 2 else p.st.get("ln_mm_prev", 0)
                p.wait("sync", "pe", prev_mm)
                p.wait("sync", "dve", p.st.get(f"ln_xcuse{cp}", 0))
                t_ld = p.emit("sync", lambda e, c=c, cp=cp, cols=cols: e.dma_start(
                    out=xc[cp][...], in_=src[c * 128:(c + 1) * 128, cols]), f"d{cp}", 16)
                if src_f32:
                    p.wait("vector", f"d{cp}", t_ld)
                    p.wait("vector", "pe", prev_mm)
                    t_xb = p.emit("vector", lambda e, cp=cp: e.tensor_copy(xb[cp][...], xc[cp][...]), "dve")
                    p.st[f"ln_xcuse{cp}"] = t_xb
                else:
                    t_xb = 0
                p.wait("scalar", f"d{cp}", t_ld)
                p.wait("scalar", "pe", prev_mm)
                if src_f32:
                    p.wait("scalar", "dve", t_xb)
                t_xq = p.emit("scalar", lambda e, cp=cp: e.activation(
                    xq[cp][...], (xb[cp] if src_f32 else xc[cp])[...], AF.Square), "act")
                if src_f32:
                    p.wait("tensor", "dve", t_xb)
                p.wait("tensor", "act", t_xq)
                p.emit("tensor", lambda e, c=c, cp=cp: e.matmul(
                    ps_sx[0:1, :], ones_col_b[...], (xb[cp] if src_f32 else xc[cp])[...],
                    start=(c == 0), stop=(c == NC - 1)), None)
                t_mm[c] = p.emit("tensor", lambda e, c=c, cp=cp: e.matmul(
                    ps_sq[0:1, :], ones_col_b[...], xq[cp][...],
                    start=(c == 0), stop=(c == NC - 1)), "pe")
            p.st["ln_mm_prev"] = t_mm[NC - 1]
            m = rows_m[0:1, cols]
            rstd = rows_r[0:1, cols]
            nmr = rows_n[0:1, cols]
            p.wait("vector", "pe", t_mm[NC - 1])
            ta = p.emit("vector", lambda e, m=m: e.tensor_scalar(m, ps_sx[0:1, :], 1.0 / D, None, ALU.mult), "dve")
            tb = p.emit("vector", lambda e, nmr=nmr: e.tensor_scalar(nmr, ps_sq[0:1, :], 1.0 / D, None, ALU.mult), "dve")
            p.wait("vector", "dve", ta)
            tc = p.emit("vector", lambda e, m=m, rstd=rstd: e.tensor_tensor(rstd, m, m, ALU.mult), "dve")
            p.wait("vector", "dve", tc)
            td = p.emit("vector", lambda e, rstd=rstd, nmr=nmr: e.tensor_tensor(rstd, nmr, rstd, ALU.subtract), "dve")
            p.wait("scalar", "dve", td)
            te = p.emit("scalar", lambda e, rstd=rstd: e.activation(rstd, rstd, AF.Ln, bias=eps5[0:1, 0:1]), "act")
            p.wait("scalar", "act", te)
            tf = p.emit("scalar", lambda e, rstd=rstd: e.activation(rstd, rstd, AF.Exp, scale=-0.5), "act")
            p.wait("vector", "act", tf)
            tg = p.emit("vector", lambda e, m=m, rstd=rstd, nmr=nmr: e.scalar_tensor_tensor(
                nmr, m, -1.0, rstd, ALU.mult, ALU.mult), "dve")
            p.st[f"ln_rows{s}"] = (tf, tg)
            if DEBUG_LN:
                p.wait("scalar", "dve", tg)
                p.emit("scalar", lambda e, cols=cols: e.dma_start(
                    out=r3[0:1, cols], in_=rows_m[0:1, cols]), "st", 16)
                p.emit("scalar", lambda e, cols=cols: e.dma_start(
                    out=r3[1:2, cols], in_=rows_r[0:1, cols]), "st", 16)
                p.emit("scalar", lambda e, cols=cols: e.dma_start(
                    out=r3[2:3, cols], in_=rows_n[0:1, cols]), "st", 16)
        for s in range(NS):
            cols = slice(s * 512, (s + 1) * 512)
            tf, tg = p.st[f"ln_rows{s}"]
            p.wait("tensor", "act", tf)
            p.wait("tensor", "dve", tg)
            p.wait("tensor", "dve", p.st.get("ln_bc_free", 0))
            p.emit("tensor", lambda e, cols=cols: e.matmul(
                ps_b1[...], ones_f[0:1, :], rows_r[0:1, cols], start=True, stop=True), None)
            t_bc = p.emit("tensor", lambda e, cols=cols: e.matmul(
                ps_b2[...], ones_f[0:1, :], rows_n[0:1, cols], start=True, stop=True), "pe")
            t_t1 = [0] * NC
            t_t2 = [0] * NC
            for c in range(NC):
                cp = c % 2
                p.wait("sync", "pe", p.st.get("ln_mm_prev", 0))
                p.wait("sync", "dve", max(t_t1[c - 2] if c >= 2 else 0,
                                          p.st.get(f"ln_xcuse{cp}", 0)))
                t_ld = p.emit("sync", lambda e, c=c, cp=cp, cols=cols: e.dma_start(
                    out=xc[cp][...], in_=src[c * 128:(c + 1) * 128, cols]), f"d{cp}", 16)
                p.wait("vector", f"d{cp}", t_ld)
                p.wait("vector", "pe", t_bc)
                p.wait("vector", "dve", t_t2[c - 2] if c >= 2 else 0)
                t_t1[c] = p.emit("vector", lambda e, cp=cp: e.tensor_tensor(
                    tmpn[cp][...], xc[cp][...], ps_b1[...], ALU.mult), "dve")
                p.st[f"ln_xcuse{cp}"] = t_t1[c]
                p.wait("vector", "dve", t_t1[c])
                if affine is None:
                    t_t2[c] = p.emit("vector", lambda e, cp=cp, c=c, cols=cols: e.tensor_tensor(
                        out_sb[:, c, cols], tmpn[cp][...], ps_b2[...], ALU.add), "dve")
                else:
                    t2a = p.emit("vector", lambda e, cp=cp: e.tensor_tensor(
                        tmp2[cp][...], tmpn[cp][...], ps_b2[...], ALU.add), "dve")
                    p.wait("vector", "dve", t2a)
                    p.wait("vector", f"d{2 + cp}", p.st.get(f"ln_st{cp}", 0))
                    w_col, b_col = affine
                    t_t2[c] = p.emit("vector", lambda e, cp=cp, c=c: e.tensor_scalar(
                        xq[cp][...], tmp2[cp][...],
                        w_col[:, c:c + 1], b_col[:, c:c + 1], ALU.mult, ALU.add), "dve")
                if out_dram is not None:
                    p.wait("scalar", "dve", t_t2[c])
                    tl = p.emit("scalar", lambda e, c=c, cols=cols, cp=cp: e.dma_start(
                        out=out_dram[c * 128:(c + 1) * 128, cols], in_=xq[cp][...]), f"d{2 + cp}", 16)
                    p.st[f"ln_st{cp}"] = tl
            p.st["ln_bc_free"] = t_t2[NC - 1]

    def make_ln(src_name, src, tag, src_f32=True, affine=None, out_sb=None, out_dram=None):
        def run():
            with ExitStack() as es:
                xdt = F32 if src_f32 else BF16
                xc = [es.enter_context(nc.sbuf_tensor(f"xc{tag}{i}", [128, 512], xdt)) for i in range(2)]
                xb = [es.enter_context(nc.sbuf_tensor(f"xbb{tag}{i}", [128, 512], BF16)) for i in range(2)]
                xq = [es.enter_context(nc.sbuf_tensor(f"xqq{tag}{i}", [128, 512], BF16)) for i in range(2)]
                rows_m = es.enter_context(nc.sbuf_tensor(f"rm{tag}", [1, T], F32))
                rows_r = es.enter_context(nc.sbuf_tensor(f"rr{tag}", [1, T], F32))
                rows_n = es.enter_context(nc.sbuf_tensor(f"rn{tag}", [1, T], F32))
                tmpn = [es.enter_context(nc.sbuf_tensor(f"tn{tag}{i}", [128, 512], F32)) for i in range(2)]
                tmp2 = [es.enter_context(nc.sbuf_tensor(f"t2{tag}{i}", [128, 512], F32)) for i in range(2)]
                ps_sx = es.enter_context(nc.psum_tensor(f"px{tag}", [1, 512], F32))
                ps_sq = es.enter_context(nc.psum_tensor(f"pq{tag}", [1, 512], F32))
                ps_b1 = es.enter_context(nc.psum_tensor(f"p1{tag}", [128, 512], F32))
                ps_b2 = es.enter_context(nc.psum_tensor(f"p2{tag}", [128, 512], F32))
                bufs = (xc, xb, xq, rows_m, rows_r, rows_n, tmpn, tmp2, ps_sx, ps_sq, ps_b1, ps_b2)
                pb.phase(lambda p: ph_ln(p, bufs, src, src_f32, affine, out_sb, out_dram), tag)
        return run

    def ph_proj(p, wbufs, ps_t, wsrc, ob0, nob, bias_row, consume):
        t_mm = [0] * nob
        t_cons = [None] * nob
        for ob in range(nob):
            par = ob % 2
            if ob >= 2:
                p.wait("gpsimd", "pe", t_mm[ob - 2])
            t_w = p.emit("gpsimd", lambda e, ob=ob, par=par: e.dma_start(
                out=wbufs[par][...], in_=wsrc[ob0 + ob]), f"d{4 + par}", 16)
            p.wait("tensor", f"d{4 + par}", t_w)
            if ob >= 2:
                c2 = t_cons[ob - 2]
                p.wait("tensor", "dve", c2.get("dve", 0))
            for c in range(NC):
                for tt in range(NTT):
                    stop = (bias_row is None and c == NC - 1)
                    last = (bias_row is None and c == NC - 1 and tt == NTT - 1)
                    t_mm[ob] = p.emit("tensor", lambda e, c=c, tt=tt, par=par, stop=stop: e.matmul(
                        ps_t[par][tt][...], wbufs[par][:, c * 128:(c + 1) * 128],
                        big_sb[:, c, tt * 512:(tt + 1) * 512],
                        start=(c == 0), stop=stop), "pe" if last else None) or t_mm[ob]
            if bias_row is not None:
                for tt in range(NTT):
                    t_mm[ob] = p.emit("tensor", lambda e, tt=tt, par=par, ob=ob: e.matmul(
                        ps_t[par][tt][...], bias_row[0:1, ob * 128:(ob + 1) * 128],
                        ones_b[0:1, 0:512], start=False, stop=True),
                        "pe" if tt == NTT - 1 else None) or t_mm[ob]
            t_cons[ob] = consume(p, ob, par, t_mm[ob])

    def rope_consume(tag, ropc, rops, tmpr, rot, t_rope_ld, out_dram, ps_t):
        def consume(p, ob, par, t_mm):
            tl_act = None
            tl_dve = None
            for tt in range(NTT):
                kp = tt % 2
                ps = ps_t[par][tt]
                cols = slice(tt * 512, (tt + 1) * 512)
                p.wait("vector", "pe", t_mm)
                p.wait("vector", "d7", t_rope_ld)
                p.wait("vector", f"d{8 + kp}", p.st.get(f"{tag}_rst{kp}", 0))
                p.wait("vector", "dve", p.st.get(f"{tag}_ropeprev", 0))
                p.emit("vector", lambda e, ps=ps, cols=cols: e.tensor_tensor(
                    tmpr[0][...], ps[0:64, :], ropc[:, cols], ALU.mult), "dve")
                tb = p.emit("vector", lambda e, ps=ps, cols=cols: e.tensor_tensor(
                    tmpr[1][...], ps[64:128, :], rops[:, cols], ALU.mult), "dve")
                p.emit("vector", lambda e, ps=ps, cols=cols: e.tensor_tensor(
                    tmpr[2][...], ps[0:64, :], rops[:, cols], ALU.mult), "dve")
                td = p.emit("vector", lambda e, ps=ps, cols=cols: e.tensor_tensor(
                    tmpr[3][...], ps[64:128, :], ropc[:, cols], ALU.mult), "dve")
                p.wait("vector", "dve", tb)
                p.emit("vector", lambda e, kp=kp: e.tensor_tensor(
                    rot[kp][0:64, :], tmpr[0][...], tmpr[1][...], ALU.subtract), "dve")
                p.wait("vector", "dve", td)
                tf = p.emit("vector", lambda e, kp=kp: e.tensor_tensor(
                    rot[kp][64:128, :], tmpr[2][...], tmpr[3][...], ALU.add), "dve")
                p.st[f"{tag}_ropeprev"] = tf
                tl_dve = tf
                p.wait("scalar", "dve", tf)
                tl_act = p.emit("scalar", lambda e, ob=ob, cols=cols, kp=kp: e.dma_start(
                    out=out_dram[ob * 128:(ob + 1) * 128, cols], in_=rot[kp][...]), f"d{8 + kp}", 16)
                p.st[f"{tag}_rst{kp}"] = tl_act
            return {"dve": tl_dve, "act": tl_act}
        return consume

    def clip_consume(tag, kt, out_dram, ps_t, do_clip):
        def consume(p, ob, par, t_mm):
            tl_act = None
            tl_dve = None
            for tt in range(NTT):
                kp = tt % 2
                cols = slice(tt * 512, (tt + 1) * 512)
                p.wait("vector", "pe", t_mm)
                p.wait("vector", f"d{8 + kp}", p.st.get(f"{tag}_cst{kp}", 0))
                if do_clip:
                    ta = p.emit("vector", lambda e, kp=kp, par=par, tt=tt: e.tensor_scalar(
                        kt[kp][...], ps_t[par][tt][...], CLIP, -CLIP, ALU.min, ALU.max), "dve")
                else:
                    ta = p.emit("vector", lambda e, kp=kp, par=par, tt=tt: e.tensor_copy(
                        kt[kp][...], ps_t[par][tt][...]), "dve")
                tl_dve = ta
                p.wait("scalar", "dve", ta)
                tl_act = p.emit("scalar", lambda e, ob=ob, cols=cols, kp=kp: e.dma_start(
                    out=out_dram[ob * 128:(ob + 1) * 128, cols], in_=kt[kp][...]), f"d{8 + kp}", 16)
                p.st[f"{tag}_cst{kp}"] = tl_act
            return {"dve": tl_dve, "act": tl_act}
        return consume

    def resid_consume(tag, rt, ot, resid_in, resid_out, ps_t):
        def consume(p, ob, par, t_mm):
            tl_act = None
            tl_dve = None
            for tt in range(NTT):
                kp = tt % 2
                cols = slice(tt * 512, (tt + 1) * 512)
                p.wait("sync", "dve", p.st.get(f"{tag}_rr{kp}", 0))
                t_r = p.emit("sync", lambda e, ob=ob, cols=cols, kp=kp: e.dma_start(
                    out=rt[kp][...], in_=resid_in[ob * 128:(ob + 1) * 128, cols]), f"d{kp}", 16)
                p.wait("vector", "pe", t_mm)
                p.wait("vector", f"d{kp}", t_r)
                p.wait("vector", f"d{8 + kp}", p.st.get(f"{tag}_ro{kp}", 0))
                ta = p.emit("vector", lambda e, kp=kp, par=par, tt=tt: e.tensor_tensor(
                    ot[kp][...], ps_t[par][tt][...], rt[kp][...], ALU.add), "dve")
                p.st[f"{tag}_rr{kp}"] = ta
                tl_dve = ta
                p.wait("scalar", "dve", ta)
                tl_act = p.emit("scalar", lambda e, ob=ob, cols=cols, kp=kp: e.dma_start(
                    out=resid_out[ob * 128:(ob + 1) * 128, cols], in_=ot[kp][...]), f"d{8 + kp}", 16)
                p.st[f"{tag}_ro{kp}"] = tl_act
            return {"dve": tl_dve, "act": tl_act}
        return consume

    def make_proj(tag, wsrc, ob0, nob, kind, **kw):
        def run():
            with ExitStack() as es:
                wbufs = [es.enter_context(nc.sbuf_tensor(f"wb{tag}{i}", [128, 4096], BF16)) for i in range(2)]
                ps_t = [[es.enter_context(nc.psum_tensor(f"ps{tag}{i}_{t}", [128, 512], F32)) for t in range(NTT)]
                        for i in range(2)]
                bias_i = kw.get("bias_i")
                bias_row = None
                if bias_i is not None:
                    bias_row = es.enter_context(nc.sbuf_tensor(f"br{tag}", [1, kw["bias_n"]], BF16))
                if kind == "rope":
                    ropc = es.enter_context(nc.sbuf_tensor(f"rc{tag}", [64, T], F32))
                    rops = es.enter_context(nc.sbuf_tensor(f"rs{tag}", [64, T], F32))
                    tmpr = [es.enter_context(nc.sbuf_tensor(f"tr{tag}{i}", [64, 512], F32)) for i in range(4)]
                    rot = [es.enter_context(nc.sbuf_tensor(f"ro{tag}{i}", [128, 512], BF16)) for i in range(2)]
                elif kind == "clip":
                    kt = [es.enter_context(nc.sbuf_tensor(f"kt{tag}{i}", [128, 512], BF16)) for i in range(2)]
                else:
                    rt = [es.enter_context(nc.sbuf_tensor(f"rt{tag}{i}", [128, 512], F32)) for i in range(2)]
                    ot = [es.enter_context(nc.sbuf_tensor(f"ot{tag}{i}", [128, 512], F32)) for i in range(2)]

                def fn(p):
                    if bias_i is not None:
                        t_b = p.emit("gpsimd", lambda e: e.dma_start(out=bias_row[...], in_=bias_i[...]), "d6", 16)
                        p.wait("tensor", "d6", t_b)
                    if kind == "rope":
                        p.emit("sync", lambda e: e.dma_start(out=ropc[...], in_=kw["cos_i"][...]), "d7", 16)
                        t_rl = p.emit("sync", lambda e: e.dma_start(out=rops[...], in_=kw["sin_i"][...]), "d7", 16)
                        consume = rope_consume(tag, ropc, rops, tmpr, rot, t_rl, kw["out_dram"], ps_t)
                    elif kind == "clip":
                        consume = clip_consume(tag, kt, kw["out_dram"], ps_t, kw.get("do_clip", True))
                    else:
                        consume = resid_consume(tag, rt, ot, kw["resid_in"], kw["resid_out"], ps_t)
                    ph_proj(p, wbufs, ps_t, wsrc, ob0, nob, bias_row, consume)

                pb.phase(fn, tag)
        return run

    def make_vtr(nheads, v_in, v_stage_out, ccs, tag):
        def run():
            with ExitStack() as es:
                vin = [es.enter_context(nc.sbuf_tensor(f"vi{tag}{i}", [128, T], BF16)) for i in range(2)]
                vot = [es.enter_context(nc.sbuf_tensor(f"vo{tag}{i}", [128, 128], BF16)) for i in range(2)]
                ps_tr = [es.enter_context(nc.psum_tensor(f"pt{tag}{i}", [128, 128], BF16)) for i in range(2)]

                def fn(p):
                    for h in range(nheads):
                        hp = h % 2
                        p.wait("sync", "pe", p.st.get(f"{tag}_pe{hp}", 0))
                        t_ld = p.emit("sync", lambda e, h=h, hp=hp: e.dma_start(
                            out=vin[hp][...], in_=v_in[h * 128:(h + 1) * 128, :]), f"d{hp}", 16)
                        for kt_ in range(16):
                            kp = kt_ % 2
                            p.wait("tensor", f"d{hp}", t_ld)
                            p.wait("tensor", "dve", p.st.get(f"{tag}_cp{kp}", 0))
                            t_tr = p.emit("tensor", lambda e, kt_=kt_, hp=hp, kp=kp: e.transpose(
                                ps_tr[kp][...], vin[hp][:, kt_ * 128:(kt_ + 1) * 128], eye_b[...]), "pe")
                            p.wait("vector", "pe", t_tr)
                            p.wait("vector", f"d{8 + kp}", p.st.get(f"{tag}_st{kp}", 0))
                            t_cp = p.emit("vector", lambda e, kp=kp: e.tensor_copy(vot[kp][...], ps_tr[kp][...]), "dve")
                            p.st[f"{tag}_cp{kp}"] = t_cp
                            p.wait("scalar", "dve", t_cp)
                            t_st = p.emit("scalar", lambda e, h=h, kt_=kt_, kp=kp: e.dma_start(
                                out=v_stage_out[(h * 16 + kt_) * 128:(h * 16 + kt_ + 1) * 128, :],
                                in_=vot[kp][...]), f"d{8 + kp}", 16)
                            p.st[f"{tag}_st{kp}"] = t_st
                        p.st[f"{tag}_pe{hp}"] = p.tag("pe")
                    for s_ in ("st", "d2", "d3", "d8", "d9"):
                        p.wait("gpsimd", s_, p.tag(s_))
                    for cin, cout in ccs:
                        p.emit("gpsimd", lambda e, cin=cin, cout=cout: e.collective_compute(
                            "AllGather", ALU.bypass, replica_groups=RG,
                            ins=[cin[...]], outs=[cout[...]]), "cc")

                pb.phase(fn, tag)
        return run

    def make_attn(variant):
        n_outer = NG if variant == "a" else NH
        heads_per = 4 if variant == "a" else 1
        k_out = cka_out if variant == "a" else ckb_out
        v_out = cva_out if variant == "a" else cvb_out
        krows = 1024 if variant == "a" else 4096
        DEPTH = 6
        LAG = 3

        def run():
            with ExitStack() as es:
                K_sb = es.enter_context(nc.sbuf_tensor(f"Kx{variant}", [128, 4096], BF16))
                V_sb = es.enter_context(nc.sbuf_tensor(f"Vx{variant}", [128, 4096], BF16))
                Q_sb = es.enter_context(nc.sbuf_tensor(f"Qx{variant}", [128, T], BF16))
                probs = [es.enter_context(nc.sbuf_tensor(f"pr{variant}{i}", [128, 512], BF16)) for i in range(DEPTH)]
                aqb = es.enter_context(nc.sbuf_tensor(f"aqb{variant}", [1, T], BF16))
                sq2 = [es.enter_context(nc.sbuf_tensor(f"sq{variant}{i}", [128, 512], BF16)) for i in range(2)]
                nrow = es.enter_context(nc.sbuf_tensor(f"nr{variant}", [1, 512], F32))
                sc11 = es.enter_context(nc.sbuf_tensor(f"sc{variant}", [1, 8], F32))
                biasH = es.enter_context(nc.sbuf_tensor(f"bH{variant}", [128, 1], F32))
                bias32 = es.enter_context(nc.sbuf_tensor(f"b32{variant}", [128, 32], F32))
                rl = es.enter_context(nc.sbuf_tensor(f"rl{variant}", [1, 512], F32))
                rbx = es.enter_context(nc.sbuf_tensor(f"rb{variant}", [128, 512], F32))
                ps_r = [es.enter_context(nc.psum_tensor(f"psr{variant}{i}", [128, 512], F32)) for i in range(DEPTH)]
                ps_ctx = es.enter_context(nc.psum_tensor(f"px{variant}", [128, 512], F32))
                ps_l = es.enter_context(nc.psum_tensor(f"pl{variant}", [1, 512], F32))

                def norm_max(p, src_sb, nblk, out_col):
                    for i in range(nblk):
                        sp = i % 2
                        p.wait("scalar", "pe", p.st.get(f"sqf{sp}", 0))
                        tsq = p.emit("scalar", lambda e, i=i, sp=sp: e.activation(
                            sq2[sp][...], src_sb[:, i * 512:(i + 1) * 512], AF.Square), "act")
                        p.wait("tensor", "act", tsq)
                        p.wait("tensor", "dve", p.st.get("ps_l_free", 0))
                        p.wait("tensor", "act", p.st.get("nrow_ev", 0))
                        t_pe = p.emit("tensor", lambda e, sp=sp: e.matmul(
                            ps_l[0:1, :], ones_col_b[...], sq2[sp][...],
                            start=True, stop=True), "pe")
                        p.st[f"sqf{sp}"] = t_pe
                        p.wait("scalar", "pe", t_pe)
                        p.wait("scalar", "dve", p.st.get("nrow_used", 0))
                        t_ev = p.emit("scalar", lambda e: e.activation(nrow[...], ps_l[0:1, :], AF.Copy), "act")
                        p.st["nrow_ev"] = t_ev
                        p.st["ps_l_free"] = p.tag("dve")
                        p.wait("vector", "act", t_ev)
                        if i == 0:
                            t_red = p.emit("vector", lambda e, out_col=out_col: e.tensor_reduce(
                                sc11[0:1, out_col:out_col + 1], nrow[...], AX.X, ALU.max), "dve")
                        else:
                            t_red = p.emit("vector", lambda e: e.tensor_reduce(
                                sc11[0:1, 4:5], nrow[...], AX.X, ALU.max), "dve")
                            p.wait("vector", "dve", t_red)
                            t_red = p.emit("vector", lambda e, out_col=out_col: e.tensor_tensor(
                                sc11[0:1, out_col:out_col + 1], sc11[0:1, out_col:out_col + 1],
                                sc11[0:1, 4:5], ALU.max), "dve")
                        p.st["nrow_used"] = t_red
                    return t_red

                def fn(p):
                    p.wait("sync", "cc", p.tag("cc"))
                    for go in range(n_outer):
                        p.wait("sync", "pe", p.st.get("kv_pe", 0))
                        p.wait("sync", "act", p.st.get("kv_act", 0))
                        kv_tags = []
                        for r in range(2):
                            kv_tags.append(("lx" if r == 0 else "d0", p.emit(
                                "sync", lambda e, go=go, r=r: e.dma_start(
                                    out=K_sb[:, r * 2048:(r + 1) * 2048],
                                    in_=k_out[r * krows + go * 128: r * krows + go * 128 + 128, :]),
                                "lx" if r == 0 else "d0", 16)))
                            kv_tags.append(("d1" if r == 0 else "d3", p.emit(
                                "sync", lambda e, go=go, r=r: e.dma_start(
                                    out=V_sb.ap().rearrange("p (kt c) -> p kt c", c=128)[:, r * 16:(r + 1) * 16, :],
                                    in_=v_out[(r * n_outer + go) * 2048:(r * n_outer + go + 1) * 2048, :]
                                    .rearrange("(kt p) c -> p kt c", p=128)),
                                "d1" if r == 0 else "d3", 16)))
                        for s_, t_ in kv_tags:
                            p.wait("scalar", s_, t_)
                        norm_max(p, K_sb, 8, 0)
                        p.wait("scalar", "dve", p.st["nrow_used"])
                        tkl = p.emit("scalar", lambda e: e.activation(sc11[0:1, 1:2], sc11[0:1, 0:1], AF.Ln), "act")
                        for rh in range(heads_per):
                            h = go * heads_per + rh
                            p.wait("sync", "pe", p.st.get("q_pe", 0))
                            p.wait("sync", "act", p.st.get("q_act", 0))
                            t_q = p.emit("sync", lambda e, h=h: e.dma_start(
                                out=Q_sb[...], in_=q_buf[h * 128:(h + 1) * 128, :]), "d7", 16)
                            p.wait("scalar", "d7", t_q)
                            if variant == "b":
                                p.wait("gpsimd", "pe", p.st.get("q_pe", 0))
                                t_aq = p.emit("gpsimd", lambda e, h=h: e.dma_start(
                                    out=aqb[...], in_=alibi_q_i[h:h + 1, :]), "d6", 16)
                                p.wait("tensor", "d6", t_aq)
                            norm_max(p, Q_sb, 4, 2)
                            p.wait("scalar", "dve", p.st["nrow_used"])
                            p.wait("scalar", "act", tkl)
                            tq1 = p.emit("scalar", lambda e: e.activation(sc11[0:1, 3:4], sc11[0:1, 2:3], AF.Ln), "act")
                            p.wait("vector", "act", tq1)
                            tq2 = p.emit("vector", lambda e: e.tensor_tensor(
                                sc11[0:1, 5:6], sc11[0:1, 1:2], sc11[0:1, 3:4], ALU.add), "dve")
                            p.wait("scalar", "dve", tq2)
                            tq3 = p.emit("scalar", lambda e: e.activation(
                                sc11[0:1, 5:6], sc11[0:1, 5:6], AF.Exp, scale=0.5, bias=eps5[0:1, 0:1]), "act")
                            p.wait("vector", "act", tq3)
                            tq4 = p.emit("vector", lambda e: e.tensor_scalar(
                                sc11[0:1, 5:6], sc11[0:1, 5:6], -1.01, None, ALU.mult), "dve")
                            p.wait("tensor", "dve", tq4)
                            p.wait("tensor", "dve", p.st.get("ctx_free", 0))
                            tq5 = p.emit("tensor", lambda e: e.matmul(
                                ps_ctx[:, 0:1], ones_f[0:1, :], sc11[0:1, 5:6], start=True, stop=True), "pe")
                            p.wait("scalar", "pe", tq5)
                            t_bH = p.emit("scalar", lambda e: e.activation(biasH[...], ps_ctx[:, 0:1], AF.Copy), "act")
                            if variant == "b":
                                p.wait("vector", "act", t_bH)
                                p.wait("vector", "act", p.st.get("b32_used", 0))
                                t_b32 = p.emit("vector", lambda e, h=h: e.tensor_scalar(
                                    bias32[...], alibi_k_sb[:, h * 32:(h + 1) * 32],
                                    biasH[:, 0:1], None, ALU.add), "dve")
                            else:
                                t_b32 = 0
                            bias_ready = ("dve", t_b32) if variant == "b" else ("act", t_bH)

                            for own_i in range(NBLK):
                                qcols = slice(own_i * 512, (own_i + 1) * 512)
                                visits = []
                                for j in range(own_i + 1):
                                    for r_k in range(2):
                                        for dd in range(4):
                                            visits.append((j, r_k, dd))
                                nv = len(visits)
                                t_sc = [0] * nv
                                t_exp = [0] * nv
                                t_msk = [0] * nv
                                t_pv = [0] * nv

                                def emit_scores(p, v):
                                    j, r_k, dd = visits[v]
                                    sl_ = v % DEPTH
                                    kcol = r_k * 2048 + (j * 4 + dd) * 128
                                    if v >= DEPTH:
                                        p.wait("tensor", "act", t_exp[v - DEPTH])
                                        if t_msk[v - DEPTH]:
                                            p.wait("tensor", "dve", t_msk[v - DEPTH])
                                    else:
                                        p.wait("tensor", "act", p.st.get(f"ringfree{sl_}", 0))
                                        p.wait("tensor", "dve", p.st.get(f"ringfree_d{sl_}", 0))
                                    if variant == "b":
                                        p.emit("tensor", lambda e, sl_=sl_, kcol=kcol, qcols=qcols: e.matmul(
                                            ps_r[sl_][...], K_sb[:, kcol:kcol + 128], Q_sb[:, qcols],
                                            start=True, stop=False), None)
                                        t_sc[v] = p.emit("tensor", lambda e, sl_=sl_, qcols=qcols: e.matmul(
                                            ps_r[sl_][...], ones_b[0:1, 0:128], aqb[0:1, qcols],
                                            start=False, stop=True), "pe")
                                    else:
                                        t_sc[v] = p.emit("tensor", lambda e, sl_=sl_, kcol=kcol, qcols=qcols: e.matmul(
                                            ps_r[sl_][...], K_sb[:, kcol:kcol + 128], Q_sb[:, qcols],
                                            start=True, stop=True), "pe")
                                    j_, r_k_, dd_ = visits[v]
                                    if j_ == own_i:
                                        p.wait("vector", "pe", t_sc[v])
                                        p.wait("vector", "act", p.st.get("rb_used", 0))
                                        t_msk[v] = p.emit("vector", lambda e, sl_=sl_, r_k_=r_k_, dd_=dd_: e.tensor_tensor(
                                            ps_r[sl_][...], ps_r[sl_][...],
                                            msk_sb[:, (r_k_ * 4 + dd_) * 512:(r_k_ * 4 + dd_ + 1) * 512],
                                            ALU.add), "dve")
                                    p.wait("scalar", "pe", t_sc[v])
                                    if t_msk[v]:
                                        p.wait("scalar", "dve", t_msk[v])
                                    if v >= DEPTH:
                                        p.wait("scalar", "pe", t_pv[v - DEPTH] if t_pv[v - DEPTH] else 0)
                                    p.wait("scalar", bias_ready[0], bias_ready[1])
                                    if variant == "b":
                                        ki = (j_ * 2 + r_k_) * 4 + dd_
                                        t_exp[v] = p.emit("scalar", lambda e, sl_=sl_, ki=ki: e.activation(
                                            probs[sl_][...], ps_r[sl_][...], AF.Exp,
                                            bias=bias32[:, ki:ki + 1]), "act")
                                    else:
                                        t_exp[v] = p.emit("scalar", lambda e, sl_=sl_: e.activation(
                                            probs[sl_][...], ps_r[sl_][...], AF.Exp,
                                            bias=biasH[:, 0:1]), "act")

                                def emit_pv(p, v):
                                    j, r_k, dd = visits[v]
                                    sl_ = v % DEPTH
                                    kidx = r_k * 16 + j * 4 + dd
                                    p.wait("tensor", "act", t_exp[v])
                                    if v == 0:
                                        p.wait("tensor", "dve", p.st.get("ctx_free", 0))
                                        p.wait("tensor", "dve", p.st.get("ps_l_free", 0))
                                    p.emit("tensor", lambda e, sl_=sl_, kidx=kidx, v=v, nv=nv: e.matmul(
                                        ps_ctx[...], V_sb[:, kidx * 128:(kidx + 1) * 128], probs[sl_][...],
                                        start=(v == 0), stop=(v == nv - 1)), None)
                                    t_pv[v] = p.emit("tensor", lambda e, sl_=sl_, v=v, nv=nv: e.matmul(
                                        ps_l[0:1, :], ones_col_b[...], probs[sl_][...],
                                        start=(v == 0), stop=(v == nv - 1)), "pe")

                                for v in range(nv + LAG):
                                    if v < nv:
                                        emit_scores(p, v)
                                    if v - LAG >= 0:
                                        emit_pv(p, v - LAG)
                                p.wait("vector", "pe", t_pv[nv - 1])
                                tr1 = p.emit("vector", lambda e: e.reciprocal(rl[...], ps_l[0:1, :]), "dve")
                                p.st["ps_l_free"] = tr1
                                rslot = nv % DEPTH
                                p.wait("tensor", "dve", tr1)
                                p.wait("tensor", "act", t_exp[nv - 1] if nv >= 1 else 0)
                                p.wait("tensor", "act", t_exp[nv - DEPTH] if nv >= DEPTH else p.st.get(f"ringfree{rslot}", 0))
                                t_rb = p.emit("tensor", lambda e, rslot=rslot: e.matmul(
                                    ps_r[rslot][...], ones_f[0:1, :], rl[0:1, :], start=True, stop=True), "pe")
                                p.wait("scalar", "pe", t_rb)
                                tr3 = p.emit("scalar", lambda e, rslot=rslot: e.activation(
                                    rbx[...], ps_r[rslot][...], AF.Copy), "act")
                                p.st[f"ringfree{rslot}"] = tr3
                                p.st["rb_used"] = tr3
                                for sl_ in range(DEPTH):
                                    p.st[f"ringfree{sl_}"] = max(p.st.get(f"ringfree{sl_}", 0),
                                                                 t_exp[max(0, nv - DEPTH + (sl_ - nv % DEPTH) % DEPTH)] if nv else 0)
                                p.wait("vector", "act", tr3)
                                p.wait("vector", "pe", t_pv[nv - 1])
                                tr4 = p.emit("vector", lambda e, h=h, qcols=qcols: e.tensor_tensor(
                                    big_sb[:, h, qcols], ps_ctx[...], rbx[...], ALU.mult), "dve")
                                p.st["ctx_free"] = tr4
                                p.st["rb_used"] = max(p.st.get("rb_used", 0), p.tag("act"))
                                for sl_ in range(DEPTH):
                                    p.st[f"ringfree_d{sl_}"] = p.tag("dve")
                            p.st["b32_used"] = p.tag("act")
                            p.st["q_pe"] = p.tag("pe")
                            p.st["q_act"] = p.tag("act")
                        p.st["kv_pe"] = p.tag("pe")
                        p.st["kv_act"] = p.tag("act")

                pb.phase(fn, f"attn_{variant}")
        return run

    def make_ffn_a(tag):
        def run():
            with ExitStack() as es:
                la_sb = es.enter_context(nc.sbuf_tensor(f"la{tag}", [128, NC, 128], BF16))
                rt_sb = es.enter_context(nc.sbuf_tensor(f"rt{tag}", [128, NC, 16], BF16))
                z_sb = es.enter_context(nc.sbuf_tensor(f"zs{tag}", [128, 512], BF16))
                lg_b = es.enter_context(nc.sbuf_tensor(f"lg{tag}", [16, 512], BF16))
                lg_tm = es.enter_context(nc.sbuf_tensor(f"lt{tag}", [128, 64], F32))
                mx4 = es.enter_context(nc.sbuf_tensor(f"mx{tag}", [128, 4], F32))
                et = es.enter_context(nc.sbuf_tensor(f"et{tag}", [128, 64], F32))
                sm4 = es.enter_context(nc.sbuf_tensor(f"sm{tag}", [128, 4], F32))
                al_b = es.enter_context(nc.sbuf_tensor(f"ab{tag}", [128, 64], BF16))
                al_em = es.enter_context(nc.sbuf_tensor(f"ae{tag}", [16, 512], BF16))
                ps_z = es.enter_context(nc.psum_tensor(f"pz{tag}", [128, 512], F32))
                ps_lg = es.enter_context(nc.psum_tensor(f"pg{tag}", [16, 512], F32))
                ps_tp = es.enter_context(nc.psum_tensor(f"pp{tag}", [128, 64], BF16))
                ps_tb = es.enter_context(nc.psum_tensor(f"pb{tag}", [16, 512], BF16))
                ps_ax = es.enter_context(nc.psum_tensor(f"pa{tag}", [128, 512], F32))

                def fn(p):
                    p.emit("gpsimd", lambda e: e.dma_start(
                        out=la_sb.ap().rearrange("p c e -> p (c e)"),
                        in_=lora_a_t.rearrange("p c e -> p (c e)")), "lw", 16)
                    t_la = p.emit("gpsimd", lambda e: e.dma_start(
                        out=rt_sb.ap().rearrange("p c e -> p (c e)"),
                        in_=router_t.rearrange("p c e -> p (c e)")), "lw", 16)
                    for tt in range(NTT):
                        cols = slice(tt * 512, (tt + 1) * 512)
                        p.wait("tensor", "lw", t_la)
                        p.wait("tensor", "act", p.st.get("zev", 0))
                        for c in range(NC):
                            p.emit("tensor", lambda e, c=c, cols=cols: e.matmul(
                                ps_z[...], la_sb[:, c, :], big_sb[:, c, cols],
                                start=(c == 0), stop=(c == NC - 1)),
                                "pe" if c == NC - 1 else None)
                        t_z = p.tag("pe")
                        p.wait("tensor", "act", p.st.get("lgev", 0))
                        for c in range(NC):
                            p.emit("tensor", lambda e, c=c, cols=cols: e.matmul(
                                ps_lg[...], rt_sb[:, c, :], big_sb[:, c, cols],
                                start=(c == 0), stop=False), None)
                        p.wait("tensor", "act", p.st["setup_done"])
                        t_lg = p.emit("tensor", lambda e: e.matmul(
                            ps_lg[...], c16_b[...], ones_b[0:1, 0:512], start=False, stop=True), "pe")
                        p.wait("scalar", "pe", t_z)
                        p.wait("scalar", "dve", p.st.get("zpd", 0))
                        t_zev = p.emit("scalar", lambda e: e.activation(z_sb[...], ps_z[...], AF.Copy), "act")
                        p.st["zev"] = t_zev
                        p.wait("scalar", "pe", t_lg)
                        t_lgev = p.emit("scalar", lambda e: e.activation(lg_b[...], ps_lg[...], AF.Copy), "act")
                        p.st["lgev"] = t_lgev
                        p.wait("tensor", "act", t_lgev)
                        p.wait("tensor", "dve", p.st.get("tpc", 0))
                        t_tp = 0
                        for s4 in range(4):
                            t_tp = p.emit("tensor", lambda e, s4=s4: e.transpose(
                                ps_tp[:, s4 * 16:(s4 + 1) * 16], lg_b[:, s4 * 128:(s4 + 1) * 128],
                                eye_b[0:16, 0:16]), "pe")
                        p.wait("vector", "pe", t_tp)
                        t_c = p.emit("vector", lambda e: e.tensor_copy(lg_tm[...], ps_tp[...]), "dve")
                        p.st["tpc"] = t_c
                        p.wait("vector", "dve", t_c)
                        t_mx = p.emit("vector", lambda e: e.tensor_reduce(
                            mx4[...], lg_tm.ap().rearrange("p (s e) -> p s e", e=16),
                            AX.X, ALU.max, negate=True), "dve")
                        t_ex = [0] * 4
                        for s4 in range(4):
                            p.wait("scalar", "dve", t_mx)
                            t_ex[s4] = p.emit("scalar", lambda e, s4=s4: e.activation(
                                et[:, s4 * 16:(s4 + 1) * 16], lg_tm[:, s4 * 16:(s4 + 1) * 16],
                                AF.Exp, bias=mx4[:, s4:s4 + 1]), "act")
                        t_ct = [0] * 4
                        for s4 in range(4):
                            p.wait("vector", "act", t_ex[s4])
                            t_ct[s4] = p.emit("vector", lambda e, s4=s4: e.tensor_tensor(
                                et[:, s4 * 16:(s4 + 1) * 16], et[:, s4 * 16:(s4 + 1) * 16],
                                cnt_bc[...], ALU.mult), "dve")
                        p.wait("vector", "dve", t_ct[3])
                        t_sm = p.emit("vector", lambda e: e.tensor_reduce(
                            sm4[...], et.ap().rearrange("p (s e) -> p s e", e=16), AX.X, ALU.add), "dve")
                        p.wait("vector", "dve", t_sm)
                        t_rc = p.emit("vector", lambda e: e.reciprocal(sm4[...], sm4[...]), "dve")
                        t_al = [0] * 4
                        for s4 in range(4):
                            p.wait("vector", "dve", t_rc)
                            t_al[s4] = p.emit("vector", lambda e, s4=s4: e.tensor_scalar(
                                al_b[:, s4 * 16:(s4 + 1) * 16], et[:, s4 * 16:(s4 + 1) * 16],
                                sm4[:, s4:s4 + 1], 2.0, ALU.mult, ALU.mult), "dve")
                        p.wait("tensor", "dve", t_al[3])
                        p.wait("tensor", "dve", p.st.get("emc", 0))
                        t_tb = 0
                        for s4 in range(4):
                            t_tb = p.emit("tensor", lambda e, s4=s4: e.transpose(
                                ps_tb[:, s4 * 128:(s4 + 1) * 128], al_b[:, s4 * 16:(s4 + 1) * 16],
                                eye_b[...]), "pe")
                        p.wait("vector", "pe", t_tb)
                        t_em = p.emit("vector", lambda e: e.tensor_copy(al_em[...], ps_tb[...]), "dve")
                        p.st["emc"] = t_em
                        p.wait("tensor", "dve", t_em)
                        p.wait("tensor", "dve", p.st.get("zpd", 0))
                        t_ax = p.emit("tensor", lambda e: e.matmul(
                            ps_ax[...], emat_b[...], al_em[...], start=True, stop=True), "pe")
                        p.wait("vector", "pe", t_ax)
                        p.wait("vector", "act", t_zev)
                        t_zp = p.emit("vector", lambda e, cols=cols: e.tensor_tensor(
                            zp_sb[:, cols], ps_ax[...], z_sb[...], ALU.mult), "dve")
                        p.st["zpd"] = t_zp
                    p.st["zp_done"] = p.st["zpd"]

                pb.phase(fn, tag)
        return run

    def make_ffn_b(resid_in, resid_out, tag):
        def run():
            with ExitStack() as es:
                lb_sb = es.enter_context(nc.sbuf_tensor(f"lb{tag}", [128, 4096], BF16))
                bias_row = es.enter_context(nc.sbuf_tensor(f"bb{tag}", [1, 4096], BF16))
                wbufs = [es.enter_context(nc.sbuf_tensor(f"wb{tag}{i}", [128, 4096], BF16)) for i in range(2)]
                ps_t = [[es.enter_context(nc.psum_tensor(f"ps{tag}{i}_{t}", [128, 512], F32)) for t in range(NTT)]
                        for i in range(2)]
                rt = [es.enter_context(nc.sbuf_tensor(f"rt{tag}{i}", [128, 512], F32)) for i in range(2)]
                ot = [es.enter_context(nc.sbuf_tensor(f"ot{tag}{i}", [128, 512], F32)) for i in range(2)]

                def fn(p):
                    p.emit("gpsimd", lambda e: e.dma_start(out=lb_sb[...], in_=lora_b_f[...]), "d6", 16)
                    t_lb = p.emit("gpsimd", lambda e: e.dma_start(out=bias_row[...], in_=bbase_i[...]), "d6", 16)
                    consume = resid_consume(tag, rt, ot, resid_in, resid_out, ps_t)
                    t_mm = [0] * 32
                    t_cons = [None] * 32
                    for ob in range(32):
                        par = ob % 2
                        if ob >= 2:
                            p.wait("gpsimd", "pe", t_mm[ob - 2])
                        t_w = p.emit("gpsimd", lambda e, ob=ob, par=par: e.dma_start(
                            out=wbufs[par][...], in_=wbase_t[ob]), f"d{4 + par}", 16)
                        p.wait("tensor", f"d{4 + par}", t_w)
                        p.wait("tensor", "d6", t_lb)
                        p.wait("tensor", "dve", p.st["zp_done"])
                        if ob >= 2:
                            c2 = t_cons[ob - 2]
                            p.wait("tensor", "dve", c2.get("dve", 0))
                        for c in range(NC):
                            for tt in range(NTT):
                                p.emit("tensor", lambda e, c=c, tt=tt, par=par: e.matmul(
                                    ps_t[par][tt][...], wbufs[par][:, c * 128:(c + 1) * 128],
                                    big_sb[:, c, tt * 512:(tt + 1) * 512],
                                    start=(c == 0), stop=False), None)
                        for tt in range(NTT):
                            p.emit("tensor", lambda e, tt=tt, par=par, ob=ob: e.matmul(
                                ps_t[par][tt][...], lb_sb[:, ob * 128:(ob + 1) * 128],
                                zp_sb[:, tt * 512:(tt + 1) * 512], start=False, stop=False), None)
                            t_mm[ob] = p.emit("tensor", lambda e, tt=tt, par=par, ob=ob: e.matmul(
                                ps_t[par][tt][...], bias_row[0:1, ob * 128:(ob + 1) * 128],
                                ones_b[0:1, 0:512], start=False, stop=True),
                                "pe" if tt == NTT - 1 else None) or t_mm[ob]
                        t_cons[ob] = consume(p, ob, par, t_mm[ob])

                pb.phase(fn, tag)
        return run

    def make_copy(src_fn, dst, tag, rows=D, dt=F32, width=T):
        def run():
            with ExitStack() as es:
                buf = [es.enter_context(nc.sbuf_tensor(f"cp{tag}{i}", [128, width], dt)) for i in range(2)]
                src = src_fn() if callable(src_fn) else src_fn

                def fn(p):
                    for c in range(rows // 128):
                        cp = c % 2
                        p.wait("sync", f"d{8 + cp}", p.st.get(f"{tag}_s{cp}", 0))
                        t_l = p.emit("sync", lambda e, c=c, cp=cp: e.dma_start(
                            out=buf[cp][...], in_=src[c * 128:(c + 1) * 128, :]), f"d{cp}", 16)
                        p.wait("scalar", f"d{cp}", t_l)
                        t_s = p.emit("scalar", lambda e, c=c, cp=cp: e.dma_start(
                            out=dst[c * 128:(c + 1) * 128, :], in_=buf[cp][...]), f"d{8 + cp}", 16)
                        p.st[f"{tag}_s{cp}"] = t_s

                pb.phase(fn, tag)
        return run

    phases = [
        ("setup", None),
        ("ln1", make_ln("xT", xT, "ln1", out_sb=big_sb)),
        ("ka", make_proj("ka", wk_t, 0, NG, "rope", cos_i=ropek_cos_i, sin_i=ropek_sin_i,
                         out_dram=cka_in, bias_i=bk_i, bias_n=1024)),
        ("va", make_proj("va", wv_t, 0, NG, "clip", out_dram=vstage, do_clip=False,
                         bias_i=bv_i, bias_n=1024)),
        ("vta", make_vtr(NG, vstage, cva_in, [(cka_in, cka_out), (cva_in, cva_out)], "vta")),
        ("qa", make_proj("qa", wq_t, 0, NH, "rope", cos_i=ropeq_cos_i, sin_i=ropeq_sin_i,
                         out_dram=q_buf, bias_i=bq_i, bias_n=4096)),
        ("attn_a", make_attn("a")),
        ("out_a", make_proj("out_a", wo_t, 0, 32, "resid", resid_in=xT, resid_out=r1,
                            bias_i=bo_i, bias_n=4096)),
        ("ln2", make_ln("r1", r1, "ln2", out_sb=big_sb)),
        ("ffn1a", make_ffn_a("f1")),
        ("ffn1b", make_ffn_b(r1, r2, "f1b")),
        ("ln3", make_ln("r2", r2, "ln3", out_sb=big_sb)),
        ("kb", make_proj("kb", wqkv_t, 32, 32, "clip", out_dram=kstage)),
        ("klnb", make_ln("kstage", kstage, "klnb", src_f32=False,
                         affine=(klnw_c, klnb_c), out_dram=ckb_in)),
        ("vb", make_proj("vb", wqkv_t, 64, 32, "clip", out_dram=vstage)),
        ("vtb", make_vtr(NH, vstage, cvb_in, [(ckb_in, ckb_out), (cvb_in, cvb_out)], "vtb")),
        ("qb", make_proj("qb", wqkv_t, 0, 32, "clip", out_dram=qstage)),
        ("qlnb", make_ln("qstage", qstage, "qlnb", src_f32=False,
                         affine=(qlnw_c, qlnb_c), out_dram=q_buf)),
        ("attn_b", make_attn("b")),
        ("out_b", make_proj("out_b", wout_t, 0, 32, "resid", resid_in=r2, resid_out=r3)),
        ("ln4", make_ln("r3", r3, "ln4", out_sb=big_sb)),
        ("ffn2a", make_ffn_a("f2")),
        ("ffn2b", make_ffn_b(r3, yT, "f2b")),
    ]

    resid_after = [xT, xT, xT, xT, xT, xT, xT, xT, r1, r1, r1, r2, r2, r2, r2,
                   r2, r2, r2, r2, r2, r3, r3, r3]

    nrun = min(upto, len(phases))
    for i in range(nrun):
        name, fn = phases[i]
        if name == "setup":
            with nc.psum_tensor("ps_set", [128, 16], F32) as ps:
                pb.phase(ph_setup_fn(ps), "setup")
        else:
            fn()
    if nrun < len(phases):
        make_copy(resid_after[nrun], yT, "fincp")()
    if dump is not None:
        if dump[0] == "big_sb":
            def dump_big(p):
                for c in range(NC):
                    t_s = p.emit("scalar", lambda e, c=c: e.dma_start(
                        out=dump_out[c * 128:(c + 1) * 128, :], in_=big_sb[:, c, :]), "st", 16)
                p.wait("scalar", "st", p.tag("st"))
            pb.phase(dump_big, "dump_big")
        elif dump[0] == "zp_sb":
            def dump_zp(p):
                t_s = p.emit("scalar", lambda e: e.dma_start(out=dump_out[...], in_=zp_sb[...]), "st", 16)
                p.wait("scalar", "st", p.tag("st"))
            pb.phase(dump_zp, "dump_zp")
        else:
            src = dram_map[dump[0]]
            make_copy(src, dump_out, "dmp", rows=dump[1][0], dt=dump[2], width=dump[1][1])()

    def fin(p):
        for s_ in ("st", "lx", "lw", "cc") + tuple(f"d{i}" for i in range(10)):
            p.wait("sync", s_, p.tag(s_))
    pb.phase(fin, "fin_wait")
    return nc



def _tile_w(w_in_out):
    i_, o_ = w_in_out.shape
    t = w_in_out.reshape(i_ // 128, 128, o_ // 128, 128)
    t = t.transpose(2, 1, 0, 3).reshape(o_ // 128, 128, (i_ // 128) * 128)
    return np.ascontiguousarray(t)


def prepare_in_maps(inputs):
    x = np.asarray(inputs["L0_attn_a_hidden_states"])
    f32 = np.float32

    wq_t = _tile_w(np.asarray(inputs["L0_attn_a_Wq"]))
    wk_t = _tile_w(np.asarray(inputs["L0_attn_a_Wk"]))
    wv_t = _tile_w(np.asarray(inputs["L0_attn_a_Wv"]))
    wo_t = _tile_w(np.asarray(inputs["L0_attn_a_Wo"]))
    wqkv_t = _tile_w(np.ascontiguousarray(np.asarray(inputs["L0_attn_b_Wqkv"]).T))
    wout_t = _tile_w(np.ascontiguousarray(np.asarray(inputs["L0_attn_b_Wout"]).T))
    wbase_t = _tile_w(np.ascontiguousarray(np.asarray(inputs["L0_ffn_W_base"]).T))
    lora_a = np.asarray(inputs["L0_ffn_lora_A"])
    lora_a_t = np.ascontiguousarray(
        lora_a.reshape(128, 4096).T.reshape(32, 128, 128).transpose(1, 0, 2))
    lora_b = np.asarray(inputs["L0_ffn_lora_B"])
    lora_b_f = np.ascontiguousarray(lora_b.transpose(0, 2, 1).reshape(128, 4096))
    router_w = np.asarray(inputs["L0_ffn_router_W"])
    router_t = np.ascontiguousarray(
        router_w.T.reshape(32, 128, 16).transpose(1, 0, 2))

    def row(v, n):
        return np.ascontiguousarray(np.asarray(v, dtype=f32).reshape(1, n))

    def col32(v):
        return np.ascontiguousarray(np.asarray(v, dtype=f32).reshape(32, 128).T)

    shared = dict(
        wq_t=wq_t, wk_t=wk_t, wv_t=wv_t, wo_t=wo_t, wqkv_t=wqkv_t,
        wout_t=wout_t, wbase_t=wbase_t, lora_a_t=lora_a_t, lora_b_f=lora_b_f,
        router_t=router_t,
        bq=row(inputs["L0_attn_a_bq"], 4096), bk=row(inputs["L0_attn_a_bk"], 1024),
        bv=row(inputs["L0_attn_a_bv"], 1024), bo=row(inputs["L0_attn_a_bo"], 4096),
        b_base=row(inputs["L0_ffn_b_base"], 4096),
        qlnw_col=col32(inputs["L0_attn_b_q_ln_w"]), qlnb_col=col32(inputs["L0_attn_b_q_ln_b"]),
        klnw_col=col32(inputs["L0_attn_b_k_ln_w"]), klnb_col=col32(inputs["L0_attn_b_k_ln_b"]),
        router_b=row(inputs["L0_ffn_router_b"], 16),
        prior=row(inputs["L0_ffn_domain_prior"], 16),
        active=np.ascontiguousarray(np.asarray(inputs["L0_ffn_active"], dtype=np.int32).reshape(1, 4)),
        iota16=np.arange(16, dtype=f32).reshape(1, 16),
        eye128=np.eye(128, dtype=f32),
    )
    emat = np.zeros((16, 128), dtype=f32)
    for e_ in range(16):
        emat[e_, e_ * 8:(e_ + 1) * 8] = 1.0
    shared["emat"] = emat

    half = 64
    inv_freq = 1.0 / (10000.0 ** (np.arange(half, dtype=np.float64) / half))
    m_ = np.arange(1, 33, dtype=np.float64) * (8.0 / 32)
    sl = 1.0 / (2.0 ** m_)

    al_k = np.zeros((128, 32, 8, 4), dtype=np.float64)
    pp = np.arange(128)
    for h in range(32):
        for j in range(4):
            for r in range(2):
                for d_ in range(4):
                    kpos = (2 * j + r) * 512 + d_ * 128 + pp
                    al_k[:, h, j * 2 + r, d_] = sl[h] * kpos
    shared["alibi_k"] = np.ascontiguousarray(al_k.reshape(128, 1024).astype(f32))

    NEG = -1.0e30

    def masks_for(parity):
        mk = np.zeros((128, 8, 512), dtype=f32)
        pcol = np.arange(128).reshape(128, 1)
        ccol = np.arange(512).reshape(1, 512)
        for r in range(2):
            for d_ in range(4):
                if r == parity:
                    mk[:, r * 4 + d_, :] = np.where(d_ * 128 + pcol <= ccol, 0.0, NEG)
                else:
                    mk[:, r * 4 + d_, :] = 0.0 if parity == 1 else NEG
        return np.ascontiguousarray(mk.reshape(128, 8 * 512))

    in_maps = []
    for core in range(8):
        b = core // 2
        parity = core % 2
        blocks = [2 * i + parity for i in range(4)]
        own_pos = np.concatenate([np.arange(g * 512, (g + 1) * 512) for g in blocks])
        xT_core = np.ascontiguousarray(x[b][own_pos, :].T.astype(f32))
        ang = own_pos.reshape(1, T).astype(np.float64) * inv_freq.reshape(half, 1)
        cos_t = np.cos(ang)
        sin_t = np.sin(ang)
        m = dict(shared)
        m.update(
            xT=xT_core,
            ropeq_cos=np.ascontiguousarray((cos_t * SCALE).astype(f32)),
            ropeq_sin=np.ascontiguousarray((sin_t * SCALE).astype(f32)),
            ropek_cos=np.ascontiguousarray(cos_t.astype(f32)),
            ropek_sin=np.ascontiguousarray(sin_t.astype(f32)),
            alibi_q=np.ascontiguousarray((-sl.reshape(32, 1) * own_pos.reshape(1, T)).astype(f32)),
            masks=masks_for(parity),
        )
        in_maps.append(m)
    return in_maps


def assemble_output(results, key="yT"):
    out = np.zeros((B, S, D), dtype=np.float32)
    for core in range(8):
        b = core // 2
        parity = core % 2
        yc = results[core][key]
        for i in range(4):
            g = 2 * i + parity
            out[b, g * 512:(g + 1) * 512, :] = yc[:, i * 512:(i + 1) * 512].T
    return out


_BUILD_CFG = {"upto": 99, "dump": None, "debug_ln": False}


def kernel(**inputs):
    nc = build_program(upto=_BUILD_CFG["upto"], dump=_BUILD_CFG["dump"],
                       debug_ln=_BUILD_CFG.get("debug_ln", False))
    in_maps = prepare_in_maps(inputs)
    res = run_bass_kernel_spmd(nc, in_maps, core_ids=list(range(8)))
    if _BUILD_CFG["dump"] is not None:
        return res.results
    return assemble_output(res.results)


# version 03
# speedup vs baseline: 1.3847x; 1.2008x over previous
DEBUG_LN = False

import math
from contextlib import ExitStack

import numpy as np

import concourse.bass as bass
import concourse.mybir as mybir
from concourse.bass_utils import run_bass_kernel_spmd

F32 = mybir.dt.float32
BF16 = mybir.dt.bfloat16
I32 = mybir.dt.int32
AF = mybir.ActivationFunctionType
ALU = mybir.AluOpType
AX = mybir.AxisListType

D = 4096
B = 4
S = 4096
T = 2048
NC = 32
NTT = 4
NH = 32
NG = 8
LN_EPS = 1e-5
CLIP = 8.0
SCALE = 1.0 / math.sqrt(128.0)
SWEEPS = [(0, 3), (1, 2)]
NBLK = 4
RG = [[0, 1], [2, 3], [4, 5], [6, 7]]

ENGS = ("tensor", "scalar", "vector", "sync", "gpsimd")


class PB:

    def __init__(self, nc):
        self.nc = nc
        self.sems = {}
        self.ct = {}
        self.st = {}
        self.cur = None
        self.eng = None
        names = ["pe", "act", "dve", "pool", "lx", "lw", "st", "cc"] + [f"d{i}" for i in range(10)]
        for name in names:
            self.sems[name] = nc.alloc_semaphore("sem_" + name)
            self.ct[name] = 0

    def phase(self, fn, name=None):
        snap_ct = dict(self.ct)
        snap_st = dict(self.st)
        final = [None, None]
        with self.nc.Block(name) as blk:
            for engname in ENGS:
                self.ct = dict(snap_ct)
                self.st = dict(snap_st)
                self.cur = engname

                def body(eng):
                    self.eng = eng
                    for s_ in ("st", "d2", "d3", "d8", "d9", "cc"):
                        if self.ct[s_]:
                            eng.wait_ge(self.sems[s_], self.ct[s_])
                    fn(self)
                    if final[0] is None:
                        final[0] = dict(self.ct)
                        final[1] = dict(self.st)
                    else:
                        assert final[0] == self.ct, (
                            f"phase {name}: ct divergence on {self.cur}: {final[0]} vs {self.ct}")

                getattr(blk, engname)(body)
        self.ct = final[0]
        self.st = final[1]

    def emit(self, engine, fn, inc=None, n=1):
        if inc is not None and n == 16:
            self.wait(engine, inc, self.ct[inc])
        if self.cur == engine:
            inst = fn(self.eng)
            if inc is not None:
                inst.then_inc(self.sems[inc], n)
        if inc is not None:
            self.ct[inc] += n
            return self.ct[inc]
        return None

    def wait(self, engine, sem, val):
        if not val:
            return
        if self.cur == engine:
            self.eng.wait_ge(self.sems[sem], val)

    def tag(self, sem):
        return self.ct[sem]


def build_program(upto=99, dump=None, debug_ln=False):
    global DEBUG_LN
    DEBUG_LN = debug_ln
    nc = bass.Bass("TRN2", target_bir_lowering=False)

    def inp(name, shape, dt=F32):
        return nc.dram_tensor(name, list(shape), dt, kind="ExternalInput").ap()

    xT = inp("xT", [D, T])
    wq_t = inp("wq_t", [32, 128, 4096])
    wk_t = inp("wk_t", [8, 128, 4096])
    wv_t = inp("wv_t", [8, 128, 4096])
    wo_t = inp("wo_t", [32, 128, 4096])
    wqkv_t = inp("wqkv_t", [96, 128, 4096])
    wout_t = inp("wout_t", [32, 128, 4096])
    wbase_t = inp("wbase_t", [32, 128, 4096])
    lora_a_t = inp("lora_a_t", [128, 32, 128])
    lora_b_f = inp("lora_b_f", [128, 4096])
    router_t = inp("router_t", [128, 32, 16])
    bq_i = inp("bq", [1, 4096])
    bk_i = inp("bk", [1, 1024])
    bv_i = inp("bv", [1, 1024])
    bo_i = inp("bo", [1, 4096])
    bbase_i = inp("b_base", [1, 4096])
    qlnw_i = inp("qlnw_col", [128, 32])
    qlnb_i = inp("qlnb_col", [128, 32])
    klnw_i = inp("klnw_col", [128, 32])
    klnb_i = inp("klnb_col", [128, 32])
    rb_i = inp("router_b", [1, 16])
    prior_i = inp("prior", [1, 16])
    active_i = inp("active", [1, 4], I32)
    iota16_i = inp("iota16", [1, 16])
    eye_i = inp("eye128", [128, 128])
    emat_i = inp("emat", [16, 128])
    ropeq_cos_i = inp("ropeq_cos", [64, T])
    ropeq_sin_i = inp("ropeq_sin", [64, T])
    ropek_cos_i = inp("ropek_cos", [64, T])
    ropek_sin_i = inp("ropek_sin", [64, T])
    alibi_q_i = inp("alibi_q", [32, T])
    alibi_k_i = inp("alibi_k", [128, 1024])
    masks_i = inp("masks", [128, 8 * 512], F32)

    yT = nc.dram_tensor("yT", [D, T], F32, kind="ExternalOutput").ap()
    dump_out = None
    if dump is not None:
        dump_out = nc.dram_tensor("dump", list(dump[1]), dump[2], kind="ExternalOutput").ap()

    def internal(name, shape, dt):
        return nc.dram_tensor(name, list(shape), dt, kind="Internal").ap()

    r1 = internal("r1", [D, T], F32)
    r2 = internal("r2", [D, T], F32)
    r3 = internal("r3", [D, T], F32)
    q_buf = internal("q_buf", [D, T], BF16)
    kstage = internal("kstage", [D, T], BF16)
    qstage = internal("qstage", [D, T], BF16)
    vstage = internal("vstage", [D, T], BF16)
    cka_in = internal("cka_in", [1024, T], BF16)
    cka_out = internal("cka_out", [2048, T], BF16)
    cva_in = internal("cva_in", [NG * 16 * 128, 128], BF16)
    cva_out = internal("cva_out", [2 * NG * 16 * 128, 128], BF16)
    ckb_in = internal("ckb_in", [4096, T], BF16)
    ckb_out = internal("ckb_out", [8192, T], BF16)
    cvb_in = internal("cvb_in", [NH * 16 * 128, 128], BF16)
    cvb_out = internal("cvb_out", [2 * NH * 16 * 128, 128], BF16)
    dram_map = {"r1": r1, "r2": r2, "r3": r3, "q_buf": q_buf, "kstage": kstage,
                "qstage": qstage, "vstage": vstage, "cka_in": cka_in,
                "cka_out": cka_out, "cva_in": cva_in, "cva_out": cva_out,
                "ckb_in": ckb_in, "ckb_out": ckb_out, "cvb_in": cvb_in,
                "cvb_out": cvb_out, "xT": xT, "yT": yT, "ln": None}

    sbt = nc.alloc_sbuf_tensor
    big_guard = nc.sbuf_tensor("big_sb", [128, NC, T], BF16)
    big_sb = big_guard.__enter__()
    zp_sb = sbt("zp_sb", [128, T], BF16)
    ones_b = sbt("ones_b", [1, 512], BF16)
    ones_f = sbt("ones_f", [1, 128], F32)
    ones_col_b = sbt("ones_col_b", [128, 1], BF16)
    eps5 = sbt("eps5", [1, 1], F32)
    eps12 = sbt("eps12", [1, 1], F32)
    eye_b = sbt("eye_b", [128, 128], BF16)
    emat_b = sbt("emat_b", [16, 128], BF16)
    qlnw_c = sbt("qlnw_c", [128, 32], F32)
    qlnb_c = sbt("qlnb_c", [128, 32], F32)
    klnw_c = sbt("klnw_c", [128, 32], F32)
    klnb_c = sbt("klnb_c", [128, 32], F32)
    c16_b = sbt("c16_b", [1, 16], BF16)
    cnt_bc = sbt("cnt_bc", [128, 16], F32)
    actf = sbt("actf", [1, 4], F32)
    acti = sbt("acti", [1, 4], I32)
    iota16 = sbt("iota16_s", [1, 16], F32)
    msk_sb = sbt("msk_sb", [128, 8 * 512], BF16)
    alibi_k_sb = sbt("alibi_k_sb", [128, 1024], F32)
    tmp16a = sbt("tmp16a", [1, 16], F32)
    tmp16b = sbt("tmp16b", [1, 16], F32)
    cnt16 = sbt("cnt16", [1, 16], F32)
    rbf = sbt("rbf", [1, 16], F32)
    priorf = sbt("priorf", [1, 16], F32)
    nrm = sbt("nrm", [1, 320], F32)

    pb = PB(nc)

    def ph_setup_fn(ps):
        def fn(p):
            t_lx = None
            for dst, src in ((qlnw_c, qlnw_i), (qlnb_c, qlnb_i), (klnw_c, klnw_i),
                             (klnb_c, klnb_i), (iota16, iota16_i), (alibi_k_sb, alibi_k_i),
                             (acti, active_i), (rbf, rb_i), (priorf, prior_i)):
                t_lx = p.emit("sync", lambda e, d=dst, s=src: e.dma_start(out=d[...], in_=s[...]), "lx", 16)
            for dst, src in ((eye_b, eye_i), (emat_b, emat_i), (msk_sb, masks_i)):
                p.emit("gpsimd", lambda e, d=dst, s=src: e.dma_start(out=d[...], in_=s[...]), "lw", 16)
            for tile, v in ((ones_b, 1.0), (ones_f, 1.0), (ones_col_b, 1.0),
                            (eps5, LN_EPS), (eps12, 1e-12), (cnt16, 0.0)):
                t_ms = p.emit("vector", lambda e, t=tile, v=v: e.memset(t[...], v), "dve")
            p.st["consts_done"] = t_ms
            def _preload(e):
                from concourse.hw_specs import get_activation_tables
                tables = get_activation_tables(nc.m.arch)
                idx = list(tables.keys()).index("natural_log_exp_and_others")
                return e.add_instruction(mybir.InstLoadActFuncSet(
                    name=nc.get_next_instruction_name(), act_func_set_id=idx,
                    ins=[], outs=[]))
            p.emit("scalar", _preload)
            p.wait("vector", "lx", t_lx)
            p.emit("vector", lambda e: e.tensor_scalar(qlnw_c[...], qlnw_c[...], SCALE, None, ALU.mult), "dve")
            p.emit("vector", lambda e: e.tensor_scalar(qlnb_c[...], qlnb_c[...], SCALE, None, ALU.mult), "dve")
            t3 = p.emit("vector", lambda e: e.tensor_copy(actf[...], acti[...]), "dve")
            tj = t3
            for j in range(4):
                p.wait("vector", "dve", tj)
                ta = p.emit("vector", lambda e, j=j: e.tensor_scalar(
                    tmp16a[...], iota16[...], actf[0:1, j:j + 1], None, ALU.is_equal), "dve")
                p.wait("vector", "dve", ta)
                tj = p.emit("vector", lambda e: e.tensor_tensor(cnt16[...], cnt16[...], tmp16a[...], ALU.add), "dve")
            p.wait("scalar", "lx", t_lx)
            p.wait("scalar", "dve", p.st["consts_done"])
            t4 = p.emit("scalar", lambda e: e.activation(tmp16b[...], priorf[...], AF.Ln, bias=eps12[0:1, 0:1]), "act")
            p.wait("vector", "act", t4)
            t5 = p.emit("vector", lambda e: e.tensor_tensor(tmp16b[...], tmp16b[...], rbf[...], ALU.add), "dve")
            p.wait("vector", "dve", t5)
            p.emit("vector", lambda e: e.tensor_copy(c16_b[...], tmp16b[...]), "dve")
            p.wait("tensor", "dve", tj)
            t7 = p.emit("tensor", lambda e: e.matmul(ps[...], ones_f[0:1, :], cnt16[...], start=True, stop=True), "pe")
            p.wait("scalar", "pe", t7)
            p.st["setup_done"] = p.emit("scalar", lambda e: e.activation(cnt_bc[...], ps[...], AF.Copy), "act")
            p.wait("gpsimd", "lw", p.tag("lw"))
            p.wait("sync", "lx", p.tag("lx"))
        return fn

    def ph_ln(p, bufs, src, src_f32, affine, out_sb, out_dram):
        (xc, xb, xq, rows_m, rows_r, rows_n, tmpn, tmp2, ps_sx, ps_sq, ps_b1, ps_b2) = bufs
        NS = 4
        for s in range(NS):
            cols = slice(s * 512, (s + 1) * 512)
            t_mm = [0] * NC
            for c in range(NC):
                cp = c % 2
                prev_mm = t_mm[c - 2] if c >= 2 else p.st.get("ln_mm_prev", 0)
                p.wait("sync", "pe", prev_mm)
                p.wait("sync", "dve", p.st.get(f"ln_xcuse{cp}", 0))
                t_ld = p.emit("sync", lambda e, c=c, cp=cp, cols=cols: e.dma_start(
                    out=xc[cp][...], in_=src[c * 128:(c + 1) * 128, cols]), f"d{cp}", 16)
                if src_f32:
                    p.wait("vector", f"d{cp}", t_ld)
                    p.wait("vector", "pe", prev_mm)
                    t_xb = p.emit("vector", lambda e, cp=cp: e.tensor_copy(xb[cp][...], xc[cp][...]), "dve")
                    p.st[f"ln_xcuse{cp}"] = t_xb
                else:
                    t_xb = 0
                p.wait("scalar", f"d{cp}", t_ld)
                p.wait("scalar", "pe", prev_mm)
                if src_f32:
                    p.wait("scalar", "dve", t_xb)
                t_xq = p.emit("scalar", lambda e, cp=cp: e.activation(
                    xq[cp][...], (xb[cp] if src_f32 else xc[cp])[...], AF.Square), "act")
                if src_f32:
                    p.wait("tensor", "dve", t_xb)
                p.wait("tensor", "act", t_xq)
                p.emit("tensor", lambda e, c=c, cp=cp: e.matmul(
                    ps_sx[0:1, :], ones_col_b[...], (xb[cp] if src_f32 else xc[cp])[...],
                    start=(c == 0), stop=(c == NC - 1)), None)
                t_mm[c] = p.emit("tensor", lambda e, c=c, cp=cp: e.matmul(
                    ps_sq[0:1, :], ones_col_b[...], xq[cp][...],
                    start=(c == 0), stop=(c == NC - 1)), "pe")
            p.st["ln_mm_prev"] = t_mm[NC - 1]
            m = rows_m[0:1, cols]
            rstd = rows_r[0:1, cols]
            nmr = rows_n[0:1, cols]
            p.wait("vector", "pe", t_mm[NC - 1])
            ta = p.emit("vector", lambda e, m=m: e.tensor_scalar(m, ps_sx[0:1, :], 1.0 / D, None, ALU.mult), "dve")
            tb = p.emit("vector", lambda e, nmr=nmr: e.tensor_scalar(nmr, ps_sq[0:1, :], 1.0 / D, None, ALU.mult), "dve")
            p.wait("vector", "dve", ta)
            tc = p.emit("vector", lambda e, m=m, rstd=rstd: e.tensor_tensor(rstd, m, m, ALU.mult), "dve")
            p.wait("vector", "dve", tc)
            td = p.emit("vector", lambda e, rstd=rstd, nmr=nmr: e.tensor_tensor(rstd, nmr, rstd, ALU.subtract), "dve")
            p.wait("scalar", "dve", td)
            te = p.emit("scalar", lambda e, rstd=rstd: e.activation(rstd, rstd, AF.Ln, bias=eps5[0:1, 0:1]), "act")
            p.wait("scalar", "act", te)
            tf = p.emit("scalar", lambda e, rstd=rstd: e.activation(rstd, rstd, AF.Exp, scale=-0.5), "act")
            p.wait("vector", "act", tf)
            tg = p.emit("vector", lambda e, m=m, rstd=rstd, nmr=nmr: e.scalar_tensor_tensor(
                nmr, m, -1.0, rstd, ALU.mult, ALU.mult), "dve")
            p.st[f"ln_rows{s}"] = (tf, tg)
            if DEBUG_LN:
                p.wait("scalar", "dve", tg)
                p.emit("scalar", lambda e, cols=cols: e.dma_start(
                    out=r3[0:1, cols], in_=rows_m[0:1, cols]), "st", 16)
                p.emit("scalar", lambda e, cols=cols: e.dma_start(
                    out=r3[1:2, cols], in_=rows_r[0:1, cols]), "st", 16)
                p.emit("scalar", lambda e, cols=cols: e.dma_start(
                    out=r3[2:3, cols], in_=rows_n[0:1, cols]), "st", 16)
        for s in range(NS):
            cols = slice(s * 512, (s + 1) * 512)
            tf, tg = p.st[f"ln_rows{s}"]
            p.wait("tensor", "act", tf)
            p.wait("tensor", "dve", tg)
            p.wait("tensor", "dve", p.st.get("ln_bc_free", 0))
            p.emit("tensor", lambda e, cols=cols: e.matmul(
                ps_b1[...], ones_f[0:1, :], rows_r[0:1, cols], start=True, stop=True), None)
            t_bc = p.emit("tensor", lambda e, cols=cols: e.matmul(
                ps_b2[...], ones_f[0:1, :], rows_n[0:1, cols], start=True, stop=True), "pe")
            t_t1 = [0] * NC
            t_t2 = [0] * NC
            for c in range(NC):
                cp = c % 2
                p.wait("sync", "pe", p.st.get("ln_mm_prev", 0))
                p.wait("sync", "dve", max(t_t1[c - 2] if c >= 2 else 0,
                                          p.st.get(f"ln_xcuse{cp}", 0)))
                t_ld = p.emit("sync", lambda e, c=c, cp=cp, cols=cols: e.dma_start(
                    out=xc[cp][...], in_=src[c * 128:(c + 1) * 128, cols]), f"d{cp}", 16)
                p.wait("vector", f"d{cp}", t_ld)
                p.wait("vector", "pe", t_bc)
                p.wait("vector", "dve", t_t2[c - 2] if c >= 2 else 0)
                t_t1[c] = p.emit("vector", lambda e, cp=cp: e.tensor_tensor(
                    tmpn[cp][...], xc[cp][...], ps_b1[...], ALU.mult), "dve")
                p.st[f"ln_xcuse{cp}"] = t_t1[c]
                p.wait("vector", "dve", t_t1[c])
                if affine is None:
                    t_t2[c] = p.emit("vector", lambda e, cp=cp, c=c, cols=cols: e.tensor_tensor(
                        out_sb[:, c, cols], tmpn[cp][...], ps_b2[...], ALU.add), "dve")
                else:
                    t2a = p.emit("vector", lambda e, cp=cp: e.tensor_tensor(
                        tmp2[cp][...], tmpn[cp][...], ps_b2[...], ALU.add), "dve")
                    p.wait("vector", "dve", t2a)
                    p.wait("vector", f"d{2 + cp}", p.st.get(f"ln_st{cp}", 0))
                    w_col, b_col = affine
                    t_t2[c] = p.emit("vector", lambda e, cp=cp, c=c: e.tensor_scalar(
                        xq[cp][...], tmp2[cp][...],
                        w_col[:, c:c + 1], b_col[:, c:c + 1], ALU.mult, ALU.add), "dve")
                if out_dram is not None:
                    p.wait("scalar", "dve", t_t2[c])
                    tl = p.emit("scalar", lambda e, c=c, cols=cols, cp=cp: e.dma_start(
                        out=out_dram[c * 128:(c + 1) * 128, cols], in_=xq[cp][...]), f"d{2 + cp}", 16)
                    p.st[f"ln_st{cp}"] = tl
            p.st["ln_bc_free"] = t_t2[NC - 1]

    def make_ln(src_name, src, tag, src_f32=True, affine=None, out_sb=None, out_dram=None):
        def run():
            with ExitStack() as es:
                xdt = F32 if src_f32 else BF16
                xc = [es.enter_context(nc.sbuf_tensor(f"xc{tag}{i}", [128, 512], xdt)) for i in range(2)]
                xb = [es.enter_context(nc.sbuf_tensor(f"xbb{tag}{i}", [128, 512], BF16)) for i in range(2)]
                xq = [es.enter_context(nc.sbuf_tensor(f"xqq{tag}{i}", [128, 512], BF16)) for i in range(2)]
                rows_m = es.enter_context(nc.sbuf_tensor(f"rm{tag}", [1, T], F32))
                rows_r = es.enter_context(nc.sbuf_tensor(f"rr{tag}", [1, T], F32))
                rows_n = es.enter_context(nc.sbuf_tensor(f"rn{tag}", [1, T], F32))
                tmpn = [es.enter_context(nc.sbuf_tensor(f"tn{tag}{i}", [128, 512], F32)) for i in range(2)]
                tmp2 = [es.enter_context(nc.sbuf_tensor(f"t2{tag}{i}", [128, 512], F32)) for i in range(2)]
                ps_sx = es.enter_context(nc.psum_tensor(f"px{tag}", [1, 512], F32))
                ps_sq = es.enter_context(nc.psum_tensor(f"pq{tag}", [1, 512], F32))
                ps_b1 = es.enter_context(nc.psum_tensor(f"p1{tag}", [128, 512], F32))
                ps_b2 = es.enter_context(nc.psum_tensor(f"p2{tag}", [128, 512], F32))
                bufs = (xc, xb, xq, rows_m, rows_r, rows_n, tmpn, tmp2, ps_sx, ps_sq, ps_b1, ps_b2)
                pb.phase(lambda p: ph_ln(p, bufs, src, src_f32, affine, out_sb, out_dram), tag)
        return run

    def ph_proj(p, wbufs, ps_t, wsrc, ob0, nob, bias_row, consume):
        t_mm = [0] * nob
        t_cons = [None] * nob
        for ob in range(nob):
            par = ob % 2
            if ob >= 2:
                p.wait("gpsimd", "pe", t_mm[ob - 2])
            t_w = p.emit("gpsimd", lambda e, ob=ob, par=par: e.dma_start(
                out=wbufs[par][...], in_=wsrc[ob0 + ob]), f"d{4 + par}", 16)
            p.wait("tensor", f"d{4 + par}", t_w)
            if ob >= 2:
                c2 = t_cons[ob - 2]
                p.wait("tensor", "dve", c2.get("dve", 0))
            for c in range(NC):
                for tt in range(NTT):
                    stop = (bias_row is None and c == NC - 1)
                    last = (bias_row is None and c == NC - 1 and tt == NTT - 1)
                    t_mm[ob] = p.emit("tensor", lambda e, c=c, tt=tt, par=par, stop=stop: e.matmul(
                        ps_t[par][tt][...], wbufs[par][:, c * 128:(c + 1) * 128],
                        big_sb[:, c, tt * 512:(tt + 1) * 512],
                        start=(c == 0), stop=stop), "pe" if last else None) or t_mm[ob]
            if bias_row is not None:
                for tt in range(NTT):
                    t_mm[ob] = p.emit("tensor", lambda e, tt=tt, par=par, ob=ob: e.matmul(
                        ps_t[par][tt][...], bias_row[0:1, ob * 128:(ob + 1) * 128],
                        ones_b[0:1, 0:512], start=False, stop=True),
                        "pe" if tt == NTT - 1 else None) or t_mm[ob]
            t_cons[ob] = consume(p, ob, par, t_mm[ob])

    def rope_consume(tag, ropc, rops, tmpr, rot, t_rope_ld, out_dram, ps_t):
        def consume(p, ob, par, t_mm):
            tl_act = None
            tl_dve = None
            for tt in range(NTT):
                kp = tt % 2
                ps = ps_t[par][tt]
                cols = slice(tt * 512, (tt + 1) * 512)
                p.wait("vector", "pe", t_mm)
                p.wait("vector", "d7", t_rope_ld)
                p.wait("vector", f"d{8 + kp}", p.st.get(f"{tag}_rst{kp}", 0))
                p.wait("vector", "dve", p.st.get(f"{tag}_ropeprev", 0))
                p.emit("vector", lambda e, ps=ps, cols=cols: e.tensor_tensor(
                    tmpr[0][...], ps[0:64, :], ropc[:, cols], ALU.mult), "dve")
                tb = p.emit("vector", lambda e, ps=ps, cols=cols: e.tensor_tensor(
                    tmpr[1][...], ps[64:128, :], rops[:, cols], ALU.mult), "dve")
                p.emit("vector", lambda e, ps=ps, cols=cols: e.tensor_tensor(
                    tmpr[2][...], ps[0:64, :], rops[:, cols], ALU.mult), "dve")
                td = p.emit("vector", lambda e, ps=ps, cols=cols: e.tensor_tensor(
                    tmpr[3][...], ps[64:128, :], ropc[:, cols], ALU.mult), "dve")
                p.wait("vector", "dve", tb)
                p.emit("vector", lambda e, kp=kp: e.tensor_tensor(
                    rot[kp][0:64, :], tmpr[0][...], tmpr[1][...], ALU.subtract), "dve")
                p.wait("vector", "dve", td)
                tf = p.emit("vector", lambda e, kp=kp: e.tensor_tensor(
                    rot[kp][64:128, :], tmpr[2][...], tmpr[3][...], ALU.add), "dve")
                p.st[f"{tag}_ropeprev"] = tf
                tl_dve = tf
                p.wait("scalar", "dve", tf)
                tl_act = p.emit("scalar", lambda e, ob=ob, cols=cols, kp=kp: e.dma_start(
                    out=out_dram[ob * 128:(ob + 1) * 128, cols], in_=rot[kp][...]), f"d{8 + kp}", 16)
                p.st[f"{tag}_rst{kp}"] = tl_act
            return {"dve": tl_dve, "act": tl_act}
        return consume

    def clip_consume(tag, kt, out_dram, ps_t, do_clip):
        def consume(p, ob, par, t_mm):
            tl_act = None
            tl_dve = None
            for tt in range(NTT):
                kp = tt % 2
                cols = slice(tt * 512, (tt + 1) * 512)
                p.wait("vector", "pe", t_mm)
                p.wait("vector", f"d{8 + kp}", p.st.get(f"{tag}_cst{kp}", 0))
                if do_clip:
                    ta = p.emit("vector", lambda e, kp=kp, par=par, tt=tt: e.tensor_scalar(
                        kt[kp][...], ps_t[par][tt][...], CLIP, -CLIP, ALU.min, ALU.max), "dve")
                else:
                    ta = p.emit("vector", lambda e, kp=kp, par=par, tt=tt: e.tensor_copy(
                        kt[kp][...], ps_t[par][tt][...]), "dve")
                tl_dve = ta
                p.wait("scalar", "dve", ta)
                tl_act = p.emit("scalar", lambda e, ob=ob, cols=cols, kp=kp: e.dma_start(
                    out=out_dram[ob * 128:(ob + 1) * 128, cols], in_=kt[kp][...]), f"d{8 + kp}", 16)
                p.st[f"{tag}_cst{kp}"] = tl_act
            return {"dve": tl_dve, "act": tl_act}
        return consume

    def resid_consume(tag, rt, ot, resid_in, resid_out, ps_t):
        def consume(p, ob, par, t_mm):
            tl_act = None
            tl_dve = None
            for tt in range(NTT):
                kp = tt % 2
                cols = slice(tt * 512, (tt + 1) * 512)
                p.wait("sync", "dve", p.st.get(f"{tag}_rr{kp}", 0))
                t_r = p.emit("sync", lambda e, ob=ob, cols=cols, kp=kp: e.dma_start(
                    out=rt[kp][...], in_=resid_in[ob * 128:(ob + 1) * 128, cols]), f"d{kp}", 16)
                p.wait("vector", "pe", t_mm)
                p.wait("vector", f"d{kp}", t_r)
                p.wait("vector", f"d{8 + kp}", p.st.get(f"{tag}_ro{kp}", 0))
                ta = p.emit("vector", lambda e, kp=kp, par=par, tt=tt: e.tensor_tensor(
                    ot[kp][...], ps_t[par][tt][...], rt[kp][...], ALU.add), "dve")
                p.st[f"{tag}_rr{kp}"] = ta
                tl_dve = ta
                p.wait("scalar", "dve", ta)
                tl_act = p.emit("scalar", lambda e, ob=ob, cols=cols, kp=kp: e.dma_start(
                    out=resid_out[ob * 128:(ob + 1) * 128, cols], in_=ot[kp][...]), f"d{8 + kp}", 16)
                p.st[f"{tag}_ro{kp}"] = tl_act
            return {"dve": tl_dve, "act": tl_act}
        return consume

    def make_proj(tag, wsrc, ob0, nob, kind, **kw):
        def run():
            with ExitStack() as es:
                wbufs = [es.enter_context(nc.sbuf_tensor(f"wb{tag}{i}", [128, 4096], BF16)) for i in range(2)]
                ps_t = [[es.enter_context(nc.psum_tensor(f"ps{tag}{i}_{t}", [128, 512], F32)) for t in range(NTT)]
                        for i in range(2)]
                bias_i = kw.get("bias_i")
                bias_row = None
                if bias_i is not None:
                    bias_row = es.enter_context(nc.sbuf_tensor(f"br{tag}", [1, kw["bias_n"]], BF16))
                if kind == "rope":
                    ropc = es.enter_context(nc.sbuf_tensor(f"rc{tag}", [64, T], F32))
                    rops = es.enter_context(nc.sbuf_tensor(f"rs{tag}", [64, T], F32))
                    tmpr = [es.enter_context(nc.sbuf_tensor(f"tr{tag}{i}", [64, 512], F32)) for i in range(4)]
                    rot = [es.enter_context(nc.sbuf_tensor(f"ro{tag}{i}", [128, 512], BF16)) for i in range(2)]
                elif kind == "clip":
                    kt = [es.enter_context(nc.sbuf_tensor(f"kt{tag}{i}", [128, 512], BF16)) for i in range(2)]
                else:
                    rt = [es.enter_context(nc.sbuf_tensor(f"rt{tag}{i}", [128, 512], F32)) for i in range(2)]
                    ot = [es.enter_context(nc.sbuf_tensor(f"ot{tag}{i}", [128, 512], F32)) for i in range(2)]

                def fn(p):
                    if bias_i is not None:
                        t_b = p.emit("gpsimd", lambda e: e.dma_start(out=bias_row[...], in_=bias_i[...]), "d6", 16)
                        p.wait("tensor", "d6", t_b)
                    if kind == "rope":
                        p.emit("sync", lambda e: e.dma_start(out=ropc[...], in_=kw["cos_i"][...]), "d7", 16)
                        t_rl = p.emit("sync", lambda e: e.dma_start(out=rops[...], in_=kw["sin_i"][...]), "d7", 16)
                        consume = rope_consume(tag, ropc, rops, tmpr, rot, t_rl, kw["out_dram"], ps_t)
                    elif kind == "clip":
                        consume = clip_consume(tag, kt, kw["out_dram"], ps_t, kw.get("do_clip", True))
                    else:
                        consume = resid_consume(tag, rt, ot, kw["resid_in"], kw["resid_out"], ps_t)
                    ph_proj(p, wbufs, ps_t, wsrc, ob0, nob, bias_row, consume)

                pb.phase(fn, tag)
        return run

    def make_vtr(nheads, v_in, v_stage_out, ccs, tag):
        def run():
            with ExitStack() as es:
                vin = [es.enter_context(nc.sbuf_tensor(f"vi{tag}{i}", [128, T], BF16)) for i in range(2)]
                vot = [es.enter_context(nc.sbuf_tensor(f"vo{tag}{i}", [128, 128], BF16)) for i in range(2)]
                ps_tr = [es.enter_context(nc.psum_tensor(f"pt{tag}{i}", [128, 128], BF16)) for i in range(2)]

                def fn(p):
                    for h in range(nheads):
                        hp = h % 2
                        p.wait("sync", "pe", p.st.get(f"{tag}_pe{hp}", 0))
                        t_ld = p.emit("sync", lambda e, h=h, hp=hp: e.dma_start(
                            out=vin[hp][...], in_=v_in[h * 128:(h + 1) * 128, :]), f"d{hp}", 16)
                        for kt_ in range(16):
                            kp = kt_ % 2
                            p.wait("tensor", f"d{hp}", t_ld)
                            p.wait("tensor", "dve", p.st.get(f"{tag}_cp{kp}", 0))
                            t_tr = p.emit("tensor", lambda e, kt_=kt_, hp=hp, kp=kp: e.transpose(
                                ps_tr[kp][...], vin[hp][:, kt_ * 128:(kt_ + 1) * 128], eye_b[...]), "pe")
                            p.wait("vector", "pe", t_tr)
                            p.wait("vector", f"d{8 + kp}", p.st.get(f"{tag}_st{kp}", 0))
                            t_cp = p.emit("vector", lambda e, kp=kp: e.tensor_copy(vot[kp][...], ps_tr[kp][...]), "dve")
                            p.st[f"{tag}_cp{kp}"] = t_cp
                            p.wait("scalar", "dve", t_cp)
                            t_st = p.emit("scalar", lambda e, h=h, kt_=kt_, kp=kp: e.dma_start(
                                out=v_stage_out[(h * 16 + kt_) * 128:(h * 16 + kt_ + 1) * 128, :],
                                in_=vot[kp][...]), f"d{8 + kp}", 16)
                            p.st[f"{tag}_st{kp}"] = t_st
                        p.st[f"{tag}_pe{hp}"] = p.tag("pe")
                    for s_ in ("st", "d2", "d3", "d8", "d9"):
                        p.wait("gpsimd", s_, p.tag(s_))
                    for cin, cout in ccs:
                        p.emit("gpsimd", lambda e, cin=cin, cout=cout: e.collective_compute(
                            "AllGather", ALU.bypass, replica_groups=RG,
                            ins=[cin[...]], outs=[cout[...]]), "cc")

                pb.phase(fn, tag)
        return run

    def make_attn(variant):
        n_outer = NG if variant == "a" else NH
        heads_per = 4 if variant == "a" else 1
        k_out = cka_out if variant == "a" else ckb_out
        v_out = cva_out if variant == "a" else cvb_out
        krows = 1024 if variant == "a" else 4096
        mcol0 = 128 if variant == "a" else 160
        DEPTH = 6
        LAG = 3

        def run():
            with ExitStack() as es:
                K_sb = [es.enter_context(nc.sbuf_tensor(f"Kx{variant}{i}", [128, 4096], BF16)) for i in range(2)]
                V_sb = [es.enter_context(nc.sbuf_tensor(f"Vx{variant}{i}", [128, 4096], BF16)) for i in range(2)]
                Q_sb = [es.enter_context(nc.sbuf_tensor(f"Qx{variant}{i}", [128, T], BF16)) for i in range(2)]
                aqb = [es.enter_context(nc.sbuf_tensor(f"aqb{variant}{i}", [1, T], BF16)) for i in range(2)]
                probs = [es.enter_context(nc.sbuf_tensor(f"pr{variant}{i}", [128, 512], BF16)) for i in range(DEPTH)]
                biasH = es.enter_context(nc.sbuf_tensor(f"bH{variant}", [128, 1], F32))
                bias32 = es.enter_context(nc.sbuf_tensor(f"b32{variant}", [128, 32], F32))
                rl = es.enter_context(nc.sbuf_tensor(f"rl{variant}", [1, 512], F32))
                rbx = es.enter_context(nc.sbuf_tensor(f"rb{variant}", [128, 512], F32))
                ps_r = [es.enter_context(nc.psum_tensor(f"psr{variant}{i}", [128, 512], F32)) for i in range(DEPTH)]
                ps_ctx = es.enter_context(nc.psum_tensor(f"px{variant}", [128, 512], F32))
                ps_l = es.enter_context(nc.psum_tensor(f"pl{variant}", [1, 512], F32))

                def emit_kv_load(p, go):
                    par = go % 2
                    p.wait("sync", "pe", p.st.get(f"kv_pe{par}", 0))
                    p.wait("sync", "act", p.st.get(f"kv_act{par}", 0))
                    tags = []
                    for r in range(2):
                        tags.append(("lx" if r == 0 else "d0", p.emit(
                            "sync", lambda e, go=go, r=r, par=par: e.dma_start(
                                out=K_sb[par][:, r * 2048:(r + 1) * 2048],
                                in_=k_out[r * krows + go * 128: r * krows + go * 128 + 128, :]),
                            "lx" if r == 0 else "d0", 16)))
                        tags.append(("d1" if r == 0 else "d3", p.emit(
                            "sync", lambda e, go=go, r=r, par=par: e.dma_start(
                                out=V_sb[par].ap().rearrange("p (kt c) -> p kt c", c=128)[:, r * 16:(r + 1) * 16, :],
                                in_=v_out[(r * n_outer + go) * 2048:(r * n_outer + go + 1) * 2048, :]
                                .rearrange("(kt p) c -> p kt c", p=128)),
                            "d1" if r == 0 else "d3", 16)))
                    p.st[f"kvtags{par}"] = tags

                def emit_q_load(p, h):
                    hp = h % 2
                    p.wait("sync", "pe", p.st.get(f"q_pe{hp}", 0))
                    p.wait("sync", "act", p.st.get(f"q_act{hp}", 0))
                    t_q = p.emit("sync", lambda e, h=h, hp=hp: e.dma_start(
                        out=Q_sb[hp][...], in_=q_buf[h * 128:(h + 1) * 128, :]), "d7", 16)
                    p.st[f"qtag{hp}"] = t_q
                    if variant == "b":
                        p.wait("gpsimd", "pe", p.st.get(f"q_pe{hp}", 0))
                        t_aq = p.emit("gpsimd", lambda e, h=h, hp=hp: e.dma_start(
                            out=aqb[hp][...], in_=alibi_q_i[h:h + 1, :]), "d6", 16)
                        p.st[f"aqtag{hp}"] = t_aq

                def fn(p):
                    p.wait("sync", "cc", p.tag("cc"))
                    emit_kv_load(p, 0)
                    emit_q_load(p, 0)
                    for go in range(n_outer):
                        par = go % 2
                        if go + 1 < n_outer:
                            emit_kv_load(p, go + 1)
                        kv_tags = p.st[f"kvtags{par}"]
                        for rh in range(heads_per):
                            h = go * heads_per + rh
                            hp = h % 2
                            if h + 1 < n_outer * heads_per:
                                emit_q_load(p, h + 1)
                            for s_, t_ in kv_tags:
                                p.wait("tensor", s_, t_)
                            p.wait("tensor", "d7", p.st[f"qtag{hp}"])
                            if variant == "b":
                                p.wait("tensor", "d6", p.st[f"aqtag{hp}"])
                            p.wait("tensor", "dve", p.st.get(f"negM_{variant}", 0))
                            p.wait("tensor", "dve", p.st.get("ctx_free", 0))
                            tq5 = p.emit("tensor", lambda e, h=h: e.matmul(
                                ps_ctx[:, 0:1], ones_f[0:1, :],
                                nrm[0:1, mcol0 + h:mcol0 + h + 1], start=True, stop=True), "pe")
                            p.wait("scalar", "pe", tq5)
                            p.wait("scalar", "act", p.st.get("bias_used", 0))
                            t_bH = p.emit("scalar", lambda e: e.activation(biasH[...], ps_ctx[:, 0:1], AF.Copy), "act")
                            if variant == "b":
                                p.wait("vector", "act", t_bH)
                                p.wait("vector", "act", p.st.get("b32_used", 0))
                                t_b32 = p.emit("vector", lambda e, h=h: e.tensor_scalar(
                                    bias32[...], alibi_k_sb[:, h * 32:(h + 1) * 32],
                                    biasH[:, 0:1], None, ALU.add), "dve")
                                bias_ready = ("dve", t_b32)
                            else:
                                bias_ready = ("act", t_bH)

                            for own_i in range(NBLK):
                                qcols = slice(own_i * 512, (own_i + 1) * 512)
                                visits = []
                                for j in range(own_i + 1):
                                    for r_k in range(2):
                                        for dd in range(4):
                                            visits.append((j, r_k, dd))
                                nv = len(visits)
                                t_sc = [0] * nv
                                t_exp = [0] * nv
                                t_msk = [0] * nv
                                t_pv = [0] * nv

                                def emit_scores(p, v):
                                    j, r_k, dd = visits[v]
                                    sl_ = v % DEPTH
                                    kcol = r_k * 2048 + (j * 4 + dd) * 128
                                    if v >= DEPTH:
                                        p.wait("tensor", "act", t_exp[v - DEPTH])
                                        if t_msk[v - DEPTH]:
                                            p.wait("tensor", "dve", t_msk[v - DEPTH])
                                    else:
                                        p.wait("tensor", "act", p.st.get(f"ringfree{sl_}", 0))
                                        p.wait("tensor", "dve", p.st.get(f"ringfree_d{sl_}", 0))
                                    if variant == "b":
                                        p.emit("tensor", lambda e, sl_=sl_, kcol=kcol, qcols=qcols, par=par, hp=hp: e.matmul(
                                            ps_r[sl_][...], K_sb[par][:, kcol:kcol + 128], Q_sb[hp][:, qcols],
                                            start=True, stop=False), None)
                                        t_sc[v] = p.emit("tensor", lambda e, sl_=sl_, qcols=qcols, hp=hp: e.matmul(
                                            ps_r[sl_][...], ones_b[0:1, 0:128], aqb[hp][0:1, qcols],
                                            start=False, stop=True), "pe")
                                    else:
                                        t_sc[v] = p.emit("tensor", lambda e, sl_=sl_, kcol=kcol, qcols=qcols, par=par, hp=hp: e.matmul(
                                            ps_r[sl_][...], K_sb[par][:, kcol:kcol + 128], Q_sb[hp][:, qcols],
                                            start=True, stop=True), "pe")
                                    if j == own_i:
                                        p.wait("vector", "pe", t_sc[v])
                                        p.wait("vector", "act", p.st.get("rb_used", 0))
                                        t_msk[v] = p.emit("vector", lambda e, sl_=sl_, r_k=r_k, dd=dd: e.tensor_tensor(
                                            ps_r[sl_][...], ps_r[sl_][...],
                                            msk_sb[:, (r_k * 4 + dd) * 512:(r_k * 4 + dd + 1) * 512],
                                            ALU.add), "dve")
                                    p.wait("scalar", "pe", t_sc[v])
                                    if t_msk[v]:
                                        p.wait("scalar", "dve", t_msk[v])
                                    if v >= DEPTH:
                                        p.wait("scalar", "pe", t_pv[v - DEPTH] if t_pv[v - DEPTH] else 0)
                                    p.wait("scalar", bias_ready[0], bias_ready[1])
                                    if variant == "b":
                                        ki = (j * 2 + r_k) * 4 + dd
                                        t_exp[v] = p.emit("scalar", lambda e, sl_=sl_, ki=ki: e.activation(
                                            probs[sl_][...], ps_r[sl_][...], AF.Exp,
                                            bias=bias32[:, ki:ki + 1]), "act")
                                    else:
                                        t_exp[v] = p.emit("scalar", lambda e, sl_=sl_: e.activation(
                                            probs[sl_][...], ps_r[sl_][...], AF.Exp,
                                            bias=biasH[:, 0:1]), "act")

                                def emit_pv(p, v):
                                    j, r_k, dd = visits[v]
                                    sl_ = v % DEPTH
                                    kidx = r_k * 16 + j * 4 + dd
                                    p.wait("tensor", "act", t_exp[v])
                                    if v == 0:
                                        p.wait("tensor", "dve", p.st.get("ctx_free", 0))
                                        p.wait("tensor", "dve", p.st.get("ps_l_free", 0))
                                    p.emit("tensor", lambda e, sl_=sl_, kidx=kidx, v=v, nv=nv, par=par: e.matmul(
                                        ps_ctx[...], V_sb[par][:, kidx * 128:(kidx + 1) * 128], probs[sl_][...],
                                        start=(v == 0), stop=(v == nv - 1)), None)
                                    t_pv[v] = p.emit("tensor", lambda e, sl_=sl_, v=v, nv=nv: e.matmul(
                                        ps_l[0:1, :], ones_col_b[...], probs[sl_][...],
                                        start=(v == 0), stop=(v == nv - 1)), "pe")

                                for v in range(nv + LAG):
                                    if v < nv:
                                        emit_scores(p, v)
                                    if v - LAG >= 0:
                                        emit_pv(p, v - LAG)
                                p.wait("vector", "pe", t_pv[nv - 1])
                                tr1 = p.emit("vector", lambda e: e.reciprocal(rl[...], ps_l[0:1, :]), "dve")
                                p.st["ps_l_free"] = tr1
                                rslot = nv % DEPTH
                                p.wait("tensor", "dve", tr1)
                                p.wait("tensor", "act", t_exp[nv - 1])
                                p.wait("tensor", "act", t_exp[nv - DEPTH] if nv >= DEPTH else p.st.get(f"ringfree{rslot}", 0))
                                t_rb = p.emit("tensor", lambda e, rslot=rslot: e.matmul(
                                    ps_r[rslot][...], ones_f[0:1, :], rl[0:1, :], start=True, stop=True), "pe")
                                p.wait("scalar", "pe", t_rb)
                                tr3 = p.emit("scalar", lambda e, rslot=rslot: e.activation(
                                    rbx[...], ps_r[rslot][...], AF.Copy), "act")
                                p.st[f"ringfree{rslot}"] = tr3
                                p.st["rb_used"] = tr3
                                p.wait("vector", "act", tr3)
                                p.wait("vector", "pe", t_pv[nv - 1])
                                tr4 = p.emit("vector", lambda e, h=h, qcols=qcols: e.tensor_tensor(
                                    big_sb[:, h, qcols], ps_ctx[...], rbx[...], ALU.mult), "dve")
                                p.st["ctx_free"] = tr4
                                for sl_ in range(DEPTH):
                                    p.st[f"ringfree{sl_}"] = p.tag("act")
                                    p.st[f"ringfree_d{sl_}"] = p.tag("dve")
                            p.st["b32_used"] = p.tag("act")
                            p.st["bias_used"] = p.tag("act")
                            p.st[f"q_pe{hp}"] = p.tag("pe")
                            p.st[f"q_act{hp}"] = p.tag("act")
                        p.st[f"kv_pe{par}"] = p.tag("pe")
                        p.st[f"kv_act{par}"] = p.tag("act")

                pb.phase(fn, f"attn_{variant}")
        return run

    def make_ffn_a(tag):
        def run():
            with ExitStack() as es:
                la_sb = es.enter_context(nc.sbuf_tensor(f"la{tag}", [128, NC, 128], BF16))
                rt_sb = es.enter_context(nc.sbuf_tensor(f"rt{tag}", [128, NC, 16], BF16))
                z_sb = es.enter_context(nc.sbuf_tensor(f"zs{tag}", [128, 512], BF16))
                lg_b = es.enter_context(nc.sbuf_tensor(f"lg{tag}", [16, 512], BF16))
                lg_tm = es.enter_context(nc.sbuf_tensor(f"lt{tag}", [128, 64], F32))
                mx4 = es.enter_context(nc.sbuf_tensor(f"mx{tag}", [128, 4], F32))
                et = es.enter_context(nc.sbuf_tensor(f"et{tag}", [128, 64], F32))
                sm4 = es.enter_context(nc.sbuf_tensor(f"sm{tag}", [128, 4], F32))
                al_b = es.enter_context(nc.sbuf_tensor(f"ab{tag}", [128, 64], BF16))
                al_em = es.enter_context(nc.sbuf_tensor(f"ae{tag}", [16, 512], BF16))
                ps_z = es.enter_context(nc.psum_tensor(f"pz{tag}", [128, 512], F32))
                ps_lg = es.enter_context(nc.psum_tensor(f"pg{tag}", [16, 512], F32))
                ps_tp = es.enter_context(nc.psum_tensor(f"pp{tag}", [128, 64], BF16))
                ps_tb = es.enter_context(nc.psum_tensor(f"pb{tag}", [16, 512], BF16))
                ps_ax = es.enter_context(nc.psum_tensor(f"pa{tag}", [128, 512], F32))

                def fn(p):
                    p.emit("gpsimd", lambda e: e.dma_start(
                        out=la_sb.ap().rearrange("p c e -> p (c e)"),
                        in_=lora_a_t.rearrange("p c e -> p (c e)")), "lw", 16)
                    t_la = p.emit("gpsimd", lambda e: e.dma_start(
                        out=rt_sb.ap().rearrange("p c e -> p (c e)"),
                        in_=router_t.rearrange("p c e -> p (c e)")), "lw", 16)
                    for tt in range(NTT):
                        cols = slice(tt * 512, (tt + 1) * 512)
                        p.wait("tensor", "lw", t_la)
                        p.wait("tensor", "act", p.st.get("zev", 0))
                        for c in range(NC):
                            p.emit("tensor", lambda e, c=c, cols=cols: e.matmul(
                                ps_z[...], la_sb[:, c, :], big_sb[:, c, cols],
                                start=(c == 0), stop=(c == NC - 1)),
                                "pe" if c == NC - 1 else None)
                        t_z = p.tag("pe")
                        p.wait("tensor", "act", p.st.get("lgev", 0))
                        for c in range(NC):
                            p.emit("tensor", lambda e, c=c, cols=cols: e.matmul(
                                ps_lg[...], rt_sb[:, c, :], big_sb[:, c, cols],
                                start=(c == 0), stop=False), None)
                        p.wait("tensor", "act", p.st["setup_done"])
                        t_lg = p.emit("tensor", lambda e: e.matmul(
                            ps_lg[...], c16_b[...], ones_b[0:1, 0:512], start=False, stop=True), "pe")
                        p.wait("scalar", "pe", t_z)
                        p.wait("scalar", "dve", p.st.get("zpd", 0))
                        t_zev = p.emit("scalar", lambda e: e.activation(z_sb[...], ps_z[...], AF.Copy), "act")
                        p.st["zev"] = t_zev
                        p.wait("scalar", "pe", t_lg)
                        t_lgev = p.emit("scalar", lambda e: e.activation(lg_b[...], ps_lg[...], AF.Copy), "act")
                        p.st["lgev"] = t_lgev
                        p.wait("tensor", "act", t_lgev)
                        p.wait("tensor", "dve", p.st.get("tpc", 0))
                        t_tp = 0
                        for s4 in range(4):
                            t_tp = p.emit("tensor", lambda e, s4=s4: e.transpose(
                                ps_tp[:, s4 * 16:(s4 + 1) * 16], lg_b[:, s4 * 128:(s4 + 1) * 128],
                                eye_b[0:16, 0:16]), "pe")
                        p.wait("vector", "pe", t_tp)
                        t_c = p.emit("vector", lambda e: e.tensor_copy(lg_tm[...], ps_tp[...]), "dve")
                        p.st["tpc"] = t_c
                        p.wait("vector", "dve", t_c)
                        t_mx = p.emit("vector", lambda e: e.tensor_reduce(
                            mx4[...], lg_tm.ap().rearrange("p (s e) -> p s e", e=16),
                            AX.X, ALU.max, negate=True), "dve")
                        t_ex = [0] * 4
                        for s4 in range(4):
                            p.wait("scalar", "dve", t_mx)
                            t_ex[s4] = p.emit("scalar", lambda e, s4=s4: e.activation(
                                et[:, s4 * 16:(s4 + 1) * 16], lg_tm[:, s4 * 16:(s4 + 1) * 16],
                                AF.Exp, bias=mx4[:, s4:s4 + 1]), "act")
                        t_ct = [0] * 4
                        for s4 in range(4):
                            p.wait("vector", "act", t_ex[s4])
                            t_ct[s4] = p.emit("vector", lambda e, s4=s4: e.tensor_tensor(
                                et[:, s4 * 16:(s4 + 1) * 16], et[:, s4 * 16:(s4 + 1) * 16],
                                cnt_bc[...], ALU.mult), "dve")
                        p.wait("vector", "dve", t_ct[3])
                        t_sm = p.emit("vector", lambda e: e.tensor_reduce(
                            sm4[...], et.ap().rearrange("p (s e) -> p s e", e=16), AX.X, ALU.add), "dve")
                        p.wait("vector", "dve", t_sm)
                        t_rc = p.emit("vector", lambda e: e.reciprocal(sm4[...], sm4[...]), "dve")
                        t_al = [0] * 4
                        for s4 in range(4):
                            p.wait("vector", "dve", t_rc)
                            t_al[s4] = p.emit("vector", lambda e, s4=s4: e.tensor_scalar(
                                al_b[:, s4 * 16:(s4 + 1) * 16], et[:, s4 * 16:(s4 + 1) * 16],
                                sm4[:, s4:s4 + 1], 2.0, ALU.mult, ALU.mult), "dve")
                        p.wait("tensor", "dve", t_al[3])
                        p.wait("tensor", "dve", p.st.get("emc", 0))
                        t_tb = 0
                        for s4 in range(4):
                            t_tb = p.emit("tensor", lambda e, s4=s4: e.transpose(
                                ps_tb[:, s4 * 128:(s4 + 1) * 128], al_b[:, s4 * 16:(s4 + 1) * 16],
                                eye_b[...]), "pe")
                        p.wait("vector", "pe", t_tb)
                        t_em = p.emit("vector", lambda e: e.tensor_copy(al_em[...], ps_tb[...]), "dve")
                        p.st["emc"] = t_em
                        p.wait("tensor", "dve", t_em)
                        p.wait("tensor", "dve", p.st.get("zpd", 0))
                        t_ax = p.emit("tensor", lambda e: e.matmul(
                            ps_ax[...], emat_b[...], al_em[...], start=True, stop=True), "pe")
                        p.wait("vector", "pe", t_ax)
                        p.wait("vector", "act", t_zev)
                        t_zp = p.emit("vector", lambda e, cols=cols: e.tensor_tensor(
                            zp_sb[:, cols], ps_ax[...], z_sb[...], ALU.mult), "dve")
                        p.st["zpd"] = t_zp
                    p.st["zp_done"] = p.st["zpd"]

                pb.phase(fn, tag)
        return run

    def make_ffn_b(resid_in, resid_out, tag):
        def run():
            with ExitStack() as es:
                lb_sb = es.enter_context(nc.sbuf_tensor(f"lb{tag}", [128, 4096], BF16))
                bias_row = es.enter_context(nc.sbuf_tensor(f"bb{tag}", [1, 4096], BF16))
                wbufs = [es.enter_context(nc.sbuf_tensor(f"wb{tag}{i}", [128, 4096], BF16)) for i in range(2)]
                ps_t = [[es.enter_context(nc.psum_tensor(f"ps{tag}{i}_{t}", [128, 512], F32)) for t in range(NTT)]
                        for i in range(2)]
                rt = [es.enter_context(nc.sbuf_tensor(f"rt{tag}{i}", [128, 512], F32)) for i in range(2)]
                ot = [es.enter_context(nc.sbuf_tensor(f"ot{tag}{i}", [128, 512], F32)) for i in range(2)]

                def fn(p):
                    p.emit("gpsimd", lambda e: e.dma_start(out=lb_sb[...], in_=lora_b_f[...]), "d6", 16)
                    t_lb = p.emit("gpsimd", lambda e: e.dma_start(out=bias_row[...], in_=bbase_i[...]), "d6", 16)
                    consume = resid_consume(tag, rt, ot, resid_in, resid_out, ps_t)
                    t_mm = [0] * 32
                    t_cons = [None] * 32
                    for ob in range(32):
                        par = ob % 2
                        if ob >= 2:
                            p.wait("gpsimd", "pe", t_mm[ob - 2])
                        t_w = p.emit("gpsimd", lambda e, ob=ob, par=par: e.dma_start(
                            out=wbufs[par][...], in_=wbase_t[ob]), f"d{4 + par}", 16)
                        p.wait("tensor", f"d{4 + par}", t_w)
                        p.wait("tensor", "d6", t_lb)
                        p.wait("tensor", "dve", p.st["zp_done"])
                        if ob >= 2:
                            c2 = t_cons[ob - 2]
                            p.wait("tensor", "dve", c2.get("dve", 0))
                        for c in range(NC):
                            for tt in range(NTT):
                                p.emit("tensor", lambda e, c=c, tt=tt, par=par: e.matmul(
                                    ps_t[par][tt][...], wbufs[par][:, c * 128:(c + 1) * 128],
                                    big_sb[:, c, tt * 512:(tt + 1) * 512],
                                    start=(c == 0), stop=False), None)
                        for tt in range(NTT):
                            p.emit("tensor", lambda e, tt=tt, par=par, ob=ob: e.matmul(
                                ps_t[par][tt][...], lb_sb[:, ob * 128:(ob + 1) * 128],
                                zp_sb[:, tt * 512:(tt + 1) * 512], start=False, stop=False), None)
                            t_mm[ob] = p.emit("tensor", lambda e, tt=tt, par=par, ob=ob: e.matmul(
                                ps_t[par][tt][...], bias_row[0:1, ob * 128:(ob + 1) * 128],
                                ones_b[0:1, 0:512], start=False, stop=True),
                                "pe" if tt == NTT - 1 else None) or t_mm[ob]
                        t_cons[ob] = consume(p, ob, par, t_mm[ob])

                pb.phase(fn, tag)
        return run

    def make_copy(src_fn, dst, tag, rows=D, dt=F32, width=T):
        def run():
            with ExitStack() as es:
                buf = [es.enter_context(nc.sbuf_tensor(f"cp{tag}{i}", [128, width], dt)) for i in range(2)]
                src = src_fn() if callable(src_fn) else src_fn

                def fn(p):
                    for c in range(rows // 128):
                        cp = c % 2
                        p.wait("sync", f"d{8 + cp}", p.st.get(f"{tag}_s{cp}", 0))
                        t_l = p.emit("sync", lambda e, c=c, cp=cp: e.dma_start(
                            out=buf[cp][...], in_=src[c * 128:(c + 1) * 128, :]), f"d{cp}", 16)
                        p.wait("scalar", f"d{cp}", t_l)
                        t_s = p.emit("scalar", lambda e, c=c, cp=cp: e.dma_start(
                            out=dst[c * 128:(c + 1) * 128, :], in_=buf[cp][...]), f"d{8 + cp}", 16)
                        p.st[f"{tag}_s{cp}"] = t_s

                pb.phase(fn, tag)
        return run

    def make_mprep(variant):
        qsrc = q_buf
        ksrc = cka_in if variant == "a" else ckb_in
        nk = NG if variant == "a" else NH
        qcol0 = 0 if variant == "a" else 64
        kcol0 = 32 if variant == "a" else 96
        mcol0 = 128 if variant == "a" else 160

        def run():
            with ExitStack() as es:
                til = [es.enter_context(nc.sbuf_tensor(f"mp{variant}{i}", [128, 512], BF16)) for i in range(2)]
                sqt = [es.enter_context(nc.sbuf_tensor(f"mq{variant}{i}", [128, 512], BF16)) for i in range(2)]
                nrb = [es.enter_context(nc.sbuf_tensor(f"mn{variant}{i}", [1, 512], F32)) for i in range(2)]
                ps2 = [es.enter_context(nc.psum_tensor(f"mps{variant}{i}", [1, 512], F32)) for i in range(2)]

                def one_tensor(p, src, nh, col0):
                    for h in range(nh):
                        t_hmax = 0
                        for tt in range(NTT):
                            bp = (h * NTT + tt) % 2
                            p.wait("sync", "act", p.st.get(f"mp_sq{bp}", 0))
                            t_ld = p.emit("sync", lambda e, h=h, tt=tt, bp=bp, src=src: e.dma_start(
                                out=til[bp][...], in_=src[h * 128:(h + 1) * 128, tt * 512:(tt + 1) * 512]),
                                f"d{bp}", 16)
                            p.wait("scalar", f"d{bp}", t_ld)
                            p.wait("scalar", "pe", p.st.get(f"mp_mm{bp}", 0))
                            t_sq = p.emit("scalar", lambda e, bp=bp: e.activation(
                                sqt[bp][...], til[bp][...], AF.Square), "act")
                            p.st[f"mp_sq{bp}"] = t_sq
                            p.wait("tensor", "act", t_sq)
                            p.wait("tensor", "act", p.st.get(f"mp_ev{bp}", 0))
                            t_mm = p.emit("tensor", lambda e, bp=bp: e.matmul(
                                ps2[bp][0:1, :], ones_col_b[...], sqt[bp][...],
                                start=True, stop=True), "pe")
                            p.st[f"mp_mm{bp}"] = t_mm
                            p.wait("scalar", "pe", t_mm)
                            p.wait("scalar", "dve", p.st.get(f"mp_red{bp}", 0))
                            t_ev = p.emit("scalar", lambda e, bp=bp: e.activation(
                                nrb[bp][...], ps2[bp][0:1, :], AF.Copy), "act")
                            p.st[f"mp_ev{bp}"] = t_ev
                            p.wait("vector", "act", t_ev)
                            if tt == 0:
                                t_red = p.emit("vector", lambda e, h=h, col0=col0, bp=bp: e.tensor_reduce(
                                    nrm[0:1, col0 + h:col0 + h + 1], nrb[bp][...], AX.X, ALU.max), "dve")
                            else:
                                p.wait("vector", "dve", t_hmax)
                                t_red = p.emit("vector", lambda e, bp=bp: e.tensor_reduce(
                                    nrm[0:1, 192:193], nrb[bp][...], AX.X, ALU.max), "dve")
                                p.wait("vector", "dve", t_red)
                                t_red = p.emit("vector", lambda e, h=h, col0=col0: e.tensor_tensor(
                                    nrm[0:1, col0 + h:col0 + h + 1], nrm[0:1, col0 + h:col0 + h + 1],
                                    nrm[0:1, 192:193], ALU.max), "dve")
                            t_hmax = t_red
                            p.st[f"mp_red{bp}"] = t_red

                def fn(p):
                    one_tensor(p, qsrc, NH, qcol0)
                    one_tensor(p, ksrc, nk, kcol0)
                    a1 = nrm[0:1, 200:232]
                    a2 = nrm[0:1, 232:264] if False else nrm[0:1, 192 + 8:232] if False else None
                    t0 = p.st[f"mp_red0"]
                    t1 = p.st[f"mp_red1"]
                    p.wait("scalar", "dve", max(t0, t1))
                    tq = p.emit("scalar", lambda e: e.activation(a1, nrm[0:1, qcol0:qcol0 + 32], AF.Ln), "act")
                    if variant == "a":
                        tk = p.emit("scalar", lambda e: e.activation(
                            nrm[0:1, 192:200], nrm[0:1, kcol0:kcol0 + 8], AF.Ln), "act")
                        tcp = 0
                        for rr in range(4):
                            p.wait("vector", "act", tk)
                            tcp = p.emit("vector", lambda e, rr=rr: e.tensor_copy(
                                nrm.ap()[0:1, 232:264].rearrange("u (g r) -> u g r", r=4)[:, :, rr],
                                nrm[0:1, 192:200]), "dve")
                        p.wait("vector", "act", tq)
                        p.wait("vector", "dve", tcp)
                        t_add = p.emit("vector", lambda e: e.tensor_tensor(
                            a1, a1, nrm[0:1, 232:264], ALU.add), "dve")
                    else:
                        tk = p.emit("scalar", lambda e: e.activation(
                            nrm[0:1, 232:264], nrm[0:1, kcol0:kcol0 + 32], AF.Ln), "act")
                        p.wait("vector", "act", tk)
                        p.wait("vector", "act", tq)
                        t_add = p.emit("vector", lambda e: e.tensor_tensor(
                            a1, a1, nrm[0:1, 232:264], ALU.add), "dve")
                    p.wait("scalar", "dve", t_add)
                    t_exp = p.emit("scalar", lambda e: e.activation(a1, a1, AF.Exp, scale=0.5), "act")
                    p.wait("vector", "act", t_exp)
                    t_m = p.emit("vector", lambda e, mcol0=mcol0: e.tensor_scalar(
                        nrm[0:1, mcol0:mcol0 + 32], a1, -1.01, None, ALU.mult), "dve")
                    p.st[f"negM_{variant}"] = t_m

                pb.phase(fn, f"mprep_{variant}")
        return run

    phases = [
        ("setup", None),
        ("ln1", make_ln("xT", xT, "ln1", out_sb=big_sb)),
        ("ka", make_proj("ka", wk_t, 0, NG, "rope", cos_i=ropek_cos_i, sin_i=ropek_sin_i,
                         out_dram=cka_in, bias_i=bk_i, bias_n=1024)),
        ("va", make_proj("va", wv_t, 0, NG, "clip", out_dram=vstage, do_clip=False,
                         bias_i=bv_i, bias_n=1024)),
        ("vta", make_vtr(NG, vstage, cva_in, [(cka_in, cka_out), (cva_in, cva_out)], "vta")),
        ("qa", make_proj("qa", wq_t, 0, NH, "rope", cos_i=ropeq_cos_i, sin_i=ropeq_sin_i,
                         out_dram=q_buf, bias_i=bq_i, bias_n=4096)),
        ("mpa", make_mprep("a")),
        ("attn_a", make_attn("a")),
        ("out_a", make_proj("out_a", wo_t, 0, 32, "resid", resid_in=xT, resid_out=r1,
                            bias_i=bo_i, bias_n=4096)),
        ("ln2", make_ln("r1", r1, "ln2", out_sb=big_sb)),
        ("ffn1a", make_ffn_a("f1")),
        ("ffn1b", make_ffn_b(r1, r2, "f1b")),
        ("ln3", make_ln("r2", r2, "ln3", out_sb=big_sb)),
        ("kb", make_proj("kb", wqkv_t, 32, 32, "clip", out_dram=kstage)),
        ("klnb", make_ln("kstage", kstage, "klnb", src_f32=False,
                         affine=(klnw_c, klnb_c), out_dram=ckb_in)),
        ("vb", make_proj("vb", wqkv_t, 64, 32, "clip", out_dram=vstage)),
        ("vtb", make_vtr(NH, vstage, cvb_in, [(ckb_in, ckb_out), (cvb_in, cvb_out)], "vtb")),
        ("qb", make_proj("qb", wqkv_t, 0, 32, "clip", out_dram=qstage)),
        ("qlnb", make_ln("qstage", qstage, "qlnb", src_f32=False,
                         affine=(qlnw_c, qlnb_c), out_dram=q_buf)),
        ("mpb", make_mprep("b")),
        ("attn_b", make_attn("b")),
        ("out_b", make_proj("out_b", wout_t, 0, 32, "resid", resid_in=r2, resid_out=r3)),
        ("ln4", make_ln("r3", r3, "ln4", out_sb=big_sb)),
        ("ffn2a", make_ffn_a("f2")),
        ("ffn2b", make_ffn_b(r3, yT, "f2b")),
    ]

    resid_after = [xT, xT, xT, xT, xT, xT, xT, xT, xT, r1, r1, r1, r2, r2, r2,
                   r2, r2, r2, r2, r2, r2, r3, r3, r3, r3]

    nrun = min(upto, len(phases))
    for i in range(nrun):
        name, fn = phases[i]
        if name == "setup":
            with nc.psum_tensor("ps_set", [128, 16], F32) as ps:
                pb.phase(ph_setup_fn(ps), "setup")
        else:
            fn()
    if nrun < len(phases):
        make_copy(resid_after[nrun], yT, "fincp")()
    if dump is not None:
        if dump[0] == "big_sb":
            def dump_big(p):
                for c in range(NC):
                    t_s = p.emit("scalar", lambda e, c=c: e.dma_start(
                        out=dump_out[c * 128:(c + 1) * 128, :], in_=big_sb[:, c, :]), "st", 16)
                p.wait("scalar", "st", p.tag("st"))
            pb.phase(dump_big, "dump_big")
        elif dump[0] == "zp_sb":
            def dump_zp(p):
                t_s = p.emit("scalar", lambda e: e.dma_start(out=dump_out[...], in_=zp_sb[...]), "st", 16)
                p.wait("scalar", "st", p.tag("st"))
            pb.phase(dump_zp, "dump_zp")
        else:
            src = dram_map[dump[0]]
            make_copy(src, dump_out, "dmp", rows=dump[1][0], dt=dump[2], width=dump[1][1])()

    def fin(p):
        for s_ in ("st", "lx", "lw", "cc") + tuple(f"d{i}" for i in range(10)):
            p.wait("sync", s_, p.tag(s_))
    pb.phase(fin, "fin_wait")
    return nc



def _tile_w(w_in_out):
    i_, o_ = w_in_out.shape
    t = w_in_out.reshape(i_ // 128, 128, o_ // 128, 128)
    t = t.transpose(2, 1, 0, 3).reshape(o_ // 128, 128, (i_ // 128) * 128)
    return np.ascontiguousarray(t)


def prepare_in_maps(inputs):
    x = np.asarray(inputs["L0_attn_a_hidden_states"])
    f32 = np.float32

    wq_t = _tile_w(np.asarray(inputs["L0_attn_a_Wq"]))
    wk_t = _tile_w(np.asarray(inputs["L0_attn_a_Wk"]))
    wv_t = _tile_w(np.asarray(inputs["L0_attn_a_Wv"]))
    wo_t = _tile_w(np.asarray(inputs["L0_attn_a_Wo"]))
    wqkv_t = _tile_w(np.ascontiguousarray(np.asarray(inputs["L0_attn_b_Wqkv"]).T))
    wout_t = _tile_w(np.ascontiguousarray(np.asarray(inputs["L0_attn_b_Wout"]).T))
    wbase_t = _tile_w(np.ascontiguousarray(np.asarray(inputs["L0_ffn_W_base"]).T))
    lora_a = np.asarray(inputs["L0_ffn_lora_A"])
    lora_a_t = np.ascontiguousarray(
        lora_a.reshape(128, 4096).T.reshape(32, 128, 128).transpose(1, 0, 2))
    lora_b = np.asarray(inputs["L0_ffn_lora_B"])
    lora_b_f = np.ascontiguousarray(lora_b.transpose(0, 2, 1).reshape(128, 4096))
    router_w = np.asarray(inputs["L0_ffn_router_W"])
    router_t = np.ascontiguousarray(
        router_w.T.reshape(32, 128, 16).transpose(1, 0, 2))

    def row(v, n):
        return np.ascontiguousarray(np.asarray(v, dtype=f32).reshape(1, n))

    def col32(v):
        return np.ascontiguousarray(np.asarray(v, dtype=f32).reshape(32, 128).T)

    shared = dict(
        wq_t=wq_t, wk_t=wk_t, wv_t=wv_t, wo_t=wo_t, wqkv_t=wqkv_t,
        wout_t=wout_t, wbase_t=wbase_t, lora_a_t=lora_a_t, lora_b_f=lora_b_f,
        router_t=router_t,
        bq=row(inputs["L0_attn_a_bq"], 4096), bk=row(inputs["L0_attn_a_bk"], 1024),
        bv=row(inputs["L0_attn_a_bv"], 1024), bo=row(inputs["L0_attn_a_bo"], 4096),
        b_base=row(inputs["L0_ffn_b_base"], 4096),
        qlnw_col=col32(inputs["L0_attn_b_q_ln_w"]), qlnb_col=col32(inputs["L0_attn_b_q_ln_b"]),
        klnw_col=col32(inputs["L0_attn_b_k_ln_w"]), klnb_col=col32(inputs["L0_attn_b_k_ln_b"]),
        router_b=row(inputs["L0_ffn_router_b"], 16),
        prior=row(inputs["L0_ffn_domain_prior"], 16),
        active=np.ascontiguousarray(np.asarray(inputs["L0_ffn_active"], dtype=np.int32).reshape(1, 4)),
        iota16=np.arange(16, dtype=f32).reshape(1, 16),
        eye128=np.eye(128, dtype=f32),
    )
    emat = np.zeros((16, 128), dtype=f32)
    for e_ in range(16):
        emat[e_, e_ * 8:(e_ + 1) * 8] = 1.0
    shared["emat"] = emat

    half = 64
    inv_freq = 1.0 / (10000.0 ** (np.arange(half, dtype=np.float64) / half))
    m_ = np.arange(1, 33, dtype=np.float64) * (8.0 / 32)
    sl = 1.0 / (2.0 ** m_)

    al_k = np.zeros((128, 32, 8, 4), dtype=np.float64)
    pp = np.arange(128)
    for h in range(32):
        for j in range(4):
            for r in range(2):
                for d_ in range(4):
                    kpos = (2 * j + r) * 512 + d_ * 128 + pp
                    al_k[:, h, j * 2 + r, d_] = sl[h] * kpos
    shared["alibi_k"] = np.ascontiguousarray(al_k.reshape(128, 1024).astype(f32))

    NEG = -1.0e30

    def masks_for(parity):
        mk = np.zeros((128, 8, 512), dtype=f32)
        pcol = np.arange(128).reshape(128, 1)
        ccol = np.arange(512).reshape(1, 512)
        for r in range(2):
            for d_ in range(4):
                if r == parity:
                    mk[:, r * 4 + d_, :] = np.where(d_ * 128 + pcol <= ccol, 0.0, NEG)
                else:
                    mk[:, r * 4 + d_, :] = 0.0 if parity == 1 else NEG
        return np.ascontiguousarray(mk.reshape(128, 8 * 512))

    in_maps = []
    for core in range(8):
        b = core // 2
        parity = core % 2
        blocks = [2 * i + parity for i in range(4)]
        own_pos = np.concatenate([np.arange(g * 512, (g + 1) * 512) for g in blocks])
        xT_core = np.ascontiguousarray(x[b][own_pos, :].T.astype(f32))
        ang = own_pos.reshape(1, T).astype(np.float64) * inv_freq.reshape(half, 1)
        cos_t = np.cos(ang)
        sin_t = np.sin(ang)
        m = dict(shared)
        m.update(
            xT=xT_core,
            ropeq_cos=np.ascontiguousarray((cos_t * SCALE).astype(f32)),
            ropeq_sin=np.ascontiguousarray((sin_t * SCALE).astype(f32)),
            ropek_cos=np.ascontiguousarray(cos_t.astype(f32)),
            ropek_sin=np.ascontiguousarray(sin_t.astype(f32)),
            alibi_q=np.ascontiguousarray((-sl.reshape(32, 1) * own_pos.reshape(1, T)).astype(f32)),
            masks=masks_for(parity),
        )
        in_maps.append(m)
    return in_maps


def assemble_output(results, key="yT"):
    out = np.zeros((B, S, D), dtype=np.float32)
    for core in range(8):
        b = core // 2
        parity = core % 2
        yc = results[core][key]
        for i in range(4):
            g = 2 * i + parity
            out[b, g * 512:(g + 1) * 512, :] = yc[:, i * 512:(i + 1) * 512].T
    return out


_BUILD_CFG = {"upto": 99, "dump": None, "debug_ln": False}


def kernel(**inputs):
    nc = build_program(upto=_BUILD_CFG["upto"], dump=_BUILD_CFG["dump"],
                       debug_ln=_BUILD_CFG.get("debug_ln", False))
    in_maps = prepare_in_maps(inputs)
    res = run_bass_kernel_spmd(nc, in_maps, core_ids=list(range(8)))
    if _BUILD_CFG["dump"] is not None:
        return res.results
    return assemble_output(res.results)


# version 04
# speedup vs baseline: 1.4413x; 1.0409x over previous
DEBUG_LN = False

import math
from contextlib import ExitStack

import numpy as np

import concourse.bass as bass
import concourse.mybir as mybir
from concourse.bass_utils import run_bass_kernel_spmd

F32 = mybir.dt.float32
BF16 = mybir.dt.bfloat16
I32 = mybir.dt.int32
AF = mybir.ActivationFunctionType
ALU = mybir.AluOpType
AX = mybir.AxisListType

D = 4096
B = 4
S = 4096
T = 2048
NC = 32
NTT = 4
NH = 32
NG = 8
LN_EPS = 1e-5
CLIP = 8.0
SCALE = 1.0 / math.sqrt(128.0)
SWEEPS = [(0, 3), (1, 2)]
NBLK = 4
RG = [[0, 1], [2, 3], [4, 5], [6, 7]]

ENGS = ("tensor", "scalar", "vector", "sync", "gpsimd")


class PB:

    def __init__(self, nc):
        self.nc = nc
        self.sems = {}
        self.ct = {}
        self.st = {}
        self.cur = None
        self.eng = None
        names = ["pe", "act", "dve", "pool", "lx", "lw", "st", "cc"] + [f"d{i}" for i in range(10)]
        for name in names:
            self.sems[name] = nc.alloc_semaphore("sem_" + name)
            self.ct[name] = 0

    def phase(self, fn, name=None):
        snap_ct = dict(self.ct)
        snap_st = dict(self.st)
        final = [None, None]
        with self.nc.Block(name) as blk:
            for engname in ENGS:
                self.ct = dict(snap_ct)
                self.st = dict(snap_st)
                self.cur = engname

                def body(eng):
                    self.eng = eng
                    for s_ in ("st", "d2", "d3", "d8", "d9"):
                        if self.ct[s_]:
                            eng.wait_ge(self.sems[s_], self.ct[s_])
                    fn(self)
                    if final[0] is None:
                        final[0] = dict(self.ct)
                        final[1] = dict(self.st)
                    else:
                        assert final[0] == self.ct, (
                            f"phase {name}: ct divergence on {self.cur}: {final[0]} vs {self.ct}")

                getattr(blk, engname)(body)
        self.ct = final[0]
        self.st = final[1]

    def emit(self, engine, fn, inc=None, n=1):
        if inc is not None and n == 16:
            self.wait(engine, inc, self.ct[inc])
        if self.cur == engine:
            inst = fn(self.eng)
            if inc is not None:
                inst.then_inc(self.sems[inc], n)
        if inc is not None:
            self.ct[inc] += n
            return self.ct[inc]
        return None

    def wait(self, engine, sem, val):
        if not val:
            return
        if self.cur == engine:
            self.eng.wait_ge(self.sems[sem], val)

    def tag(self, sem):
        return self.ct[sem]


def build_program(upto=99, dump=None, debug_ln=False):
    global DEBUG_LN
    DEBUG_LN = debug_ln
    nc = bass.Bass("TRN2", target_bir_lowering=False)

    def inp(name, shape, dt=F32):
        return nc.dram_tensor(name, list(shape), dt, kind="ExternalInput").ap()

    xT = inp("xT", [D, T])
    wq_t = inp("wq_t", [32, 128, 4096])
    wk_t = inp("wk_t", [8, 128, 4096])
    wv_t = inp("wv_t", [8, 128, 4096])
    wo_t = inp("wo_t", [32, 128, 4096])
    wqkv_t = inp("wqkv_t", [96, 128, 4096])
    wout_t = inp("wout_t", [32, 128, 4096])
    wbase_t = inp("wbase_t", [32, 128, 4096])
    lora_a_t = inp("lora_a_t", [128, 32, 128])
    lora_b_f = inp("lora_b_f", [128, 4096])
    router_t = inp("router_t", [128, 32, 16])
    bq_i = inp("bq", [1, 4096])
    bk_i = inp("bk", [1, 1024])
    bv_i = inp("bv", [1, 1024])
    bo_i = inp("bo", [1, 4096])
    bbase_i = inp("b_base", [1, 4096])
    qlnw_i = inp("qlnw_col", [128, 32])
    qlnb_i = inp("qlnb_col", [128, 32])
    klnw_i = inp("klnw_col", [128, 32])
    klnb_i = inp("klnb_col", [128, 32])
    rb_i = inp("router_b", [1, 16])
    prior_i = inp("prior", [1, 16])
    active_i = inp("active", [1, 4], I32)
    iota16_i = inp("iota16", [1, 16])
    eye_i = inp("eye128", [128, 128])
    emat_i = inp("emat", [16, 128])
    ropeq_cos_i = inp("ropeq_cos", [64, T])
    ropeq_sin_i = inp("ropeq_sin", [64, T])
    ropek_cos_i = inp("ropek_cos", [64, T])
    ropek_sin_i = inp("ropek_sin", [64, T])
    alibi_q_i = inp("alibi_q", [32, T])
    alibi_k_i = inp("alibi_k", [128, 1024])
    masks_i = inp("masks", [128, 8 * 512], F32)

    yT = nc.dram_tensor("yT", [D, T], F32, kind="ExternalOutput").ap()
    dump_out = None
    if dump is not None:
        dump_out = nc.dram_tensor("dump", list(dump[1]), dump[2], kind="ExternalOutput").ap()

    def internal(name, shape, dt):
        return nc.dram_tensor(name, list(shape), dt, kind="Internal").ap()

    r1 = internal("r1", [D, T], F32)
    r2 = internal("r2", [D, T], F32)
    r3 = internal("r3", [D, T], F32)
    q_buf = internal("q_buf", [D, T], BF16)
    kstage = internal("kstage", [D, T], BF16)
    qstage = internal("qstage", [D, T], BF16)
    vstage = internal("vstage", [D, T], BF16)
    cka_in = internal("cka_in", [1024, T], BF16)
    cka_out = internal("cka_out", [2048, T], BF16)
    cva_in = internal("cva_in", [NG * 16 * 128, 128], BF16)
    cva_out = internal("cva_out", [2 * NG * 16 * 128, 128], BF16)
    ckb_in = internal("ckb_in", [4096, T], BF16)
    ckb_out = internal("ckb_out", [8192, T], BF16)
    cvb_in = internal("cvb_in", [NH * 16 * 128, 128], BF16)
    cvb_out = internal("cvb_out", [2 * NH * 16 * 128, 128], BF16)
    dram_map = {"r1": r1, "r2": r2, "r3": r3, "q_buf": q_buf, "kstage": kstage,
                "qstage": qstage, "vstage": vstage, "cka_in": cka_in,
                "cka_out": cka_out, "cva_in": cva_in, "cva_out": cva_out,
                "ckb_in": ckb_in, "ckb_out": ckb_out, "cvb_in": cvb_in,
                "cvb_out": cvb_out, "xT": xT, "yT": yT, "ln": None}

    sbt = nc.alloc_sbuf_tensor
    big_guard = nc.sbuf_tensor("big_sb", [128, NC, T], BF16)
    big_sb = big_guard.__enter__()
    zp_sb = sbt("zp_sb", [128, T], BF16)
    ones_b = sbt("ones_b", [1, 512], BF16)
    ones_f = sbt("ones_f", [1, 128], F32)
    ones_col_b = sbt("ones_col_b", [128, 1], BF16)
    eps5 = sbt("eps5", [1, 1], F32)
    eps12 = sbt("eps12", [1, 1], F32)
    eye_b = sbt("eye_b", [128, 128], BF16)
    emat_b = sbt("emat_b", [16, 128], BF16)
    qlnw_c = sbt("qlnw_c", [128, 32], F32)
    qlnb_c = sbt("qlnb_c", [128, 32], F32)
    klnw_c = sbt("klnw_c", [128, 32], F32)
    klnb_c = sbt("klnb_c", [128, 32], F32)
    c16_b = sbt("c16_b", [1, 16], BF16)
    cnt_bc = sbt("cnt_bc", [128, 16], F32)
    actf = sbt("actf", [1, 4], F32)
    acti = sbt("acti", [1, 4], I32)
    iota16 = sbt("iota16_s", [1, 16], F32)
    msk_sb = sbt("msk_sb", [128, 8 * 512], BF16)
    alibi_k_sb = sbt("alibi_k_sb", [128, 1024], F32)
    tmp16a = sbt("tmp16a", [1, 16], F32)
    tmp16b = sbt("tmp16b", [1, 16], F32)
    cnt16 = sbt("cnt16", [1, 16], F32)
    rbf = sbt("rbf", [1, 16], F32)
    priorf = sbt("priorf", [1, 16], F32)
    nrm = sbt("nrm", [1, 320], F32)

    pb = PB(nc)

    def ph_setup_fn(ps):
        def fn(p):
            t_lx = None
            for dst, src in ((qlnw_c, qlnw_i), (qlnb_c, qlnb_i), (klnw_c, klnw_i),
                             (klnb_c, klnb_i), (iota16, iota16_i), (alibi_k_sb, alibi_k_i),
                             (acti, active_i), (rbf, rb_i), (priorf, prior_i)):
                t_lx = p.emit("sync", lambda e, d=dst, s=src: e.dma_start(out=d[...], in_=s[...]), "lx", 16)
            for dst, src in ((eye_b, eye_i), (emat_b, emat_i), (msk_sb, masks_i)):
                p.emit("gpsimd", lambda e, d=dst, s=src: e.dma_start(out=d[...], in_=s[...]), "lw", 16)
            for tile, v in ((ones_b, 1.0), (ones_f, 1.0), (ones_col_b, 1.0),
                            (eps5, LN_EPS), (eps12, 1e-12), (cnt16, 0.0)):
                t_ms = p.emit("vector", lambda e, t=tile, v=v: e.memset(t[...], v), "dve")
            p.st["consts_done"] = t_ms
            def _preload(e):
                from concourse.hw_specs import get_activation_tables
                tables = get_activation_tables(nc.m.arch)
                idx = list(tables.keys()).index("natural_log_exp_and_others")
                return e.add_instruction(mybir.InstLoadActFuncSet(
                    name=nc.get_next_instruction_name(), act_func_set_id=idx,
                    ins=[], outs=[]))
            p.emit("scalar", _preload)
            p.wait("vector", "lx", t_lx)
            p.emit("vector", lambda e: e.tensor_scalar(qlnw_c[...], qlnw_c[...], SCALE, None, ALU.mult), "dve")
            p.emit("vector", lambda e: e.tensor_scalar(qlnb_c[...], qlnb_c[...], SCALE, None, ALU.mult), "dve")
            t3 = p.emit("vector", lambda e: e.tensor_copy(actf[...], acti[...]), "dve")
            tj = t3
            for j in range(4):
                p.wait("vector", "dve", tj)
                ta = p.emit("vector", lambda e, j=j: e.tensor_scalar(
                    tmp16a[...], iota16[...], actf[0:1, j:j + 1], None, ALU.is_equal), "dve")
                p.wait("vector", "dve", ta)
                tj = p.emit("vector", lambda e: e.tensor_tensor(cnt16[...], cnt16[...], tmp16a[...], ALU.add), "dve")
            p.wait("scalar", "lx", t_lx)
            p.wait("scalar", "dve", p.st["consts_done"])
            t4 = p.emit("scalar", lambda e: e.activation(tmp16b[...], priorf[...], AF.Ln, bias=eps12[0:1, 0:1]), "act")
            p.wait("vector", "act", t4)
            t5 = p.emit("vector", lambda e: e.tensor_tensor(tmp16b[...], tmp16b[...], rbf[...], ALU.add), "dve")
            p.wait("vector", "dve", t5)
            p.emit("vector", lambda e: e.tensor_copy(c16_b[...], tmp16b[...]), "dve")
            p.wait("tensor", "dve", tj)
            t7 = p.emit("tensor", lambda e: e.matmul(ps[...], ones_f[0:1, :], cnt16[...], start=True, stop=True), "pe")
            p.wait("scalar", "pe", t7)
            p.st["setup_done"] = p.emit("scalar", lambda e: e.activation(cnt_bc[...], ps[...], AF.Copy), "act")
            p.wait("gpsimd", "lw", p.tag("lw"))
            p.wait("sync", "lx", p.tag("lx"))
        return fn

    def ph_ln(p, bufs, src, src_f32, affine, out_sb, out_dram):
        (xc, xb, xq, rows_m, rows_r, rows_n, tmpn, tmp2, ps_sx, ps_sq, ps_b1, ps_b2) = bufs
        NS = 4
        for s in range(NS):
            cols = slice(s * 512, (s + 1) * 512)
            t_mm = [0] * NC
            for c in range(NC):
                cp = c % 2
                prev_mm = t_mm[c - 2] if c >= 2 else p.st.get("ln_mm_prev", 0)
                p.wait("sync", "pe", prev_mm)
                p.wait("sync", "dve", p.st.get(f"ln_xcuse{cp}", 0))
                t_ld = p.emit("sync", lambda e, c=c, cp=cp, cols=cols: e.dma_start(
                    out=xc[cp][...], in_=src[c * 128:(c + 1) * 128, cols]), f"d{cp}", 16)
                if src_f32:
                    p.wait("vector", f"d{cp}", t_ld)
                    p.wait("vector", "pe", prev_mm)
                    t_xb = p.emit("vector", lambda e, cp=cp: e.tensor_copy(xb[cp][...], xc[cp][...]), "dve")
                    p.st[f"ln_xcuse{cp}"] = t_xb
                else:
                    t_xb = 0
                p.wait("scalar", f"d{cp}", t_ld)
                p.wait("scalar", "pe", prev_mm)
                if src_f32:
                    p.wait("scalar", "dve", t_xb)
                t_xq = p.emit("scalar", lambda e, cp=cp: e.activation(
                    xq[cp][...], (xb[cp] if src_f32 else xc[cp])[...], AF.Square), "act")
                if src_f32:
                    p.wait("tensor", "dve", t_xb)
                p.wait("tensor", "act", t_xq)
                p.emit("tensor", lambda e, c=c, cp=cp: e.matmul(
                    ps_sx[0:1, :], ones_col_b[...], (xb[cp] if src_f32 else xc[cp])[...],
                    start=(c == 0), stop=(c == NC - 1)), None)
                t_mm[c] = p.emit("tensor", lambda e, c=c, cp=cp: e.matmul(
                    ps_sq[0:1, :], ones_col_b[...], xq[cp][...],
                    start=(c == 0), stop=(c == NC - 1)), "pe")
            p.st["ln_mm_prev"] = t_mm[NC - 1]
            m = rows_m[0:1, cols]
            rstd = rows_r[0:1, cols]
            nmr = rows_n[0:1, cols]
            p.wait("vector", "pe", t_mm[NC - 1])
            ta = p.emit("vector", lambda e, m=m: e.tensor_scalar(m, ps_sx[0:1, :], 1.0 / D, None, ALU.mult), "dve")
            tb = p.emit("vector", lambda e, nmr=nmr: e.tensor_scalar(nmr, ps_sq[0:1, :], 1.0 / D, None, ALU.mult), "dve")
            p.wait("vector", "dve", ta)
            tc = p.emit("vector", lambda e, m=m, rstd=rstd: e.tensor_tensor(rstd, m, m, ALU.mult), "dve")
            p.wait("vector", "dve", tc)
            td = p.emit("vector", lambda e, rstd=rstd, nmr=nmr: e.tensor_tensor(rstd, nmr, rstd, ALU.subtract), "dve")
            p.wait("scalar", "dve", td)
            te = p.emit("scalar", lambda e, rstd=rstd: e.activation(rstd, rstd, AF.Ln, bias=eps5[0:1, 0:1]), "act")
            p.wait("scalar", "act", te)
            tf = p.emit("scalar", lambda e, rstd=rstd: e.activation(rstd, rstd, AF.Exp, scale=-0.5), "act")
            p.wait("vector", "act", tf)
            tg = p.emit("vector", lambda e, m=m, rstd=rstd, nmr=nmr: e.scalar_tensor_tensor(
                nmr, m, -1.0, rstd, ALU.mult, ALU.mult), "dve")
            p.st[f"ln_rows{s}"] = (tf, tg)
            if DEBUG_LN:
                p.wait("scalar", "dve", tg)
                p.emit("scalar", lambda e, cols=cols: e.dma_start(
                    out=r3[0:1, cols], in_=rows_m[0:1, cols]), "st", 16)
                p.emit("scalar", lambda e, cols=cols: e.dma_start(
                    out=r3[1:2, cols], in_=rows_r[0:1, cols]), "st", 16)
                p.emit("scalar", lambda e, cols=cols: e.dma_start(
                    out=r3[2:3, cols], in_=rows_n[0:1, cols]), "st", 16)
        for s in range(NS):
            cols = slice(s * 512, (s + 1) * 512)
            tf, tg = p.st[f"ln_rows{s}"]
            p.wait("tensor", "act", tf)
            p.wait("tensor", "dve", tg)
            p.wait("tensor", "dve", p.st.get("ln_bc_free", 0))
            p.emit("tensor", lambda e, cols=cols: e.matmul(
                ps_b1[...], ones_f[0:1, :], rows_r[0:1, cols], start=True, stop=True), None)
            t_bc = p.emit("tensor", lambda e, cols=cols: e.matmul(
                ps_b2[...], ones_f[0:1, :], rows_n[0:1, cols], start=True, stop=True), "pe")
            t_t1 = [0] * NC
            t_t2 = [0] * NC
            for c in range(NC):
                cp = c % 2
                p.wait("sync", "pe", p.st.get("ln_mm_prev", 0))
                p.wait("sync", "dve", max(t_t1[c - 2] if c >= 2 else 0,
                                          p.st.get(f"ln_xcuse{cp}", 0)))
                t_ld = p.emit("sync", lambda e, c=c, cp=cp, cols=cols: e.dma_start(
                    out=xc[cp][...], in_=src[c * 128:(c + 1) * 128, cols]), f"d{cp}", 16)
                p.wait("vector", f"d{cp}", t_ld)
                p.wait("vector", "pe", t_bc)
                p.wait("vector", "dve", t_t2[c - 2] if c >= 2 else 0)
                t_t1[c] = p.emit("vector", lambda e, cp=cp: e.tensor_tensor(
                    tmpn[cp][...], xc[cp][...], ps_b1[...], ALU.mult), "dve")
                p.st[f"ln_xcuse{cp}"] = t_t1[c]
                p.wait("vector", "dve", t_t1[c])
                if affine is None:
                    t_t2[c] = p.emit("vector", lambda e, cp=cp, c=c, cols=cols: e.tensor_tensor(
                        out_sb[:, c, cols], tmpn[cp][...], ps_b2[...], ALU.add), "dve")
                else:
                    t2a = p.emit("vector", lambda e, cp=cp: e.tensor_tensor(
                        tmp2[cp][...], tmpn[cp][...], ps_b2[...], ALU.add), "dve")
                    p.wait("vector", "dve", t2a)
                    p.wait("vector", f"d{2 + cp}", p.st.get(f"ln_st{cp}", 0))
                    w_col, b_col = affine
                    t_t2[c] = p.emit("vector", lambda e, cp=cp, c=c: e.tensor_scalar(
                        xq[cp][...], tmp2[cp][...],
                        w_col[:, c:c + 1], b_col[:, c:c + 1], ALU.mult, ALU.add), "dve")
                if out_dram is not None:
                    p.wait("scalar", "dve", t_t2[c])
                    tl = p.emit("scalar", lambda e, c=c, cols=cols, cp=cp: e.dma_start(
                        out=out_dram[c * 128:(c + 1) * 128, cols], in_=xq[cp][...]), f"d{2 + cp}", 16)
                    p.st[f"ln_st{cp}"] = tl
            p.st["ln_bc_free"] = t_t2[NC - 1]

    def make_ln(src_name, src, tag, src_f32=True, affine=None, out_sb=None, out_dram=None):
        def run():
            with ExitStack() as es:
                xdt = F32 if src_f32 else BF16
                xc = [es.enter_context(nc.sbuf_tensor(f"xc{tag}{i}", [128, 512], xdt)) for i in range(2)]
                xb = [es.enter_context(nc.sbuf_tensor(f"xbb{tag}{i}", [128, 512], BF16)) for i in range(2)]
                xq = [es.enter_context(nc.sbuf_tensor(f"xqq{tag}{i}", [128, 512], BF16)) for i in range(2)]
                rows_m = es.enter_context(nc.sbuf_tensor(f"rm{tag}", [1, T], F32))
                rows_r = es.enter_context(nc.sbuf_tensor(f"rr{tag}", [1, T], F32))
                rows_n = es.enter_context(nc.sbuf_tensor(f"rn{tag}", [1, T], F32))
                tmpn = [es.enter_context(nc.sbuf_tensor(f"tn{tag}{i}", [128, 512], F32)) for i in range(2)]
                tmp2 = [es.enter_context(nc.sbuf_tensor(f"t2{tag}{i}", [128, 512], F32)) for i in range(2)]
                ps_sx = es.enter_context(nc.psum_tensor(f"px{tag}", [1, 512], F32))
                ps_sq = es.enter_context(nc.psum_tensor(f"pq{tag}", [1, 512], F32))
                ps_b1 = es.enter_context(nc.psum_tensor(f"p1{tag}", [128, 512], F32))
                ps_b2 = es.enter_context(nc.psum_tensor(f"p2{tag}", [128, 512], F32))
                bufs = (xc, xb, xq, rows_m, rows_r, rows_n, tmpn, tmp2, ps_sx, ps_sq, ps_b1, ps_b2)
                pb.phase(lambda p: ph_ln(p, bufs, src, src_f32, affine, out_sb, out_dram), tag)
        return run

    def ph_proj(p, wbufs, ps_t, wsrc, ob0, nob, bias_row, consume):
        t_mm = [0] * nob
        t_cons = [None] * nob
        for ob in range(nob):
            par = ob % 2
            if ob >= 2:
                p.wait("gpsimd", "pe", t_mm[ob - 2])
            t_w = p.emit("gpsimd", lambda e, ob=ob, par=par: e.dma_start(
                out=wbufs[par][...], in_=wsrc[ob0 + ob]), f"d{4 + par}", 16)
            p.wait("tensor", f"d{4 + par}", t_w)
            if ob >= 2:
                c2 = t_cons[ob - 2]
                p.wait("tensor", "dve", c2.get("dve", 0))
            for c in range(NC):
                for tt in range(NTT):
                    stop = (bias_row is None and c == NC - 1)
                    last = (bias_row is None and c == NC - 1 and tt == NTT - 1)
                    t_mm[ob] = p.emit("tensor", lambda e, c=c, tt=tt, par=par, stop=stop: e.matmul(
                        ps_t[par][tt][...], wbufs[par][:, c * 128:(c + 1) * 128],
                        big_sb[:, c, tt * 512:(tt + 1) * 512],
                        start=(c == 0), stop=stop), "pe" if last else None) or t_mm[ob]
            if bias_row is not None:
                for tt in range(NTT):
                    t_mm[ob] = p.emit("tensor", lambda e, tt=tt, par=par, ob=ob: e.matmul(
                        ps_t[par][tt][...], bias_row[0:1, ob * 128:(ob + 1) * 128],
                        ones_b[0:1, 0:512], start=False, stop=True),
                        "pe" if tt == NTT - 1 else None) or t_mm[ob]
            t_cons[ob] = consume(p, ob, par, t_mm[ob])

    def rope_consume(tag, ropc, rops, tmpr, rot, t_rope_ld, out_dram, ps_t):
        def consume(p, ob, par, t_mm):
            tl_act = None
            tl_dve = None
            for tt in range(NTT):
                kp = tt % 2
                ps = ps_t[par][tt]
                cols = slice(tt * 512, (tt + 1) * 512)
                p.wait("vector", "pe", t_mm)
                p.wait("vector", "d7", t_rope_ld)
                p.wait("vector", f"d{8 + kp}", p.st.get(f"{tag}_rst{kp}", 0))
                p.wait("vector", "dve", p.st.get(f"{tag}_ropeprev", 0))
                p.emit("vector", lambda e, ps=ps, cols=cols: e.tensor_tensor(
                    tmpr[0][...], ps[0:64, :], ropc[:, cols], ALU.mult), "dve")
                tb = p.emit("vector", lambda e, ps=ps, cols=cols: e.tensor_tensor(
                    tmpr[1][...], ps[64:128, :], rops[:, cols], ALU.mult), "dve")
                p.emit("vector", lambda e, ps=ps, cols=cols: e.tensor_tensor(
                    tmpr[2][...], ps[0:64, :], rops[:, cols], ALU.mult), "dve")
                td = p.emit("vector", lambda e, ps=ps, cols=cols: e.tensor_tensor(
                    tmpr[3][...], ps[64:128, :], ropc[:, cols], ALU.mult), "dve")
                p.wait("vector", "dve", tb)
                p.emit("vector", lambda e, kp=kp: e.tensor_tensor(
                    rot[kp][0:64, :], tmpr[0][...], tmpr[1][...], ALU.subtract), "dve")
                p.wait("vector", "dve", td)
                tf = p.emit("vector", lambda e, kp=kp: e.tensor_tensor(
                    rot[kp][64:128, :], tmpr[2][...], tmpr[3][...], ALU.add), "dve")
                p.st[f"{tag}_ropeprev"] = tf
                tl_dve = tf
                p.wait("scalar", "dve", tf)
                tl_act = p.emit("scalar", lambda e, ob=ob, cols=cols, kp=kp: e.dma_start(
                    out=out_dram[ob * 128:(ob + 1) * 128, cols], in_=rot[kp][...]), f"d{8 + kp}", 16)
                p.st[f"{tag}_rst{kp}"] = tl_act
            return {"dve": tl_dve, "act": tl_act}
        return consume

    def clip_consume(tag, kt, out_dram, ps_t, do_clip):
        def consume(p, ob, par, t_mm):
            tl_act = None
            tl_dve = None
            for tt in range(NTT):
                kp = tt % 2
                cols = slice(tt * 512, (tt + 1) * 512)
                p.wait("vector", "pe", t_mm)
                p.wait("vector", f"d{8 + kp}", p.st.get(f"{tag}_cst{kp}", 0))
                if do_clip:
                    ta = p.emit("vector", lambda e, kp=kp, par=par, tt=tt: e.tensor_scalar(
                        kt[kp][...], ps_t[par][tt][...], CLIP, -CLIP, ALU.min, ALU.max), "dve")
                else:
                    ta = p.emit("vector", lambda e, kp=kp, par=par, tt=tt: e.tensor_copy(
                        kt[kp][...], ps_t[par][tt][...]), "dve")
                tl_dve = ta
                p.wait("scalar", "dve", ta)
                tl_act = p.emit("scalar", lambda e, ob=ob, cols=cols, kp=kp: e.dma_start(
                    out=out_dram[ob * 128:(ob + 1) * 128, cols], in_=kt[kp][...]), f"d{8 + kp}", 16)
                p.st[f"{tag}_cst{kp}"] = tl_act
            return {"dve": tl_dve, "act": tl_act}
        return consume

    def resid_consume(tag, rt, ot, resid_in, resid_out, ps_t):
        def consume(p, ob, par, t_mm):
            tl_act = None
            tl_dve = None
            for tt in range(NTT):
                kp = tt % 2
                cols = slice(tt * 512, (tt + 1) * 512)
                p.wait("sync", "dve", p.st.get(f"{tag}_rr{kp}", 0))
                t_r = p.emit("sync", lambda e, ob=ob, cols=cols, kp=kp: e.dma_start(
                    out=rt[kp][...], in_=resid_in[ob * 128:(ob + 1) * 128, cols]), f"d{kp}", 16)
                p.wait("vector", "pe", t_mm)
                p.wait("vector", f"d{kp}", t_r)
                p.wait("vector", f"d{8 + kp}", p.st.get(f"{tag}_ro{kp}", 0))
                ta = p.emit("vector", lambda e, kp=kp, par=par, tt=tt: e.tensor_tensor(
                    ot[kp][...], ps_t[par][tt][...], rt[kp][...], ALU.add), "dve")
                p.st[f"{tag}_rr{kp}"] = ta
                tl_dve = ta
                p.wait("scalar", "dve", ta)
                tl_act = p.emit("scalar", lambda e, ob=ob, cols=cols, kp=kp: e.dma_start(
                    out=resid_out[ob * 128:(ob + 1) * 128, cols], in_=ot[kp][...]), f"d{8 + kp}", 16)
                p.st[f"{tag}_ro{kp}"] = tl_act
            return {"dve": tl_dve, "act": tl_act}
        return consume

    def make_proj(tag, wsrc, ob0, nob, kind, **kw):
        def run():
            with ExitStack() as es:
                wbufs = [es.enter_context(nc.sbuf_tensor(f"wb{tag}{i}", [128, 4096], BF16)) for i in range(2)]
                ps_t = [[es.enter_context(nc.psum_tensor(f"ps{tag}{i}_{t}", [128, 512], F32)) for t in range(NTT)]
                        for i in range(2)]
                bias_i = kw.get("bias_i")
                bias_row = None
                if bias_i is not None:
                    bias_row = es.enter_context(nc.sbuf_tensor(f"br{tag}", [1, kw["bias_n"]], BF16))
                if kind == "rope":
                    ropc = es.enter_context(nc.sbuf_tensor(f"rc{tag}", [64, T], F32))
                    rops = es.enter_context(nc.sbuf_tensor(f"rs{tag}", [64, T], F32))
                    tmpr = [es.enter_context(nc.sbuf_tensor(f"tr{tag}{i}", [64, 512], F32)) for i in range(4)]
                    rot = [es.enter_context(nc.sbuf_tensor(f"ro{tag}{i}", [128, 512], BF16)) for i in range(2)]
                elif kind == "clip":
                    kt = [es.enter_context(nc.sbuf_tensor(f"kt{tag}{i}", [128, 512], BF16)) for i in range(2)]
                else:
                    rt = [es.enter_context(nc.sbuf_tensor(f"rt{tag}{i}", [128, 512], F32)) for i in range(2)]
                    ot = [es.enter_context(nc.sbuf_tensor(f"ot{tag}{i}", [128, 512], F32)) for i in range(2)]

                def fn(p):
                    if bias_i is not None:
                        t_b = p.emit("gpsimd", lambda e: e.dma_start(out=bias_row[...], in_=bias_i[...]), "d6", 16)
                        p.wait("tensor", "d6", t_b)
                    if kind == "rope":
                        p.emit("sync", lambda e: e.dma_start(out=ropc[...], in_=kw["cos_i"][...]), "d7", 16)
                        t_rl = p.emit("sync", lambda e: e.dma_start(out=rops[...], in_=kw["sin_i"][...]), "d7", 16)
                        consume = rope_consume(tag, ropc, rops, tmpr, rot, t_rl, kw["out_dram"], ps_t)
                    elif kind == "clip":
                        consume = clip_consume(tag, kt, kw["out_dram"], ps_t, kw.get("do_clip", True))
                    else:
                        consume = resid_consume(tag, rt, ot, kw["resid_in"], kw["resid_out"], ps_t)
                    ph_proj(p, wbufs, ps_t, wsrc, ob0, nob, bias_row, consume)

                pb.phase(fn, tag)
        return run

    def make_vtr(nheads, v_in, v_stage_out, ccs, tag):
        def run():
            with ExitStack() as es:
                vin = [es.enter_context(nc.sbuf_tensor(f"vi{tag}{i}", [128, T], BF16)) for i in range(2)]
                vot = [es.enter_context(nc.sbuf_tensor(f"vo{tag}{i}", [128, 128], BF16)) for i in range(2)]
                ps_tr = [es.enter_context(nc.psum_tensor(f"pt{tag}{i}", [128, 128], BF16)) for i in range(2)]

                def fn(p):
                    for h in range(nheads):
                        hp = h % 2
                        p.wait("sync", "pe", p.st.get(f"{tag}_pe{hp}", 0))
                        t_ld = p.emit("sync", lambda e, h=h, hp=hp: e.dma_start(
                            out=vin[hp][...], in_=v_in[h * 128:(h + 1) * 128, :]), f"d{hp}", 16)
                        for kt_ in range(16):
                            kp = kt_ % 2
                            p.wait("tensor", f"d{hp}", t_ld)
                            p.wait("tensor", "dve", p.st.get(f"{tag}_cp{kp}", 0))
                            t_tr = p.emit("tensor", lambda e, kt_=kt_, hp=hp, kp=kp: e.transpose(
                                ps_tr[kp][...], vin[hp][:, kt_ * 128:(kt_ + 1) * 128], eye_b[...]), "pe")
                            p.wait("vector", "pe", t_tr)
                            p.wait("vector", f"d{8 + kp}", p.st.get(f"{tag}_st{kp}", 0))
                            t_cp = p.emit("vector", lambda e, kp=kp: e.tensor_copy(vot[kp][...], ps_tr[kp][...]), "dve")
                            p.st[f"{tag}_cp{kp}"] = t_cp
                            p.wait("scalar", "dve", t_cp)
                            t_st = p.emit("scalar", lambda e, h=h, kt_=kt_, kp=kp: e.dma_start(
                                out=v_stage_out[(h * 16 + kt_) * 128:(h * 16 + kt_ + 1) * 128, :],
                                in_=vot[kp][...]), f"d{8 + kp}", 16)
                            p.st[f"{tag}_st{kp}"] = t_st
                        p.st[f"{tag}_pe{hp}"] = p.tag("pe")
                    for s_ in ("st", "d2", "d3", "d8", "d9"):
                        p.wait("gpsimd", s_, p.tag(s_))
                    for cin, cout in ccs:
                        p.emit("gpsimd", lambda e, cin=cin, cout=cout: e.collective_compute(
                            "AllGather", ALU.bypass, replica_groups=RG,
                            ins=[cin[...]], outs=[cout[...]]), "cc")

                pb.phase(fn, tag)
        return run

    def make_attn(variant):
        n_outer = NG if variant == "a" else NH
        heads_per = 4 if variant == "a" else 1
        k_out = cka_out if variant == "a" else ckb_out
        v_out = cva_out if variant == "a" else cvb_out
        krows = 1024 if variant == "a" else 4096
        mcol0 = 128 if variant == "a" else 160
        DEPTH = 4
        LAG = 3

        def run():
            with ExitStack() as es:
                K_sb = [es.enter_context(nc.sbuf_tensor(f"Kx{variant}{i}", [128, 4096], BF16)) for i in range(2)]
                V_sb = [es.enter_context(nc.sbuf_tensor(f"Vx{variant}{i}", [128, 4096], BF16)) for i in range(2)]
                Q_sb = [es.enter_context(nc.sbuf_tensor(f"Qx{variant}{i}", [128, T], BF16)) for i in range(2)]
                aqb = [es.enter_context(nc.sbuf_tensor(f"aqb{variant}{i}", [1, T], BF16)) for i in range(2)]
                probs = [es.enter_context(nc.sbuf_tensor(f"pr{variant}{i}", [128, 512], BF16)) for i in range(DEPTH)]
                biasH = es.enter_context(nc.sbuf_tensor(f"bH{variant}", [128, 1], F32))
                bias32 = es.enter_context(nc.sbuf_tensor(f"b32{variant}", [128, 32], F32))
                rl = es.enter_context(nc.sbuf_tensor(f"rl{variant}", [1, 512], F32))
                rbx = es.enter_context(nc.sbuf_tensor(f"rb{variant}", [128, 512], F32))
                ps_r = [es.enter_context(nc.psum_tensor(f"psr{variant}{i}", [128, 512], F32)) for i in range(DEPTH)]
                ps_ctx = [es.enter_context(nc.psum_tensor(f"px{variant}{i}", [128, 512], F32)) for i in range(2)]
                ps_l = [es.enter_context(nc.psum_tensor(f"pl{variant}{i}", [1, 512], F32)) for i in range(2)]

                def emit_kv_load(p, go):
                    par = go % 2
                    p.wait("sync", "pe", p.st.get(f"kv_pe{par}", 0))
                    p.wait("sync", "act", p.st.get(f"kv_act{par}", 0))
                    tags = []
                    for r in range(2):
                        tags.append(("lx" if r == 0 else "d0", p.emit(
                            "sync", lambda e, go=go, r=r, par=par: e.dma_start(
                                out=K_sb[par][:, r * 2048:(r + 1) * 2048],
                                in_=k_out[r * krows + go * 128: r * krows + go * 128 + 128, :]),
                            "lx" if r == 0 else "d0", 16)))
                        tags.append(("d1" if r == 0 else "d3", p.emit(
                            "sync", lambda e, go=go, r=r, par=par: e.dma_start(
                                out=V_sb[par].ap().rearrange("p (kt c) -> p kt c", c=128)[:, r * 16:(r + 1) * 16, :],
                                in_=v_out[(r * n_outer + go) * 2048:(r * n_outer + go + 1) * 2048, :]
                                .rearrange("(kt p) c -> p kt c", p=128)),
                            "d1" if r == 0 else "d3", 16)))
                    p.st[f"kvtags{par}"] = tags

                def emit_q_load(p, h):
                    hp = h % 2
                    p.wait("sync", "pe", p.st.get(f"q_pe{hp}", 0))
                    p.wait("sync", "act", p.st.get(f"q_act{hp}", 0))
                    t_q = p.emit("sync", lambda e, h=h, hp=hp: e.dma_start(
                        out=Q_sb[hp][...], in_=q_buf[h * 128:(h + 1) * 128, :]), "d7", 16)
                    p.st[f"qtag{hp}"] = t_q
                    if variant == "b":
                        p.wait("gpsimd", "pe", p.st.get(f"q_pe{hp}", 0))
                        t_aq = p.emit("gpsimd", lambda e, h=h, hp=hp: e.dma_start(
                            out=aqb[hp][...], in_=alibi_q_i[h:h + 1, :]), "d6", 16)
                        p.st[f"aqtag{hp}"] = t_aq

                def fn(p):
                    p.wait("sync", "cc", p.tag("cc"))
                    emit_kv_load(p, 0)
                    emit_q_load(p, 0)
                    for go in range(n_outer):
                        par = go % 2
                        if go + 1 < n_outer:
                            emit_kv_load(p, go + 1)
                        kv_tags = p.st[f"kvtags{par}"]
                        for rh in range(heads_per):
                            h = go * heads_per + rh
                            hp = h % 2
                            if h + 1 < n_outer * heads_per:
                                emit_q_load(p, h + 1)
                            for s_, t_ in kv_tags:
                                p.wait("tensor", s_, t_)
                            p.wait("tensor", "d7", p.st[f"qtag{hp}"])
                            if variant == "b":
                                p.wait("tensor", "d6", p.st[f"aqtag{hp}"])
                            cb0 = p.st.get("sweep_ctr", 0) % 2
                            p.wait("tensor", "dve", p.st.get(f"negM_{variant}", 0))
                            p.wait("tensor", "dve", p.st.get(f"ctx_free{cb0}", 0))
                            tq5 = p.emit("tensor", lambda e, h=h, cb0=cb0: e.matmul(
                                ps_ctx[cb0][:, 0:1], ones_f[0:1, :],
                                nrm[0:1, mcol0 + h:mcol0 + h + 1], start=True, stop=True), "pe")
                            p.wait("scalar", "pe", tq5)
                            p.wait("scalar", "act", p.st.get("bias_used", 0))
                            t_bH = p.emit("scalar", lambda e, cb0=cb0: e.activation(biasH[...], ps_ctx[cb0][:, 0:1], AF.Copy), "act")
                            p.st["biasH_evac"] = t_bH
                            if variant == "b":
                                p.wait("vector", "act", t_bH)
                                p.wait("vector", "act", p.st.get("b32_used", 0))
                                t_b32 = p.emit("vector", lambda e, h=h: e.tensor_scalar(
                                    bias32[...], alibi_k_sb[:, h * 32:(h + 1) * 32],
                                    biasH[:, 0:1], None, ALU.add), "dve")
                                bias_ready = ("dve", t_b32)
                            else:
                                bias_ready = ("act", t_bH)

                            for own_i in range(NBLK):
                                cb = p.st.get("sweep_ctr", 0) % 2
                                p.st["sweep_ctr"] = p.st.get("sweep_ctr", 0) + 1
                                qcols = slice(own_i * 512, (own_i + 1) * 512)
                                visits = []
                                for j in range(own_i + 1):
                                    for r_k in range(2):
                                        for dd in range(4):
                                            visits.append((j, r_k, dd))
                                nv = len(visits)
                                t_sc = [0] * nv
                                t_exp = [0] * nv
                                t_msk = [0] * nv
                                t_pv = [0] * nv

                                def emit_scores(p, v):
                                    j, r_k, dd = visits[v]
                                    sl_ = v % DEPTH
                                    kcol = r_k * 2048 + (j * 4 + dd) * 128
                                    if v >= DEPTH:
                                        p.wait("tensor", "act", t_exp[v - DEPTH])
                                        if t_msk[v - DEPTH]:
                                            p.wait("tensor", "dve", t_msk[v - DEPTH])
                                    else:
                                        p.wait("tensor", "act", p.st.get(f"ringfree{sl_}", 0))
                                        p.wait("tensor", "dve", p.st.get(f"ringfree_d{sl_}", 0))
                                    if variant == "b":
                                        p.emit("tensor", lambda e, sl_=sl_, kcol=kcol, qcols=qcols, par=par, hp=hp: e.matmul(
                                            ps_r[sl_][...], K_sb[par][:, kcol:kcol + 128], Q_sb[hp][:, qcols],
                                            start=True, stop=False), None)
                                        t_sc[v] = p.emit("tensor", lambda e, sl_=sl_, qcols=qcols, hp=hp: e.matmul(
                                            ps_r[sl_][...], ones_b[0:1, 0:128], aqb[hp][0:1, qcols],
                                            start=False, stop=True), "pe")
                                    else:
                                        t_sc[v] = p.emit("tensor", lambda e, sl_=sl_, kcol=kcol, qcols=qcols, par=par, hp=hp: e.matmul(
                                            ps_r[sl_][...], K_sb[par][:, kcol:kcol + 128], Q_sb[hp][:, qcols],
                                            start=True, stop=True), "pe")
                                    if j == own_i:
                                        p.wait("vector", "pe", t_sc[v])
                                        p.wait("vector", "act", p.st.get("rb_used2", 0))
                                        t_msk[v] = p.emit("vector", lambda e, sl_=sl_, r_k=r_k, dd=dd: e.tensor_tensor(
                                            ps_r[sl_][...], ps_r[sl_][...],
                                            msk_sb[:, (r_k * 4 + dd) * 512:(r_k * 4 + dd + 1) * 512],
                                            ALU.add), "dve")
                                    p.wait("scalar", "pe", t_sc[v])
                                    if t_msk[v]:
                                        p.wait("scalar", "dve", t_msk[v])
                                    if v >= DEPTH:
                                        p.wait("scalar", "pe", t_pv[v - DEPTH] if t_pv[v - DEPTH] else 0)
                                    p.wait("scalar", bias_ready[0], bias_ready[1])
                                    if variant == "b":
                                        ki = (j * 2 + r_k) * 4 + dd
                                        t_exp[v] = p.emit("scalar", lambda e, sl_=sl_, ki=ki: e.activation(
                                            probs[sl_][...], ps_r[sl_][...], AF.Exp,
                                            bias=bias32[:, ki:ki + 1]), "act")
                                    else:
                                        t_exp[v] = p.emit("scalar", lambda e, sl_=sl_: e.activation(
                                            probs[sl_][...], ps_r[sl_][...], AF.Exp,
                                            bias=biasH[:, 0:1]), "act")

                                def emit_pv(p, v):
                                    j, r_k, dd = visits[v]
                                    sl_ = v % DEPTH
                                    kidx = r_k * 16 + j * 4 + dd
                                    p.wait("tensor", "act", t_exp[v])
                                    if v == 0:
                                        p.wait("tensor", "dve", p.st.get(f"ctx_free{cb}", 0))
                                        p.wait("tensor", "dve", p.st.get(f"ps_l_free{cb}", 0))
                                        if cb == 0:
                                            p.wait("tensor", "act", p.st.get("biasH_evac", 0))
                                    p.emit("tensor", lambda e, sl_=sl_, kidx=kidx, v=v, nv=nv, par=par, cb=cb: e.matmul(
                                        ps_ctx[cb][...], V_sb[par][:, kidx * 128:(kidx + 1) * 128], probs[sl_][...],
                                        start=(v == 0), stop=(v == nv - 1)), None)
                                    t_pv[v] = p.emit("tensor", lambda e, sl_=sl_, v=v, nv=nv, cb=cb: e.matmul(
                                        ps_l[cb][0:1, :], ones_col_b[...], probs[sl_][...],
                                        start=(v == 0), stop=(v == nv - 1)), "pe")

                                for v in range(nv + LAG):
                                    if v < nv:
                                        emit_scores(p, v)
                                    if v - LAG >= 0:
                                        emit_pv(p, v - LAG)
                                rlc = rl.ap()[0:1, cb * 256:cb * 256 + 256] if False else rl
                                p.wait("vector", "pe", t_pv[nv - 1])
                                p.wait("vector", "act", p.st.get("rb_used2", 0))
                                tr1 = p.emit("vector", lambda e, cb=cb: e.reciprocal(rl[...], ps_l[cb][0:1, :]), "dve")
                                p.st[f"ps_l_free{cb}"] = tr1
                                rslot = nv % DEPTH
                                p.wait("tensor", "dve", tr1)
                                p.wait("tensor", "act", t_exp[nv - 1])
                                p.wait("tensor", "act", t_exp[nv - DEPTH] if nv >= DEPTH else p.st.get(f"ringfree{rslot}", 0))
                                p.wait("tensor", "dve", p.st.get(f"ringfree_d{rslot}", 0) if nv < DEPTH else (t_msk[nv - DEPTH] or 0))
                                t_rb = p.emit("tensor", lambda e, rslot=rslot: e.matmul(
                                    ps_r[rslot][...], ones_f[0:1, :], rl[0:1, :], start=True, stop=True), "pe")
                                p.wait("scalar", "pe", t_rb)
                                p.wait("scalar", "dve", p.st.get("rbx_used", 0))
                                tr3 = p.emit("scalar", lambda e, rslot=rslot: e.activation(
                                    rbx[...], ps_r[rslot][...], AF.Copy), "act")
                                p.st[f"ringfree{rslot}"] = max(p.st.get(f"ringfree{rslot}", 0), tr3)
                                p.st["rb_used2"] = tr3
                                p.wait("vector", "act", tr3)
                                p.wait("vector", "pe", t_pv[nv - 1])
                                tr4 = p.emit("vector", lambda e, h=h, qcols=qcols, cb=cb: e.tensor_tensor(
                                    big_sb[:, h, qcols], ps_ctx[cb][...], rbx[...], ALU.mult), "dve")
                                p.st[f"ctx_free{cb}"] = tr4
                                p.st["rbx_used"] = tr4
                                for sl_ in range(DEPTH):
                                    if sl_ != rslot:
                                        p.st[f"ringfree{sl_}"] = max(p.st.get(f"ringfree{sl_}", 0),
                                                                     t_exp[nv - DEPTH + ((sl_ - nv) % DEPTH)] if nv - DEPTH + ((sl_ - nv) % DEPTH) >= 0 else 0)
                                    tm_i = nv - DEPTH + ((sl_ - nv) % DEPTH)
                                    if 0 <= tm_i < nv and t_msk[tm_i]:
                                        p.st[f"ringfree_d{sl_}"] = max(p.st.get(f"ringfree_d{sl_}", 0), t_msk[tm_i])
                            p.st["b32_used"] = p.tag("act")
                            p.st["bias_used"] = p.tag("act")
                            p.st[f"q_pe{hp}"] = p.tag("pe")
                            p.st[f"q_act{hp}"] = p.tag("act")
                        p.st[f"kv_pe{par}"] = p.tag("pe")
                        p.st[f"kv_act{par}"] = p.tag("act")

                pb.phase(fn, f"attn_{variant}")
        return run

    def make_ffn_a(tag):
        def run():
            with ExitStack() as es:
                la_sb = es.enter_context(nc.sbuf_tensor(f"la{tag}", [128, NC, 128], BF16))
                rt_sb = es.enter_context(nc.sbuf_tensor(f"rt{tag}", [128, NC, 16], BF16))
                z_sb = es.enter_context(nc.sbuf_tensor(f"zs{tag}", [128, 512], BF16))
                lg_b = es.enter_context(nc.sbuf_tensor(f"lg{tag}", [16, 512], BF16))
                lg_tm = es.enter_context(nc.sbuf_tensor(f"lt{tag}", [128, 64], F32))
                mx4 = es.enter_context(nc.sbuf_tensor(f"mx{tag}", [128, 4], F32))
                et = es.enter_context(nc.sbuf_tensor(f"et{tag}", [128, 64], F32))
                sm4 = es.enter_context(nc.sbuf_tensor(f"sm{tag}", [128, 4], F32))
                al_b = es.enter_context(nc.sbuf_tensor(f"ab{tag}", [128, 64], BF16))
                al_em = es.enter_context(nc.sbuf_tensor(f"ae{tag}", [16, 512], BF16))
                ps_z = es.enter_context(nc.psum_tensor(f"pz{tag}", [128, 512], F32))
                ps_lg = es.enter_context(nc.psum_tensor(f"pg{tag}", [16, 512], F32))
                ps_tp = es.enter_context(nc.psum_tensor(f"pp{tag}", [128, 64], BF16))
                ps_tb = es.enter_context(nc.psum_tensor(f"pb{tag}", [16, 512], BF16))
                ps_ax = es.enter_context(nc.psum_tensor(f"pa{tag}", [128, 512], F32))

                def fn(p):
                    p.emit("gpsimd", lambda e: e.dma_start(
                        out=la_sb.ap().rearrange("p c e -> p (c e)"),
                        in_=lora_a_t.rearrange("p c e -> p (c e)")), "lw", 16)
                    t_la = p.emit("gpsimd", lambda e: e.dma_start(
                        out=rt_sb.ap().rearrange("p c e -> p (c e)"),
                        in_=router_t.rearrange("p c e -> p (c e)")), "lw", 16)
                    for tt in range(NTT):
                        cols = slice(tt * 512, (tt + 1) * 512)
                        p.wait("tensor", "lw", t_la)
                        p.wait("tensor", "act", p.st.get("zev", 0))
                        for c in range(NC):
                            p.emit("tensor", lambda e, c=c, cols=cols: e.matmul(
                                ps_z[...], la_sb[:, c, :], big_sb[:, c, cols],
                                start=(c == 0), stop=(c == NC - 1)),
                                "pe" if c == NC - 1 else None)
                        t_z = p.tag("pe")
                        p.wait("tensor", "act", p.st.get("lgev", 0))
                        for c in range(NC):
                            p.emit("tensor", lambda e, c=c, cols=cols: e.matmul(
                                ps_lg[...], rt_sb[:, c, :], big_sb[:, c, cols],
                                start=(c == 0), stop=False), None)
                        p.wait("tensor", "act", p.st["setup_done"])
                        t_lg = p.emit("tensor", lambda e: e.matmul(
                            ps_lg[...], c16_b[...], ones_b[0:1, 0:512], start=False, stop=True), "pe")
                        p.wait("scalar", "pe", t_z)
                        p.wait("scalar", "dve", p.st.get("zpd", 0))
                        t_zev = p.emit("scalar", lambda e: e.activation(z_sb[...], ps_z[...], AF.Copy), "act")
                        p.st["zev"] = t_zev
                        p.wait("scalar", "pe", t_lg)
                        t_lgev = p.emit("scalar", lambda e: e.activation(lg_b[...], ps_lg[...], AF.Copy), "act")
                        p.st["lgev"] = t_lgev
                        p.wait("tensor", "act", t_lgev)
                        p.wait("tensor", "dve", p.st.get("tpc", 0))
                        t_tp = 0
                        for s4 in range(4):
                            t_tp = p.emit("tensor", lambda e, s4=s4: e.transpose(
                                ps_tp[:, s4 * 16:(s4 + 1) * 16], lg_b[:, s4 * 128:(s4 + 1) * 128],
                                eye_b[0:16, 0:16]), "pe")
                        p.wait("vector", "pe", t_tp)
                        t_c = p.emit("vector", lambda e: e.tensor_copy(lg_tm[...], ps_tp[...]), "dve")
                        p.st["tpc"] = t_c
                        p.wait("vector", "dve", t_c)
                        t_mx = p.emit("vector", lambda e: e.tensor_reduce(
                            mx4[...], lg_tm.ap().rearrange("p (s e) -> p s e", e=16),
                            AX.X, ALU.max, negate=True), "dve")
                        t_ex = [0] * 4
                        for s4 in range(4):
                            p.wait("scalar", "dve", t_mx)
                            t_ex[s4] = p.emit("scalar", lambda e, s4=s4: e.activation(
                                et[:, s4 * 16:(s4 + 1) * 16], lg_tm[:, s4 * 16:(s4 + 1) * 16],
                                AF.Exp, bias=mx4[:, s4:s4 + 1]), "act")
                        t_ct = [0] * 4
                        for s4 in range(4):
                            p.wait("vector", "act", t_ex[s4])
                            t_ct[s4] = p.emit("vector", lambda e, s4=s4: e.tensor_tensor(
                                et[:, s4 * 16:(s4 + 1) * 16], et[:, s4 * 16:(s4 + 1) * 16],
                                cnt_bc[...], ALU.mult), "dve")
                        p.wait("vector", "dve", t_ct[3])
                        t_sm = p.emit("vector", lambda e: e.tensor_reduce(
                            sm4[...], et.ap().rearrange("p (s e) -> p s e", e=16), AX.X, ALU.add), "dve")
                        p.wait("vector", "dve", t_sm)
                        t_rc = p.emit("vector", lambda e: e.reciprocal(sm4[...], sm4[...]), "dve")
                        t_al = [0] * 4
                        for s4 in range(4):
                            p.wait("vector", "dve", t_rc)
                            t_al[s4] = p.emit("vector", lambda e, s4=s4: e.tensor_scalar(
                                al_b[:, s4 * 16:(s4 + 1) * 16], et[:, s4 * 16:(s4 + 1) * 16],
                                sm4[:, s4:s4 + 1], 2.0, ALU.mult, ALU.mult), "dve")
                        p.wait("tensor", "dve", t_al[3])
                        p.wait("tensor", "dve", p.st.get("emc", 0))
                        t_tb = 0
                        for s4 in range(4):
                            t_tb = p.emit("tensor", lambda e, s4=s4: e.transpose(
                                ps_tb[:, s4 * 128:(s4 + 1) * 128], al_b[:, s4 * 16:(s4 + 1) * 16],
                                eye_b[...]), "pe")
                        p.wait("vector", "pe", t_tb)
                        t_em = p.emit("vector", lambda e: e.tensor_copy(al_em[...], ps_tb[...]), "dve")
                        p.st["emc"] = t_em
                        p.wait("tensor", "dve", t_em)
                        p.wait("tensor", "dve", p.st.get("zpd", 0))
                        t_ax = p.emit("tensor", lambda e: e.matmul(
                            ps_ax[...], emat_b[...], al_em[...], start=True, stop=True), "pe")
                        p.wait("vector", "pe", t_ax)
                        p.wait("vector", "act", t_zev)
                        t_zp = p.emit("vector", lambda e, cols=cols: e.tensor_tensor(
                            zp_sb[:, cols], ps_ax[...], z_sb[...], ALU.mult), "dve")
                        p.st["zpd"] = t_zp
                    p.st["zp_done"] = p.st["zpd"]

                pb.phase(fn, tag)
        return run

    def make_ffn_b(resid_in, resid_out, tag):
        def run():
            with ExitStack() as es:
                lb_sb = es.enter_context(nc.sbuf_tensor(f"lb{tag}", [128, 4096], BF16))
                bias_row = es.enter_context(nc.sbuf_tensor(f"bb{tag}", [1, 4096], BF16))
                wbufs = [es.enter_context(nc.sbuf_tensor(f"wb{tag}{i}", [128, 4096], BF16)) for i in range(2)]
                ps_t = [[es.enter_context(nc.psum_tensor(f"ps{tag}{i}_{t}", [128, 512], F32)) for t in range(NTT)]
                        for i in range(2)]
                rt = [es.enter_context(nc.sbuf_tensor(f"rt{tag}{i}", [128, 512], F32)) for i in range(2)]
                ot = [es.enter_context(nc.sbuf_tensor(f"ot{tag}{i}", [128, 512], F32)) for i in range(2)]

                def fn(p):
                    p.emit("gpsimd", lambda e: e.dma_start(out=lb_sb[...], in_=lora_b_f[...]), "d6", 16)
                    t_lb = p.emit("gpsimd", lambda e: e.dma_start(out=bias_row[...], in_=bbase_i[...]), "d6", 16)
                    consume = resid_consume(tag, rt, ot, resid_in, resid_out, ps_t)
                    t_mm = [0] * 32
                    t_cons = [None] * 32
                    for ob in range(32):
                        par = ob % 2
                        if ob >= 2:
                            p.wait("gpsimd", "pe", t_mm[ob - 2])
                        t_w = p.emit("gpsimd", lambda e, ob=ob, par=par: e.dma_start(
                            out=wbufs[par][...], in_=wbase_t[ob]), f"d{4 + par}", 16)
                        p.wait("tensor", f"d{4 + par}", t_w)
                        p.wait("tensor", "d6", t_lb)
                        p.wait("tensor", "dve", p.st["zp_done"])
                        if ob >= 2:
                            c2 = t_cons[ob - 2]
                            p.wait("tensor", "dve", c2.get("dve", 0))
                        for c in range(NC):
                            for tt in range(NTT):
                                p.emit("tensor", lambda e, c=c, tt=tt, par=par: e.matmul(
                                    ps_t[par][tt][...], wbufs[par][:, c * 128:(c + 1) * 128],
                                    big_sb[:, c, tt * 512:(tt + 1) * 512],
                                    start=(c == 0), stop=False), None)
                        for tt in range(NTT):
                            p.emit("tensor", lambda e, tt=tt, par=par, ob=ob: e.matmul(
                                ps_t[par][tt][...], lb_sb[:, ob * 128:(ob + 1) * 128],
                                zp_sb[:, tt * 512:(tt + 1) * 512], start=False, stop=False), None)
                            t_mm[ob] = p.emit("tensor", lambda e, tt=tt, par=par, ob=ob: e.matmul(
                                ps_t[par][tt][...], bias_row[0:1, ob * 128:(ob + 1) * 128],
                                ones_b[0:1, 0:512], start=False, stop=True),
                                "pe" if tt == NTT - 1 else None) or t_mm[ob]
                        t_cons[ob] = consume(p, ob, par, t_mm[ob])

                pb.phase(fn, tag)
        return run

    def make_copy(src_fn, dst, tag, rows=D, dt=F32, width=T):
        def run():
            with ExitStack() as es:
                buf = [es.enter_context(nc.sbuf_tensor(f"cp{tag}{i}", [128, width], dt)) for i in range(2)]
                src = src_fn() if callable(src_fn) else src_fn

                def fn(p):
                    p.wait("sync", "cc", p.tag("cc"))
                    for c in range(rows // 128):
                        cp = c % 2
                        p.wait("sync", f"d{8 + cp}", p.st.get(f"{tag}_s{cp}", 0))
                        t_l = p.emit("sync", lambda e, c=c, cp=cp: e.dma_start(
                            out=buf[cp][...], in_=src[c * 128:(c + 1) * 128, :]), f"d{cp}", 16)
                        p.wait("scalar", f"d{cp}", t_l)
                        t_s = p.emit("scalar", lambda e, c=c, cp=cp: e.dma_start(
                            out=dst[c * 128:(c + 1) * 128, :], in_=buf[cp][...]), f"d{8 + cp}", 16)
                        p.st[f"{tag}_s{cp}"] = t_s

                pb.phase(fn, tag)
        return run

    def make_mprep(variant):
        qsrc = q_buf
        ksrc = cka_in if variant == "a" else ckb_in
        nk = NG if variant == "a" else NH
        qcol0 = 0 if variant == "a" else 64
        kcol0 = 32 if variant == "a" else 96
        mcol0 = 128 if variant == "a" else 160

        def run():
            with ExitStack() as es:
                til = [es.enter_context(nc.sbuf_tensor(f"mp{variant}{i}", [128, 512], BF16)) for i in range(2)]
                sqt = [es.enter_context(nc.sbuf_tensor(f"mq{variant}{i}", [128, 512], BF16)) for i in range(2)]
                nrb = [es.enter_context(nc.sbuf_tensor(f"mn{variant}{i}", [1, 512], F32)) for i in range(2)]
                ps2 = [es.enter_context(nc.psum_tensor(f"mps{variant}{i}", [1, 512], F32)) for i in range(2)]

                def one_tensor(p, src, nh, col0):
                    for h in range(nh):
                        t_hmax = 0
                        for tt in range(NTT):
                            bp = (h * NTT + tt) % 2
                            p.wait("sync", "act", p.st.get(f"mp_sq{bp}", 0))
                            t_ld = p.emit("sync", lambda e, h=h, tt=tt, bp=bp, src=src: e.dma_start(
                                out=til[bp][...], in_=src[h * 128:(h + 1) * 128, tt * 512:(tt + 1) * 512]),
                                f"d{bp}", 16)
                            p.wait("scalar", f"d{bp}", t_ld)
                            p.wait("scalar", "pe", p.st.get(f"mp_mm{bp}", 0))
                            t_sq = p.emit("scalar", lambda e, bp=bp: e.activation(
                                sqt[bp][...], til[bp][...], AF.Square), "act")
                            p.st[f"mp_sq{bp}"] = t_sq
                            p.wait("tensor", "act", t_sq)
                            p.wait("tensor", "act", p.st.get(f"mp_ev{bp}", 0))
                            t_mm = p.emit("tensor", lambda e, bp=bp: e.matmul(
                                ps2[bp][0:1, :], ones_col_b[...], sqt[bp][...],
                                start=True, stop=True), "pe")
                            p.st[f"mp_mm{bp}"] = t_mm
                            p.wait("scalar", "pe", t_mm)
                            p.wait("scalar", "dve", p.st.get(f"mp_red{bp}", 0))
                            t_ev = p.emit("scalar", lambda e, bp=bp: e.activation(
                                nrb[bp][...], ps2[bp][0:1, :], AF.Copy), "act")
                            p.st[f"mp_ev{bp}"] = t_ev
                            p.wait("vector", "act", t_ev)
                            if tt == 0:
                                t_red = p.emit("vector", lambda e, h=h, col0=col0, bp=bp: e.tensor_reduce(
                                    nrm[0:1, col0 + h:col0 + h + 1], nrb[bp][...], AX.X, ALU.max), "dve")
                            else:
                                p.wait("vector", "dve", t_hmax)
                                t_red = p.emit("vector", lambda e, bp=bp: e.tensor_reduce(
                                    nrm[0:1, 192:193], nrb[bp][...], AX.X, ALU.max), "dve")
                                p.wait("vector", "dve", t_red)
                                t_red = p.emit("vector", lambda e, h=h, col0=col0: e.tensor_tensor(
                                    nrm[0:1, col0 + h:col0 + h + 1], nrm[0:1, col0 + h:col0 + h + 1],
                                    nrm[0:1, 192:193], ALU.max), "dve")
                            t_hmax = t_red
                            p.st[f"mp_red{bp}"] = t_red

                def fn(p):
                    one_tensor(p, qsrc, NH, qcol0)
                    one_tensor(p, ksrc, nk, kcol0)
                    a1 = nrm[0:1, 200:232]
                    a2 = nrm[0:1, 232:264] if False else nrm[0:1, 192 + 8:232] if False else None
                    t0 = p.st[f"mp_red0"]
                    t1 = p.st[f"mp_red1"]
                    p.wait("scalar", "dve", max(t0, t1))
                    tq = p.emit("scalar", lambda e: e.activation(a1, nrm[0:1, qcol0:qcol0 + 32], AF.Ln), "act")
                    if variant == "a":
                        tk = p.emit("scalar", lambda e: e.activation(
                            nrm[0:1, 192:200], nrm[0:1, kcol0:kcol0 + 8], AF.Ln), "act")
                        tcp = 0
                        for rr in range(4):
                            p.wait("vector", "act", tk)
                            tcp = p.emit("vector", lambda e, rr=rr: e.tensor_copy(
                                nrm.ap()[0:1, 232:264].rearrange("u (g r) -> u g r", r=4)[:, :, rr],
                                nrm[0:1, 192:200]), "dve")
                        p.wait("vector", "act", tq)
                        p.wait("vector", "dve", tcp)
                        t_add = p.emit("vector", lambda e: e.tensor_tensor(
                            a1, a1, nrm[0:1, 232:264], ALU.add), "dve")
                    else:
                        tk = p.emit("scalar", lambda e: e.activation(
                            nrm[0:1, 232:264], nrm[0:1, kcol0:kcol0 + 32], AF.Ln), "act")
                        p.wait("vector", "act", tk)
                        p.wait("vector", "act", tq)
                        t_add = p.emit("vector", lambda e: e.tensor_tensor(
                            a1, a1, nrm[0:1, 232:264], ALU.add), "dve")
                    p.wait("scalar", "dve", t_add)
                    t_exp = p.emit("scalar", lambda e: e.activation(a1, a1, AF.Exp, scale=0.5), "act")
                    p.wait("vector", "act", t_exp)
                    t_m = p.emit("vector", lambda e, mcol0=mcol0: e.tensor_scalar(
                        nrm[0:1, mcol0:mcol0 + 32], a1, -1.01, None, ALU.mult), "dve")
                    p.st[f"negM_{variant}"] = t_m

                pb.phase(fn, f"mprep_{variant}")
        return run

    phases = [
        ("setup", None),
        ("ln1", make_ln("xT", xT, "ln1", out_sb=big_sb)),
        ("ka", make_proj("ka", wk_t, 0, NG, "rope", cos_i=ropek_cos_i, sin_i=ropek_sin_i,
                         out_dram=cka_in, bias_i=bk_i, bias_n=1024)),
        ("va", make_proj("va", wv_t, 0, NG, "clip", out_dram=vstage, do_clip=False,
                         bias_i=bv_i, bias_n=1024)),
        ("vta", make_vtr(NG, vstage, cva_in, [(cka_in, cka_out), (cva_in, cva_out)], "vta")),
        ("qa", make_proj("qa", wq_t, 0, NH, "rope", cos_i=ropeq_cos_i, sin_i=ropeq_sin_i,
                         out_dram=q_buf, bias_i=bq_i, bias_n=4096)),
        ("mpa", make_mprep("a")),
        ("attn_a", make_attn("a")),
        ("out_a", make_proj("out_a", wo_t, 0, 32, "resid", resid_in=xT, resid_out=r1,
                            bias_i=bo_i, bias_n=4096)),
        ("ln2", make_ln("r1", r1, "ln2", out_sb=big_sb)),
        ("ffn1a", make_ffn_a("f1")),
        ("ffn1b", make_ffn_b(r1, r2, "f1b")),
        ("ln3", make_ln("r2", r2, "ln3", out_sb=big_sb)),
        ("kb", make_proj("kb", wqkv_t, 32, 32, "clip", out_dram=kstage)),
        ("klnb", make_ln("kstage", kstage, "klnb", src_f32=False,
                         affine=(klnw_c, klnb_c), out_dram=ckb_in)),
        ("vb", make_proj("vb", wqkv_t, 64, 32, "clip", out_dram=vstage)),
        ("vtb", make_vtr(NH, vstage, cvb_in, [(ckb_in, ckb_out), (cvb_in, cvb_out)], "vtb")),
        ("qb", make_proj("qb", wqkv_t, 0, 32, "clip", out_dram=qstage)),
        ("qlnb", make_ln("qstage", qstage, "qlnb", src_f32=False,
                         affine=(qlnw_c, qlnb_c), out_dram=q_buf)),
        ("mpb", make_mprep("b")),
        ("attn_b", make_attn("b")),
        ("out_b", make_proj("out_b", wout_t, 0, 32, "resid", resid_in=r2, resid_out=r3)),
        ("ln4", make_ln("r3", r3, "ln4", out_sb=big_sb)),
        ("ffn2a", make_ffn_a("f2")),
        ("ffn2b", make_ffn_b(r3, yT, "f2b")),
    ]

    resid_after = [xT, xT, xT, xT, xT, xT, xT, xT, xT, r1, r1, r1, r2, r2, r2,
                   r2, r2, r2, r2, r2, r2, r3, r3, r3, r3]

    nrun = min(upto, len(phases))
    for i in range(nrun):
        name, fn = phases[i]
        if name == "setup":
            with nc.psum_tensor("ps_set", [128, 16], F32) as ps:
                pb.phase(ph_setup_fn(ps), "setup")
        else:
            fn()
    if nrun < len(phases):
        make_copy(resid_after[nrun], yT, "fincp")()
    if dump is not None:
        if dump[0] == "big_sb":
            def dump_big(p):
                for c in range(NC):
                    t_s = p.emit("scalar", lambda e, c=c: e.dma_start(
                        out=dump_out[c * 128:(c + 1) * 128, :], in_=big_sb[:, c, :]), "st", 16)
                p.wait("scalar", "st", p.tag("st"))
            pb.phase(dump_big, "dump_big")
        elif dump[0] == "zp_sb":
            def dump_zp(p):
                t_s = p.emit("scalar", lambda e: e.dma_start(out=dump_out[...], in_=zp_sb[...]), "st", 16)
                p.wait("scalar", "st", p.tag("st"))
            pb.phase(dump_zp, "dump_zp")
        else:
            src = dram_map[dump[0]]
            make_copy(src, dump_out, "dmp", rows=dump[1][0], dt=dump[2], width=dump[1][1])()

    def fin(p):
        for s_ in ("st", "lx", "lw", "cc") + tuple(f"d{i}" for i in range(10)):
            p.wait("sync", s_, p.tag(s_))
    pb.phase(fin, "fin_wait")
    return nc



def _tile_w(w_in_out):
    i_, o_ = w_in_out.shape
    t = w_in_out.reshape(i_ // 128, 128, o_ // 128, 128)
    t = t.transpose(2, 1, 0, 3).reshape(o_ // 128, 128, (i_ // 128) * 128)
    return np.ascontiguousarray(t)


def prepare_in_maps(inputs):
    x = np.asarray(inputs["L0_attn_a_hidden_states"])
    f32 = np.float32

    wq_t = _tile_w(np.asarray(inputs["L0_attn_a_Wq"]))
    wk_t = _tile_w(np.asarray(inputs["L0_attn_a_Wk"]))
    wv_t = _tile_w(np.asarray(inputs["L0_attn_a_Wv"]))
    wo_t = _tile_w(np.asarray(inputs["L0_attn_a_Wo"]))
    wqkv_t = _tile_w(np.ascontiguousarray(np.asarray(inputs["L0_attn_b_Wqkv"]).T))
    wout_t = _tile_w(np.ascontiguousarray(np.asarray(inputs["L0_attn_b_Wout"]).T))
    wbase_t = _tile_w(np.ascontiguousarray(np.asarray(inputs["L0_ffn_W_base"]).T))
    lora_a = np.asarray(inputs["L0_ffn_lora_A"])
    lora_a_t = np.ascontiguousarray(
        lora_a.reshape(128, 4096).T.reshape(32, 128, 128).transpose(1, 0, 2))
    lora_b = np.asarray(inputs["L0_ffn_lora_B"])
    lora_b_f = np.ascontiguousarray(lora_b.transpose(0, 2, 1).reshape(128, 4096))
    router_w = np.asarray(inputs["L0_ffn_router_W"])
    router_t = np.ascontiguousarray(
        router_w.T.reshape(32, 128, 16).transpose(1, 0, 2))

    def row(v, n):
        return np.ascontiguousarray(np.asarray(v, dtype=f32).reshape(1, n))

    def col32(v):
        return np.ascontiguousarray(np.asarray(v, dtype=f32).reshape(32, 128).T)

    shared = dict(
        wq_t=wq_t, wk_t=wk_t, wv_t=wv_t, wo_t=wo_t, wqkv_t=wqkv_t,
        wout_t=wout_t, wbase_t=wbase_t, lora_a_t=lora_a_t, lora_b_f=lora_b_f,
        router_t=router_t,
        bq=row(inputs["L0_attn_a_bq"], 4096), bk=row(inputs["L0_attn_a_bk"], 1024),
        bv=row(inputs["L0_attn_a_bv"], 1024), bo=row(inputs["L0_attn_a_bo"], 4096),
        b_base=row(inputs["L0_ffn_b_base"], 4096),
        qlnw_col=col32(inputs["L0_attn_b_q_ln_w"]), qlnb_col=col32(inputs["L0_attn_b_q_ln_b"]),
        klnw_col=col32(inputs["L0_attn_b_k_ln_w"]), klnb_col=col32(inputs["L0_attn_b_k_ln_b"]),
        router_b=row(inputs["L0_ffn_router_b"], 16),
        prior=row(inputs["L0_ffn_domain_prior"], 16),
        active=np.ascontiguousarray(np.asarray(inputs["L0_ffn_active"], dtype=np.int32).reshape(1, 4)),
        iota16=np.arange(16, dtype=f32).reshape(1, 16),
        eye128=np.eye(128, dtype=f32),
    )
    emat = np.zeros((16, 128), dtype=f32)
    for e_ in range(16):
        emat[e_, e_ * 8:(e_ + 1) * 8] = 1.0
    shared["emat"] = emat

    half = 64
    inv_freq = 1.0 / (10000.0 ** (np.arange(half, dtype=np.float64) / half))
    m_ = np.arange(1, 33, dtype=np.float64) * (8.0 / 32)
    sl = 1.0 / (2.0 ** m_)

    al_k = np.zeros((128, 32, 8, 4), dtype=np.float64)
    pp = np.arange(128)
    for h in range(32):
        for j in range(4):
            for r in range(2):
                for d_ in range(4):
                    kpos = (2 * j + r) * 512 + d_ * 128 + pp
                    al_k[:, h, j * 2 + r, d_] = sl[h] * kpos
    shared["alibi_k"] = np.ascontiguousarray(al_k.reshape(128, 1024).astype(f32))

    NEG = -1.0e30

    def masks_for(parity):
        mk = np.zeros((128, 8, 512), dtype=f32)
        pcol = np.arange(128).reshape(128, 1)
        ccol = np.arange(512).reshape(1, 512)
        for r in range(2):
            for d_ in range(4):
                if r == parity:
                    mk[:, r * 4 + d_, :] = np.where(d_ * 128 + pcol <= ccol, 0.0, NEG)
                else:
                    mk[:, r * 4 + d_, :] = 0.0 if parity == 1 else NEG
        return np.ascontiguousarray(mk.reshape(128, 8 * 512))

    in_maps = []
    for core in range(8):
        b = core // 2
        parity = core % 2
        blocks = [2 * i + parity for i in range(4)]
        own_pos = np.concatenate([np.arange(g * 512, (g + 1) * 512) for g in blocks])
        xT_core = np.ascontiguousarray(x[b][own_pos, :].T.astype(f32))
        ang = own_pos.reshape(1, T).astype(np.float64) * inv_freq.reshape(half, 1)
        cos_t = np.cos(ang)
        sin_t = np.sin(ang)
        m = dict(shared)
        m.update(
            xT=xT_core,
            ropeq_cos=np.ascontiguousarray((cos_t * SCALE).astype(f32)),
            ropeq_sin=np.ascontiguousarray((sin_t * SCALE).astype(f32)),
            ropek_cos=np.ascontiguousarray(cos_t.astype(f32)),
            ropek_sin=np.ascontiguousarray(sin_t.astype(f32)),
            alibi_q=np.ascontiguousarray((-sl.reshape(32, 1) * own_pos.reshape(1, T)).astype(f32)),
            masks=masks_for(parity),
        )
        in_maps.append(m)
    return in_maps


def assemble_output(results, key="yT"):
    out = np.zeros((B, S, D), dtype=np.float32)
    for core in range(8):
        b = core // 2
        parity = core % 2
        yc = results[core][key]
        for i in range(4):
            g = 2 * i + parity
            out[b, g * 512:(g + 1) * 512, :] = yc[:, i * 512:(i + 1) * 512].T
    return out


_BUILD_CFG = {"upto": 99, "dump": None, "debug_ln": False}


def kernel(**inputs):
    nc = build_program(upto=_BUILD_CFG["upto"], dump=_BUILD_CFG["dump"],
                       debug_ln=_BUILD_CFG.get("debug_ln", False))
    in_maps = prepare_in_maps(inputs)
    res = run_bass_kernel_spmd(nc, in_maps, core_ids=list(range(8)))
    if _BUILD_CFG["dump"] is not None:
        return res.results
    return assemble_output(res.results)


# version 05
# speedup vs baseline: 1.5739x; 1.0920x over previous
DEBUG_LN = False

import math
from contextlib import ExitStack

import numpy as np

import concourse.bass as bass
import concourse.mybir as mybir
from concourse.bass_utils import run_bass_kernel_spmd

F32 = mybir.dt.float32
BF16 = mybir.dt.bfloat16
I32 = mybir.dt.int32
AF = mybir.ActivationFunctionType
ALU = mybir.AluOpType
AX = mybir.AxisListType

D = 4096
B = 4
S = 4096
T = 2048
NC = 32
NTT = 4
NH = 32
NG = 8
LN_EPS = 1e-5
CLIP = 8.0
SCALE = 1.0 / math.sqrt(128.0)
SWEEPS = [(0, 3), (1, 2)]
NBLK = 4
RG = [[0, 1], [2, 3], [4, 5], [6, 7]]

ENGS = ("tensor", "scalar", "vector", "sync", "gpsimd")


class PB:

    def __init__(self, nc):
        self.nc = nc
        self.sems = {}
        self.ct = {}
        self.st = {}
        self.cur = None
        self.eng = None
        names = ["pe", "act", "dve", "pool", "lx", "lw", "st", "cc"] + [f"d{i}" for i in range(10)]
        for name in names:
            self.sems[name] = nc.alloc_semaphore("sem_" + name)
            self.ct[name] = 0

    def phase(self, fn, name=None):
        snap_ct = dict(self.ct)
        snap_st = dict(self.st)
        final = [None, None]
        with self.nc.Block(name) as blk:
            for engname in ENGS:
                self.ct = dict(snap_ct)
                self.st = dict(snap_st)
                self.cur = engname

                def body(eng):
                    self.eng = eng
                    for s_ in ("st", "d2", "d3", "d8", "d9"):
                        if self.ct[s_]:
                            eng.wait_ge(self.sems[s_], self.ct[s_])
                    fn(self)
                    if final[0] is None:
                        final[0] = dict(self.ct)
                        final[1] = dict(self.st)
                    else:
                        assert final[0] == self.ct, (
                            f"phase {name}: ct divergence on {self.cur}: {final[0]} vs {self.ct}")

                getattr(blk, engname)(body)
        self.ct = final[0]
        self.st = final[1]

    def emit(self, engine, fn, inc=None, n=1):
        if inc is not None and n == 16:
            self.wait(engine, inc, self.ct[inc])
        if self.cur == engine:
            inst = fn(self.eng)
            if inc is not None:
                inst.then_inc(self.sems[inc], n)
        if inc is not None:
            self.ct[inc] += n
            return self.ct[inc]
        return None

    def wait(self, engine, sem, val):
        if not val:
            return
        if self.cur == engine:
            self.eng.wait_ge(self.sems[sem], val)

    def tag(self, sem):
        return self.ct[sem]


def build_program(upto=99, dump=None, debug_ln=False):
    global DEBUG_LN
    DEBUG_LN = debug_ln
    nc = bass.Bass("TRN2", target_bir_lowering=False)

    def inp(name, shape, dt=F32):
        return nc.dram_tensor(name, list(shape), dt, kind="ExternalInput").ap()

    xT = inp("xT", [D, T])
    wq_t = inp("wq_t", [32, 128, 4096])
    wk_t = inp("wk_t", [8, 128, 4096])
    wv_t = inp("wv_t", [8, 128, 4096])
    wo_t = inp("wo_t", [32, 128, 4096])
    wqkv_t = inp("wqkv_t", [96, 128, 4096])
    wout_t = inp("wout_t", [32, 128, 4096])
    wbase_t = inp("wbase_t", [32, 128, 4096])
    lora_a_t = inp("lora_a_t", [128, 32, 128])
    lora_b_f = inp("lora_b_f", [128, 4096])
    router_t = inp("router_t", [128, 32, 16])
    bq_i = inp("bq", [1, 4096])
    bk_i = inp("bk", [1, 1024])
    bv_i = inp("bv", [1, 1024])
    bo_i = inp("bo", [1, 4096])
    bbase_i = inp("b_base", [1, 4096])
    qlnw_i = inp("qlnw_col", [128, 32])
    qlnb_i = inp("qlnb_col", [128, 32])
    klnw_i = inp("klnw_col", [128, 32])
    klnb_i = inp("klnb_col", [128, 32])
    rb_i = inp("router_b", [1, 16])
    prior_i = inp("prior", [1, 16])
    active_i = inp("active", [1, 4], I32)
    iota16_i = inp("iota16", [1, 16])
    eye_i = inp("eye128", [128, 128])
    emat_i = inp("emat", [16, 128])
    ropeq_cos_i = inp("ropeq_cos", [64, T])
    ropeq_sin_i = inp("ropeq_sin", [64, T])
    ropek_cos_i = inp("ropek_cos", [64, T])
    ropek_sin_i = inp("ropek_sin", [64, T])
    alibi_q_i = inp("alibi_q", [32, T])
    alibi_qb_i = inp("alibi_qb", [32, 128 * T])
    alibi_k_i = inp("alibi_k", [128, 1024])
    masks_i = inp("masks", [128, 8 * 512], F32)

    yT = nc.dram_tensor("yT", [D, T], F32, kind="ExternalOutput").ap()
    dump_out = None
    if dump is not None:
        dump_out = nc.dram_tensor("dump", list(dump[1]), dump[2], kind="ExternalOutput").ap()

    def internal(name, shape, dt):
        return nc.dram_tensor(name, list(shape), dt, kind="Internal").ap()

    r1 = internal("r1", [D, T], F32)
    r2 = internal("r2", [D, T], F32)
    r3 = internal("r3", [D, T], F32)
    q_buf = internal("q_buf", [D, T], BF16)
    kstage = internal("kstage", [D, T], BF16)
    qstage = internal("qstage", [D, T], BF16)
    vstage = internal("vstage", [D, T], BF16)
    cka_in = internal("cka_in", [1024, T], BF16)
    cka_out = internal("cka_out", [2048, T], BF16)
    cva_in = internal("cva_in", [NG * 16 * 128, 128], BF16)
    cva_out = internal("cva_out", [2 * NG * 16 * 128, 128], BF16)
    ckb_in = internal("ckb_in", [4096, T], BF16)
    ckb_out = internal("ckb_out", [8192, T], BF16)
    cvb_in = internal("cvb_in", [NH * 16 * 128, 128], BF16)
    cvb_out = internal("cvb_out", [2 * NH * 16 * 128, 128], BF16)
    dram_map = {"r1": r1, "r2": r2, "r3": r3, "q_buf": q_buf, "kstage": kstage,
                "qstage": qstage, "vstage": vstage, "cka_in": cka_in,
                "cka_out": cka_out, "cva_in": cva_in, "cva_out": cva_out,
                "ckb_in": ckb_in, "ckb_out": ckb_out, "cvb_in": cvb_in,
                "cvb_out": cvb_out, "xT": xT, "yT": yT, "ln": None}

    sbt = nc.alloc_sbuf_tensor
    big_guard = nc.sbuf_tensor("big_sb", [128, NC, T], BF16)
    big_sb = big_guard.__enter__()
    zp_sb = sbt("zp_sb", [128, T], BF16)
    ones_b = sbt("ones_b", [1, 512], BF16)
    ones_f = sbt("ones_f", [1, 128], F32)
    ones_col_b = sbt("ones_col_b", [128, 1], BF16)
    eps5 = sbt("eps5", [1, 1], F32)
    eps12 = sbt("eps12", [1, 1], F32)
    eye_b = sbt("eye_b", [128, 128], BF16)
    emat_b = sbt("emat_b", [16, 128], BF16)
    qlnw_c = sbt("qlnw_c", [128, 32], F32)
    qlnb_c = sbt("qlnb_c", [128, 32], F32)
    klnw_c = sbt("klnw_c", [128, 32], F32)
    klnb_c = sbt("klnb_c", [128, 32], F32)
    c16_b = sbt("c16_b", [1, 16], BF16)
    cnt_bc = sbt("cnt_bc", [128, 16], F32)
    actf = sbt("actf", [1, 4], F32)
    acti = sbt("acti", [1, 4], I32)
    iota16 = sbt("iota16_s", [1, 16], F32)
    msk_sb = sbt("msk_sb", [128, 8 * 512], BF16)
    alibi_k_sb = sbt("alibi_k_sb", [128, 1024], F32)
    tmp16a = sbt("tmp16a", [1, 16], F32)
    tmp16b = sbt("tmp16b", [1, 16], F32)
    cnt16 = sbt("cnt16", [1, 16], F32)
    rbf = sbt("rbf", [1, 16], F32)
    priorf = sbt("priorf", [1, 16], F32)
    nrm = sbt("nrm", [1, 320], F32)

    pb = PB(nc)

    def ph_setup_fn(ps):
        def fn(p):
            t_lx = None
            for dst, src in ((qlnw_c, qlnw_i), (qlnb_c, qlnb_i), (klnw_c, klnw_i),
                             (klnb_c, klnb_i), (iota16, iota16_i), (alibi_k_sb, alibi_k_i),
                             (acti, active_i), (rbf, rb_i), (priorf, prior_i)):
                t_lx = p.emit("sync", lambda e, d=dst, s=src: e.dma_start(out=d[...], in_=s[...]), "lx", 16)
            for dst, src in ((eye_b, eye_i), (emat_b, emat_i), (msk_sb, masks_i)):
                p.emit("gpsimd", lambda e, d=dst, s=src: e.dma_start(out=d[...], in_=s[...]), "lw", 16)
            for tile, v in ((ones_b, 1.0), (ones_f, 1.0), (ones_col_b, 1.0),
                            (eps5, LN_EPS), (eps12, 1e-12), (cnt16, 0.0)):
                t_ms = p.emit("vector", lambda e, t=tile, v=v: e.memset(t[...], v), "dve")
            p.st["consts_done"] = t_ms
            def _preload(e):
                from concourse.hw_specs import get_activation_tables
                tables = get_activation_tables(nc.m.arch)
                idx = list(tables.keys()).index("natural_log_exp_and_others")
                return e.add_instruction(mybir.InstLoadActFuncSet(
                    name=nc.get_next_instruction_name(), act_func_set_id=idx,
                    ins=[], outs=[]))
            p.emit("scalar", _preload)
            p.wait("vector", "lx", t_lx)
            p.emit("vector", lambda e: e.tensor_scalar(qlnw_c[...], qlnw_c[...], SCALE, None, ALU.mult), "dve")
            p.emit("vector", lambda e: e.tensor_scalar(qlnb_c[...], qlnb_c[...], SCALE, None, ALU.mult), "dve")
            t3 = p.emit("vector", lambda e: e.tensor_copy(actf[...], acti[...]), "dve")
            tj = t3
            for j in range(4):
                p.wait("vector", "dve", tj)
                ta = p.emit("vector", lambda e, j=j: e.tensor_scalar(
                    tmp16a[...], iota16[...], actf[0:1, j:j + 1], None, ALU.is_equal), "dve")
                p.wait("vector", "dve", ta)
                tj = p.emit("vector", lambda e: e.tensor_tensor(cnt16[...], cnt16[...], tmp16a[...], ALU.add), "dve")
            p.wait("scalar", "lx", t_lx)
            p.wait("scalar", "dve", p.st["consts_done"])
            t4 = p.emit("scalar", lambda e: e.activation(tmp16b[...], priorf[...], AF.Ln, bias=eps12[0:1, 0:1]), "act")
            p.wait("vector", "act", t4)
            t5 = p.emit("vector", lambda e: e.tensor_tensor(tmp16b[...], tmp16b[...], rbf[...], ALU.add), "dve")
            p.wait("vector", "dve", t5)
            p.emit("vector", lambda e: e.tensor_copy(c16_b[...], tmp16b[...]), "dve")
            p.wait("tensor", "dve", tj)
            t7 = p.emit("tensor", lambda e: e.matmul(ps[...], ones_f[0:1, :], cnt16[...], start=True, stop=True), "pe")
            p.wait("scalar", "pe", t7)
            p.st["setup_done"] = p.emit("scalar", lambda e: e.activation(cnt_bc[...], ps[...], AF.Copy), "act")
            p.wait("gpsimd", "lw", p.tag("lw"))
            p.wait("sync", "lx", p.tag("lx"))
        return fn

    def ph_ln(p, bufs, src, src_f32, affine, out_sb, out_dram):
        (xc, xb, xq, rows_m, rows_r, rows_n, tmpn, tmp2, ps_sx, ps_sq, ps_b1, ps_b2) = bufs
        NS = 4
        for s in range(NS):
            cols = slice(s * 512, (s + 1) * 512)
            t_mm = [0] * NC
            for c in range(NC):
                cp = c % 2
                prev_mm = t_mm[c - 2] if c >= 2 else p.st.get("ln_mm_prev", 0)
                p.wait("sync", "pe", prev_mm)
                p.wait("sync", "dve", p.st.get(f"ln_xcuse{cp}", 0))
                t_ld = p.emit("sync", lambda e, c=c, cp=cp, cols=cols: e.dma_start(
                    out=xc[cp][...], in_=src[c * 128:(c + 1) * 128, cols]), f"d{cp}", 16)
                if src_f32:
                    p.wait("vector", f"d{cp}", t_ld)
                    p.wait("vector", "pe", prev_mm)
                    t_xb = p.emit("vector", lambda e, cp=cp: e.tensor_copy(xb[cp][...], xc[cp][...]), "dve")
                    p.st[f"ln_xcuse{cp}"] = t_xb
                else:
                    t_xb = 0
                p.wait("scalar", f"d{cp}", t_ld)
                p.wait("scalar", "pe", prev_mm)
                if src_f32:
                    p.wait("scalar", "dve", t_xb)
                t_xq = p.emit("scalar", lambda e, cp=cp: e.activation(
                    xq[cp][...], (xb[cp] if src_f32 else xc[cp])[...], AF.Square), "act")
                if src_f32:
                    p.wait("tensor", "dve", t_xb)
                p.wait("tensor", "act", t_xq)
                p.emit("tensor", lambda e, c=c, cp=cp: e.matmul(
                    ps_sx[0:1, :], ones_col_b[...], (xb[cp] if src_f32 else xc[cp])[...],
                    start=(c == 0), stop=(c == NC - 1)), None)
                t_mm[c] = p.emit("tensor", lambda e, c=c, cp=cp: e.matmul(
                    ps_sq[0:1, :], ones_col_b[...], xq[cp][...],
                    start=(c == 0), stop=(c == NC - 1)), "pe")
            p.st["ln_mm_prev"] = t_mm[NC - 1]
            m = rows_m[0:1, cols]
            rstd = rows_r[0:1, cols]
            nmr = rows_n[0:1, cols]
            p.wait("vector", "pe", t_mm[NC - 1])
            ta = p.emit("vector", lambda e, m=m: e.tensor_scalar(m, ps_sx[0:1, :], 1.0 / D, None, ALU.mult), "dve")
            tb = p.emit("vector", lambda e, nmr=nmr: e.tensor_scalar(nmr, ps_sq[0:1, :], 1.0 / D, None, ALU.mult), "dve")
            p.wait("vector", "dve", ta)
            tc = p.emit("vector", lambda e, m=m, rstd=rstd: e.tensor_tensor(rstd, m, m, ALU.mult), "dve")
            p.wait("vector", "dve", tc)
            td = p.emit("vector", lambda e, rstd=rstd, nmr=nmr: e.tensor_tensor(rstd, nmr, rstd, ALU.subtract), "dve")
            p.wait("scalar", "dve", td)
            te = p.emit("scalar", lambda e, rstd=rstd: e.activation(rstd, rstd, AF.Ln, bias=eps5[0:1, 0:1]), "act")
            p.wait("scalar", "act", te)
            tf = p.emit("scalar", lambda e, rstd=rstd: e.activation(rstd, rstd, AF.Exp, scale=-0.5), "act")
            p.wait("vector", "act", tf)
            tg = p.emit("vector", lambda e, m=m, rstd=rstd, nmr=nmr: e.scalar_tensor_tensor(
                nmr, m, -1.0, rstd, ALU.mult, ALU.mult), "dve")
            p.st[f"ln_rows{s}"] = (tf, tg)
            if DEBUG_LN:
                p.wait("scalar", "dve", tg)
                p.emit("scalar", lambda e, cols=cols: e.dma_start(
                    out=r3[0:1, cols], in_=rows_m[0:1, cols]), "st", 16)
                p.emit("scalar", lambda e, cols=cols: e.dma_start(
                    out=r3[1:2, cols], in_=rows_r[0:1, cols]), "st", 16)
                p.emit("scalar", lambda e, cols=cols: e.dma_start(
                    out=r3[2:3, cols], in_=rows_n[0:1, cols]), "st", 16)
        for s in range(NS):
            cols = slice(s * 512, (s + 1) * 512)
            tf, tg = p.st[f"ln_rows{s}"]
            p.wait("tensor", "act", tf)
            p.wait("tensor", "dve", tg)
            p.wait("tensor", "dve", p.st.get("ln_bc_free", 0))
            p.emit("tensor", lambda e, cols=cols: e.matmul(
                ps_b1[...], ones_f[0:1, :], rows_r[0:1, cols], start=True, stop=True), None)
            t_bc = p.emit("tensor", lambda e, cols=cols: e.matmul(
                ps_b2[...], ones_f[0:1, :], rows_n[0:1, cols], start=True, stop=True), "pe")
            t_t1 = [0] * NC
            t_t2 = [0] * NC
            for c in range(NC):
                cp = c % 2
                p.wait("sync", "pe", p.st.get("ln_mm_prev", 0))
                p.wait("sync", "dve", max(t_t1[c - 2] if c >= 2 else 0,
                                          p.st.get(f"ln_xcuse{cp}", 0)))
                t_ld = p.emit("sync", lambda e, c=c, cp=cp, cols=cols: e.dma_start(
                    out=xc[cp][...], in_=src[c * 128:(c + 1) * 128, cols]), f"d{cp}", 16)
                p.wait("vector", f"d{cp}", t_ld)
                p.wait("vector", "pe", t_bc)
                p.wait("vector", "dve", t_t2[c - 2] if c >= 2 else 0)
                t_t1[c] = p.emit("vector", lambda e, cp=cp: e.tensor_tensor(
                    tmpn[cp][...], xc[cp][...], ps_b1[...], ALU.mult), "dve")
                p.st[f"ln_xcuse{cp}"] = t_t1[c]
                p.wait("vector", "dve", t_t1[c])
                if affine is None:
                    t_t2[c] = p.emit("vector", lambda e, cp=cp, c=c, cols=cols: e.tensor_tensor(
                        out_sb[:, c, cols], tmpn[cp][...], ps_b2[...], ALU.add), "dve")
                else:
                    t2a = p.emit("vector", lambda e, cp=cp: e.tensor_tensor(
                        tmp2[cp][...], tmpn[cp][...], ps_b2[...], ALU.add), "dve")
                    p.wait("vector", "dve", t2a)
                    p.wait("vector", f"d{2 + cp}", p.st.get(f"ln_st{cp}", 0))
                    w_col, b_col = affine
                    t_t2[c] = p.emit("vector", lambda e, cp=cp, c=c: e.tensor_scalar(
                        xq[cp][...], tmp2[cp][...],
                        w_col[:, c:c + 1], b_col[:, c:c + 1], ALU.mult, ALU.add), "dve")
                if out_dram is not None:
                    p.wait("scalar", "dve", t_t2[c])
                    tl = p.emit("scalar", lambda e, c=c, cols=cols, cp=cp: e.dma_start(
                        out=out_dram[c * 128:(c + 1) * 128, cols], in_=xq[cp][...]), f"d{2 + cp}", 16)
                    p.st[f"ln_st{cp}"] = tl
            p.st["ln_bc_free"] = t_t2[NC - 1]

    def make_ln(src_name, src, tag, src_f32=True, affine=None, out_sb=None, out_dram=None):
        def run():
            with ExitStack() as es:
                xdt = F32 if src_f32 else BF16
                xc = [es.enter_context(nc.sbuf_tensor(f"xc{tag}{i}", [128, 512], xdt)) for i in range(2)]
                xb = [es.enter_context(nc.sbuf_tensor(f"xbb{tag}{i}", [128, 512], BF16)) for i in range(2)]
                xq = [es.enter_context(nc.sbuf_tensor(f"xqq{tag}{i}", [128, 512], BF16)) for i in range(2)]
                rows_m = es.enter_context(nc.sbuf_tensor(f"rm{tag}", [1, T], F32))
                rows_r = es.enter_context(nc.sbuf_tensor(f"rr{tag}", [1, T], F32))
                rows_n = es.enter_context(nc.sbuf_tensor(f"rn{tag}", [1, T], F32))
                tmpn = [es.enter_context(nc.sbuf_tensor(f"tn{tag}{i}", [128, 512], F32)) for i in range(2)]
                tmp2 = [es.enter_context(nc.sbuf_tensor(f"t2{tag}{i}", [128, 512], F32)) for i in range(2)]
                ps_sx = es.enter_context(nc.psum_tensor(f"px{tag}", [1, 512], F32))
                ps_sq = es.enter_context(nc.psum_tensor(f"pq{tag}", [1, 512], F32))
                ps_b1 = es.enter_context(nc.psum_tensor(f"p1{tag}", [128, 512], F32))
                ps_b2 = es.enter_context(nc.psum_tensor(f"p2{tag}", [128, 512], F32))
                bufs = (xc, xb, xq, rows_m, rows_r, rows_n, tmpn, tmp2, ps_sx, ps_sq, ps_b1, ps_b2)
                pb.phase(lambda p: ph_ln(p, bufs, src, src_f32, affine, out_sb, out_dram), tag)
        return run

    def ph_proj(p, wbufs, ps_t, wsrc, ob0, nob, bias_row, consume):
        t_mm = [0] * nob
        t_cons = [None] * nob
        for ob in range(nob):
            par = ob % 2
            if ob >= 2:
                p.wait("gpsimd", "pe", t_mm[ob - 2])
            t_w = p.emit("gpsimd", lambda e, ob=ob, par=par: e.dma_start(
                out=wbufs[par][...], in_=wsrc[ob0 + ob]), f"d{4 + par}", 16)
            p.wait("tensor", f"d{4 + par}", t_w)
            if ob >= 2:
                c2 = t_cons[ob - 2]
                p.wait("tensor", "dve", c2.get("dve", 0))
            for c in range(NC):
                for tt in range(NTT):
                    stop = (bias_row is None and c == NC - 1)
                    last = (bias_row is None and c == NC - 1 and tt == NTT - 1)
                    t_mm[ob] = p.emit("tensor", lambda e, c=c, tt=tt, par=par, stop=stop: e.matmul(
                        ps_t[par][tt][...], wbufs[par][:, c * 128:(c + 1) * 128],
                        big_sb[:, c, tt * 512:(tt + 1) * 512],
                        start=(c == 0), stop=stop), "pe" if last else None) or t_mm[ob]
            if bias_row is not None:
                for tt in range(NTT):
                    t_mm[ob] = p.emit("tensor", lambda e, tt=tt, par=par, ob=ob: e.matmul(
                        ps_t[par][tt][...], bias_row[0:1, ob * 128:(ob + 1) * 128],
                        ones_b[0:1, 0:512], start=False, stop=True),
                        "pe" if tt == NTT - 1 else None) or t_mm[ob]
            t_cons[ob] = consume(p, ob, par, t_mm[ob])

    def rope_consume(tag, ropc, rops, tmpr, rot, t_rope_ld, out_dram, ps_t):
        def consume(p, ob, par, t_mm):
            tl_act = None
            tl_dve = None
            for tt in range(NTT):
                kp = tt % 2
                ps = ps_t[par][tt]
                cols = slice(tt * 512, (tt + 1) * 512)
                p.wait("vector", "pe", t_mm)
                p.wait("vector", "d7", t_rope_ld)
                p.wait("vector", f"d{8 + kp}", p.st.get(f"{tag}_rst{kp}", 0))
                p.wait("vector", "dve", p.st.get(f"{tag}_ropeprev", 0))
                p.emit("vector", lambda e, ps=ps, cols=cols: e.tensor_tensor(
                    tmpr[0][...], ps[0:64, :], ropc[:, cols], ALU.mult), "dve")
                tb = p.emit("vector", lambda e, ps=ps, cols=cols: e.tensor_tensor(
                    tmpr[1][...], ps[64:128, :], rops[:, cols], ALU.mult), "dve")
                p.emit("vector", lambda e, ps=ps, cols=cols: e.tensor_tensor(
                    tmpr[2][...], ps[0:64, :], rops[:, cols], ALU.mult), "dve")
                td = p.emit("vector", lambda e, ps=ps, cols=cols: e.tensor_tensor(
                    tmpr[3][...], ps[64:128, :], ropc[:, cols], ALU.mult), "dve")
                p.wait("vector", "dve", tb)
                p.emit("vector", lambda e, kp=kp: e.tensor_tensor(
                    rot[kp][0:64, :], tmpr[0][...], tmpr[1][...], ALU.subtract), "dve")
                p.wait("vector", "dve", td)
                tf = p.emit("vector", lambda e, kp=kp: e.tensor_tensor(
                    rot[kp][64:128, :], tmpr[2][...], tmpr[3][...], ALU.add), "dve")
                p.st[f"{tag}_ropeprev"] = tf
                tl_dve = tf
                p.wait("scalar", "dve", tf)
                tl_act = p.emit("scalar", lambda e, ob=ob, cols=cols, kp=kp: e.dma_start(
                    out=out_dram[ob * 128:(ob + 1) * 128, cols], in_=rot[kp][...]), f"d{8 + kp}", 16)
                p.st[f"{tag}_rst{kp}"] = tl_act
            return {"dve": tl_dve, "act": tl_act}
        return consume

    def clip_consume(tag, kt, out_dram, ps_t, do_clip):
        def consume(p, ob, par, t_mm):
            tl_act = None
            tl_dve = None
            for tt in range(NTT):
                kp = tt % 2
                cols = slice(tt * 512, (tt + 1) * 512)
                p.wait("vector", "pe", t_mm)
                p.wait("vector", f"d{8 + kp}", p.st.get(f"{tag}_cst{kp}", 0))
                if do_clip:
                    ta = p.emit("vector", lambda e, kp=kp, par=par, tt=tt: e.tensor_scalar(
                        kt[kp][...], ps_t[par][tt][...], CLIP, -CLIP, ALU.min, ALU.max), "dve")
                else:
                    ta = p.emit("vector", lambda e, kp=kp, par=par, tt=tt: e.tensor_copy(
                        kt[kp][...], ps_t[par][tt][...]), "dve")
                tl_dve = ta
                p.wait("scalar", "dve", ta)
                tl_act = p.emit("scalar", lambda e, ob=ob, cols=cols, kp=kp: e.dma_start(
                    out=out_dram[ob * 128:(ob + 1) * 128, cols], in_=kt[kp][...]), f"d{8 + kp}", 16)
                p.st[f"{tag}_cst{kp}"] = tl_act
            return {"dve": tl_dve, "act": tl_act}
        return consume

    def resid_consume(tag, rt, ot, resid_in, resid_out, ps_t):
        def consume(p, ob, par, t_mm):
            tl_act = None
            tl_dve = None
            for tt in range(NTT):
                kp = tt % 2
                cols = slice(tt * 512, (tt + 1) * 512)
                p.wait("sync", "dve", p.st.get(f"{tag}_rr{kp}", 0))
                t_r = p.emit("sync", lambda e, ob=ob, cols=cols, kp=kp: e.dma_start(
                    out=rt[kp][...], in_=resid_in[ob * 128:(ob + 1) * 128, cols]), f"d{kp}", 16)
                p.wait("vector", "pe", t_mm)
                p.wait("vector", f"d{kp}", t_r)
                p.wait("vector", f"d{8 + kp}", p.st.get(f"{tag}_ro{kp}", 0))
                ta = p.emit("vector", lambda e, kp=kp, par=par, tt=tt: e.tensor_tensor(
                    ot[kp][...], ps_t[par][tt][...], rt[kp][...], ALU.add), "dve")
                p.st[f"{tag}_rr{kp}"] = ta
                tl_dve = ta
                p.wait("scalar", "dve", ta)
                tl_act = p.emit("scalar", lambda e, ob=ob, cols=cols, kp=kp: e.dma_start(
                    out=resid_out[ob * 128:(ob + 1) * 128, cols], in_=ot[kp][...]), f"d{8 + kp}", 16)
                p.st[f"{tag}_ro{kp}"] = tl_act
            return {"dve": tl_dve, "act": tl_act}
        return consume

    def make_proj(tag, wsrc, ob0, nob, kind, **kw):
        def run():
            with ExitStack() as es:
                wbufs = [es.enter_context(nc.sbuf_tensor(f"wb{tag}{i}", [128, 4096], BF16)) for i in range(2)]
                ps_t = [[es.enter_context(nc.psum_tensor(f"ps{tag}{i}_{t}", [128, 512], F32)) for t in range(NTT)]
                        for i in range(2)]
                bias_i = kw.get("bias_i")
                bias_row = None
                if bias_i is not None:
                    bias_row = es.enter_context(nc.sbuf_tensor(f"br{tag}", [1, kw["bias_n"]], BF16))
                if kind == "rope":
                    ropc = es.enter_context(nc.sbuf_tensor(f"rc{tag}", [64, T], F32))
                    rops = es.enter_context(nc.sbuf_tensor(f"rs{tag}", [64, T], F32))
                    tmpr = [es.enter_context(nc.sbuf_tensor(f"tr{tag}{i}", [64, 512], F32)) for i in range(4)]
                    rot = [es.enter_context(nc.sbuf_tensor(f"ro{tag}{i}", [128, 512], BF16)) for i in range(2)]
                elif kind == "clip":
                    kt = [es.enter_context(nc.sbuf_tensor(f"kt{tag}{i}", [128, 512], BF16)) for i in range(2)]
                else:
                    rt = [es.enter_context(nc.sbuf_tensor(f"rt{tag}{i}", [128, 512], F32)) for i in range(2)]
                    ot = [es.enter_context(nc.sbuf_tensor(f"ot{tag}{i}", [128, 512], F32)) for i in range(2)]

                def fn(p):
                    if bias_i is not None:
                        t_b = p.emit("gpsimd", lambda e: e.dma_start(out=bias_row[...], in_=bias_i[...]), "d6", 16)
                        p.wait("tensor", "d6", t_b)
                    if kind == "rope":
                        p.emit("sync", lambda e: e.dma_start(out=ropc[...], in_=kw["cos_i"][...]), "d7", 16)
                        t_rl = p.emit("sync", lambda e: e.dma_start(out=rops[...], in_=kw["sin_i"][...]), "d7", 16)
                        consume = rope_consume(tag, ropc, rops, tmpr, rot, t_rl, kw["out_dram"], ps_t)
                    elif kind == "clip":
                        consume = clip_consume(tag, kt, kw["out_dram"], ps_t, kw.get("do_clip", True))
                    else:
                        consume = resid_consume(tag, rt, ot, kw["resid_in"], kw["resid_out"], ps_t)
                    ph_proj(p, wbufs, ps_t, wsrc, ob0, nob, bias_row, consume)

                pb.phase(fn, tag)
        return run

    def make_vtr(nheads, v_in, v_stage_out, ccs, tag):
        def run():
            with ExitStack() as es:
                vin = [es.enter_context(nc.sbuf_tensor(f"vi{tag}{i}", [128, T], BF16)) for i in range(2)]
                vot = [es.enter_context(nc.sbuf_tensor(f"vo{tag}{i}", [128, 128], BF16)) for i in range(2)]
                ps_tr = [es.enter_context(nc.psum_tensor(f"pt{tag}{i}", [128, 128], BF16)) for i in range(2)]

                def fn(p):
                    for h in range(nheads):
                        hp = h % 2
                        p.wait("sync", "pe", p.st.get(f"{tag}_pe{hp}", 0))
                        t_ld = p.emit("sync", lambda e, h=h, hp=hp: e.dma_start(
                            out=vin[hp][...], in_=v_in[h * 128:(h + 1) * 128, :]), f"d{hp}", 16)
                        for kt_ in range(16):
                            kp = kt_ % 2
                            p.wait("tensor", f"d{hp}", t_ld)
                            p.wait("tensor", "dve", p.st.get(f"{tag}_cp{kp}", 0))
                            t_tr = p.emit("tensor", lambda e, kt_=kt_, hp=hp, kp=kp: e.transpose(
                                ps_tr[kp][...], vin[hp][:, kt_ * 128:(kt_ + 1) * 128], eye_b[...]), "pe")
                            p.wait("vector", "pe", t_tr)
                            p.wait("vector", f"d{8 + kp}", p.st.get(f"{tag}_st{kp}", 0))
                            t_cp = p.emit("vector", lambda e, kp=kp: e.tensor_copy(vot[kp][...], ps_tr[kp][...]), "dve")
                            p.st[f"{tag}_cp{kp}"] = t_cp
                            p.wait("scalar", "dve", t_cp)
                            t_st = p.emit("scalar", lambda e, h=h, kt_=kt_, kp=kp: e.dma_start(
                                out=v_stage_out[(h * 16 + kt_) * 128:(h * 16 + kt_ + 1) * 128, :],
                                in_=vot[kp][...]), f"d{8 + kp}", 16)
                            p.st[f"{tag}_st{kp}"] = t_st
                        p.st[f"{tag}_pe{hp}"] = p.tag("pe")
                    for s_ in ("st", "d2", "d3", "d8", "d9"):
                        p.wait("gpsimd", s_, p.tag(s_))
                    for cin, cout in ccs:
                        p.emit("gpsimd", lambda e, cin=cin, cout=cout: e.collective_compute(
                            "AllGather", ALU.bypass, replica_groups=RG,
                            ins=[cin[...]], outs=[cout[...]]), "cc")

                pb.phase(fn, tag)
        return run

    def make_attn(variant):
        n_outer = NG if variant == "a" else NH
        heads_per = 4 if variant == "a" else 1
        k_out = cka_out if variant == "a" else ckb_out
        v_out = cva_out if variant == "a" else cvb_out
        krows = 1024 if variant == "a" else 4096
        mcol0 = 128 if variant == "a" else 160
        DEPTH = 4
        LAG = 3

        def run():
            with ExitStack() as es:
                K_sb = [es.enter_context(nc.sbuf_tensor(f"Kx{variant}{i}", [128, 4096], BF16)) for i in range(2)]
                V_sb = [es.enter_context(nc.sbuf_tensor(f"Vx{variant}{i}", [128, 4096], BF16)) for i in range(2)]
                Q_sb = [es.enter_context(nc.sbuf_tensor(f"Qx{variant}{i}", [128, T], BF16)) for i in range(2)]
                aqb = [es.enter_context(nc.sbuf_tensor(f"aqb{variant}{i}", [128, T], BF16)) for i in range(2)]
                probs = [es.enter_context(nc.sbuf_tensor(f"pr{variant}{i}", [128, 512], BF16)) for i in range(DEPTH)]
                biasH = es.enter_context(nc.sbuf_tensor(f"bH{variant}", [128, 1], F32))
                bias32 = es.enter_context(nc.sbuf_tensor(f"b32{variant}", [128, 32], F32))
                rl = es.enter_context(nc.sbuf_tensor(f"rl{variant}", [1, 512], F32))
                rbx = es.enter_context(nc.sbuf_tensor(f"rb{variant}", [128, 512], F32))
                ps_r = [es.enter_context(nc.psum_tensor(f"psr{variant}{i}", [128, 512], F32)) for i in range(DEPTH)]
                ps_ctx = [es.enter_context(nc.psum_tensor(f"px{variant}{i}", [128, 512], F32)) for i in range(2)]
                ps_l = [es.enter_context(nc.psum_tensor(f"pl{variant}{i}", [1, 512], F32)) for i in range(2)]

                def emit_kv_load(p, go):
                    par = go % 2
                    p.wait("sync", "pe", p.st.get(f"kv_pe{par}", 0))
                    p.wait("sync", "act", p.st.get(f"kv_act{par}", 0))
                    tags = []
                    for r in range(2):
                        tags.append(("lx" if r == 0 else "d0", p.emit(
                            "sync", lambda e, go=go, r=r, par=par: e.dma_start(
                                out=K_sb[par][:, r * 2048:(r + 1) * 2048],
                                in_=k_out[r * krows + go * 128: r * krows + go * 128 + 128, :]),
                            "lx" if r == 0 else "d0", 16)))
                        tags.append(("d1" if r == 0 else "d3", p.emit(
                            "sync", lambda e, go=go, r=r, par=par: e.dma_start(
                                out=V_sb[par].ap().rearrange("p (kt c) -> p kt c", c=128)[:, r * 16:(r + 1) * 16, :],
                                in_=v_out[(r * n_outer + go) * 2048:(r * n_outer + go + 1) * 2048, :]
                                .rearrange("(kt p) c -> p kt c", p=128)),
                            "d1" if r == 0 else "d3", 16)))
                    p.st[f"kvtags{par}"] = tags

                def emit_q_load(p, h):
                    hp = h % 2
                    p.wait("sync", "pe", p.st.get(f"q_pe{hp}", 0))
                    p.wait("sync", "act", p.st.get(f"q_act{hp}", 0))
                    t_q = p.emit("sync", lambda e, h=h, hp=hp: e.dma_start(
                        out=Q_sb[hp][...], in_=q_buf[h * 128:(h + 1) * 128, :]), "d7", 16)
                    p.st[f"qtag{hp}"] = t_q
                    if variant == "b":
                        p.wait("gpsimd", "dve", p.st.get(f"aq_dve{hp}", 0))
                        t_aq = p.emit("gpsimd", lambda e, h=h, hp=hp: e.dma_start(
                            out=aqb[hp][...],
                            in_=alibi_qb_i[h:h + 1, :].rearrange("u (p t) -> (u p) t", p=128)), "d6", 16)
                        p.st[f"aqtag{hp}"] = t_aq

                def fn(p):
                    p.wait("sync", "cc", p.tag("cc"))
                    emit_kv_load(p, 0)
                    emit_q_load(p, 0)
                    for go in range(n_outer):
                        par = go % 2
                        if go + 1 < n_outer:
                            emit_kv_load(p, go + 1)
                        kv_tags = p.st[f"kvtags{par}"]
                        for rh in range(heads_per):
                            h = go * heads_per + rh
                            hp = h % 2
                            if h + 1 < n_outer * heads_per:
                                emit_q_load(p, h + 1)
                            for s_, t_ in kv_tags:
                                p.wait("tensor", s_, t_)
                            p.wait("tensor", "d7", p.st[f"qtag{hp}"])
                            if variant == "b":
                                p.wait("vector", "d6", p.st[f"aqtag{hp}"])
                            cb0 = p.st.get("sweep_ctr", 0) % 2
                            p.wait("tensor", "dve", p.st.get(f"negM_{variant}", 0))
                            p.wait("tensor", "dve", p.st.get(f"ctx_free{cb0}", 0))
                            tq5 = p.emit("tensor", lambda e, h=h, cb0=cb0: e.matmul(
                                ps_ctx[cb0][:, 0:1], ones_f[0:1, :],
                                nrm[0:1, mcol0 + h:mcol0 + h + 1], start=True, stop=True), "pe")
                            p.wait("scalar", "pe", tq5)
                            p.wait("scalar", "act", p.st.get("bias_used", 0))
                            t_bH = p.emit("scalar", lambda e, cb0=cb0: e.activation(biasH[...], ps_ctx[cb0][:, 0:1], AF.Copy), "act")
                            p.st["biasH_evac"] = t_bH
                            if variant == "b":
                                p.wait("vector", "act", t_bH)
                                p.wait("vector", "act", p.st.get("b32_used", 0))
                                t_b32 = p.emit("vector", lambda e, h=h: e.tensor_scalar(
                                    bias32[...], alibi_k_sb[:, h * 32:(h + 1) * 32],
                                    biasH[:, 0:1], None, ALU.add), "dve")
                                bias_ready = ("dve", t_b32)
                            else:
                                bias_ready = ("act", t_bH)

                            for own_i in range(NBLK):
                                cb = p.st.get("sweep_ctr", 0) % 2
                                p.st["sweep_ctr"] = p.st.get("sweep_ctr", 0) + 1
                                qcols = slice(own_i * 512, (own_i + 1) * 512)
                                visits = []
                                for j in range(own_i + 1):
                                    for r_k in range(2):
                                        for dd in range(4):
                                            visits.append((j, r_k, dd))
                                nv = len(visits)
                                t_sc = [0] * nv
                                t_exp = [0] * nv
                                t_msk = [0] * nv
                                t_pv = [0] * nv

                                def emit_scores(p, v):
                                    j, r_k, dd = visits[v]
                                    sl_ = v % DEPTH
                                    kcol = r_k * 2048 + (j * 4 + dd) * 128
                                    if v >= DEPTH:
                                        p.wait("tensor", "act", t_exp[v - DEPTH])
                                        if t_msk[v - DEPTH]:
                                            p.wait("tensor", "dve", t_msk[v - DEPTH])
                                    else:
                                        p.wait("tensor", "act", p.st.get(f"ringfree{sl_}", 0))
                                        p.wait("tensor", "dve", p.st.get(f"ringfree_d{sl_}", 0))
                                    t_sc[v] = p.emit("tensor", lambda e, sl_=sl_, kcol=kcol, qcols=qcols, par=par, hp=hp: e.matmul(
                                        ps_r[sl_][...], K_sb[par][:, kcol:kcol + 128], Q_sb[hp][:, qcols],
                                        start=True, stop=True), "pe")
                                    if variant == "b":
                                        p.wait("vector", "pe", t_sc[v])
                                        t_row = p.emit("vector", lambda e, sl_=sl_, qcols=qcols, hp=hp: e.tensor_tensor(
                                            ps_r[sl_][...], ps_r[sl_][...], aqb[hp][:, qcols], ALU.add), "dve")
                                        t_msk[v] = t_row
                                        p.st[f"aq_dve{hp}"] = t_row
                                    if j == own_i:
                                        p.wait("vector", "pe", t_sc[v])
                                        if t_msk[v]:
                                            p.wait("vector", "dve", t_msk[v])
                                        t_msk[v] = p.emit("vector", lambda e, sl_=sl_, r_k=r_k, dd=dd: e.tensor_tensor(
                                            ps_r[sl_][...], ps_r[sl_][...],
                                            msk_sb[:, (r_k * 4 + dd) * 512:(r_k * 4 + dd + 1) * 512],
                                            ALU.add), "dve")
                                    p.wait("scalar", "pe", t_sc[v])
                                    if t_msk[v]:
                                        p.wait("scalar", "dve", t_msk[v])
                                    if v >= DEPTH:
                                        p.wait("scalar", "pe", t_pv[v - DEPTH] if t_pv[v - DEPTH] else 0)
                                    p.wait("scalar", bias_ready[0], bias_ready[1])
                                    if variant == "b":
                                        ki = (j * 2 + r_k) * 4 + dd
                                        t_exp[v] = p.emit("scalar", lambda e, sl_=sl_, ki=ki: e.activation(
                                            probs[sl_][...], ps_r[sl_][...], AF.Exp,
                                            bias=bias32[:, ki:ki + 1]), "act")
                                    else:
                                        t_exp[v] = p.emit("scalar", lambda e, sl_=sl_: e.activation(
                                            probs[sl_][...], ps_r[sl_][...], AF.Exp,
                                            bias=biasH[:, 0:1]), "act")

                                def emit_pv(p, v):
                                    j, r_k, dd = visits[v]
                                    sl_ = v % DEPTH
                                    kidx = r_k * 16 + j * 4 + dd
                                    p.wait("tensor", "act", t_exp[v])
                                    if v == 0:
                                        p.wait("tensor", "dve", p.st.get(f"ctx_free{cb}", 0))
                                        p.wait("tensor", "dve", p.st.get(f"ps_l_free{cb}", 0))
                                        if cb == 0:
                                            p.wait("tensor", "act", p.st.get("biasH_evac", 0))
                                    p.emit("tensor", lambda e, sl_=sl_, kidx=kidx, v=v, nv=nv, par=par, cb=cb: e.matmul(
                                        ps_ctx[cb][...], V_sb[par][:, kidx * 128:(kidx + 1) * 128], probs[sl_][...],
                                        start=(v == 0), stop=(v == nv - 1)), None)
                                    t_pv[v] = p.emit("tensor", lambda e, sl_=sl_, v=v, nv=nv, cb=cb: e.matmul(
                                        ps_l[cb][0:1, :], ones_col_b[...], probs[sl_][...],
                                        start=(v == 0), stop=(v == nv - 1)), "pe")

                                for v in range(nv + LAG):
                                    if v < nv:
                                        emit_scores(p, v)
                                    if v - LAG >= 0:
                                        emit_pv(p, v - LAG)
                                rlc = rl.ap()[0:1, cb * 256:cb * 256 + 256] if False else rl
                                p.wait("vector", "pe", t_pv[nv - 1])
                                p.wait("vector", "act", p.st.get("rb_used2", 0))
                                tr1 = p.emit("vector", lambda e, cb=cb: e.reciprocal(rl[...], ps_l[cb][0:1, :]), "dve")
                                p.st[f"ps_l_free{cb}"] = tr1
                                rslot = nv % DEPTH
                                p.wait("tensor", "dve", tr1)
                                p.wait("tensor", "act", t_exp[nv - 1])
                                p.wait("tensor", "act", t_exp[nv - DEPTH] if nv >= DEPTH else p.st.get(f"ringfree{rslot}", 0))
                                p.wait("tensor", "dve", p.st.get(f"ringfree_d{rslot}", 0) if nv < DEPTH else (t_msk[nv - DEPTH] or 0))
                                t_rb = p.emit("tensor", lambda e, rslot=rslot: e.matmul(
                                    ps_r[rslot][...], ones_f[0:1, :], rl[0:1, :], start=True, stop=True), "pe")
                                p.wait("scalar", "pe", t_rb)
                                p.wait("scalar", "dve", p.st.get("rbx_used", 0))
                                tr3 = p.emit("scalar", lambda e, rslot=rslot: e.activation(
                                    rbx[...], ps_r[rslot][...], AF.Copy), "act")
                                p.st[f"ringfree{rslot}"] = max(p.st.get(f"ringfree{rslot}", 0), tr3)
                                p.st["rb_used2"] = tr3
                                p.wait("vector", "act", tr3)
                                p.wait("vector", "pe", t_pv[nv - 1])
                                tr4 = p.emit("vector", lambda e, h=h, qcols=qcols, cb=cb: e.tensor_tensor(
                                    big_sb[:, h, qcols], ps_ctx[cb][...], rbx[...], ALU.mult), "dve")
                                p.st[f"ctx_free{cb}"] = tr4
                                p.st["rbx_used"] = tr4
                                for sl_ in range(DEPTH):
                                    if sl_ != rslot:
                                        p.st[f"ringfree{sl_}"] = max(p.st.get(f"ringfree{sl_}", 0),
                                                                     t_exp[nv - DEPTH + ((sl_ - nv) % DEPTH)] if nv - DEPTH + ((sl_ - nv) % DEPTH) >= 0 else 0)
                                    tm_i = nv - DEPTH + ((sl_ - nv) % DEPTH)
                                    if 0 <= tm_i < nv and t_msk[tm_i]:
                                        p.st[f"ringfree_d{sl_}"] = max(p.st.get(f"ringfree_d{sl_}", 0), t_msk[tm_i])
                            p.st["b32_used"] = p.tag("act")
                            p.st["bias_used"] = p.tag("act")
                            p.st[f"q_pe{hp}"] = p.tag("pe")
                            p.st[f"q_act{hp}"] = p.tag("act")
                        p.st[f"kv_pe{par}"] = p.tag("pe")
                        p.st[f"kv_act{par}"] = p.tag("act")

                pb.phase(fn, f"attn_{variant}")
        return run

    def make_ffn_a(tag):
        def run():
            with ExitStack() as es:
                la_sb = es.enter_context(nc.sbuf_tensor(f"la{tag}", [128, NC, 128], BF16))
                rt_sb = es.enter_context(nc.sbuf_tensor(f"rt{tag}", [128, NC, 16], BF16))
                z_sb = es.enter_context(nc.sbuf_tensor(f"zs{tag}", [128, 512], BF16))
                lg_b = es.enter_context(nc.sbuf_tensor(f"lg{tag}", [16, 512], BF16))
                lg_tm = es.enter_context(nc.sbuf_tensor(f"lt{tag}", [128, 64], F32))
                mx4 = es.enter_context(nc.sbuf_tensor(f"mx{tag}", [128, 4], F32))
                et = es.enter_context(nc.sbuf_tensor(f"et{tag}", [128, 64], F32))
                sm4 = es.enter_context(nc.sbuf_tensor(f"sm{tag}", [128, 4], F32))
                al_b = es.enter_context(nc.sbuf_tensor(f"ab{tag}", [128, 64], BF16))
                al_em = es.enter_context(nc.sbuf_tensor(f"ae{tag}", [16, 512], BF16))
                ps_z = es.enter_context(nc.psum_tensor(f"pz{tag}", [128, 512], F32))
                ps_lg = es.enter_context(nc.psum_tensor(f"pg{tag}", [16, 512], F32))
                ps_tp = es.enter_context(nc.psum_tensor(f"pp{tag}", [128, 64], BF16))
                ps_tb = es.enter_context(nc.psum_tensor(f"pb{tag}", [16, 512], BF16))
                ps_ax = es.enter_context(nc.psum_tensor(f"pa{tag}", [128, 512], F32))

                def fn(p):
                    p.emit("gpsimd", lambda e: e.dma_start(
                        out=la_sb.ap().rearrange("p c e -> p (c e)"),
                        in_=lora_a_t.rearrange("p c e -> p (c e)")), "lw", 16)
                    t_la = p.emit("gpsimd", lambda e: e.dma_start(
                        out=rt_sb.ap().rearrange("p c e -> p (c e)"),
                        in_=router_t.rearrange("p c e -> p (c e)")), "lw", 16)
                    for tt in range(NTT):
                        cols = slice(tt * 512, (tt + 1) * 512)
                        p.wait("tensor", "lw", t_la)
                        p.wait("tensor", "act", p.st.get("zev", 0))
                        for c in range(NC):
                            p.emit("tensor", lambda e, c=c, cols=cols: e.matmul(
                                ps_z[...], la_sb[:, c, :], big_sb[:, c, cols],
                                start=(c == 0), stop=(c == NC - 1)),
                                "pe" if c == NC - 1 else None)
                        t_z = p.tag("pe")
                        p.wait("tensor", "act", p.st.get("lgev", 0))
                        for c in range(NC):
                            p.emit("tensor", lambda e, c=c, cols=cols: e.matmul(
                                ps_lg[...], rt_sb[:, c, :], big_sb[:, c, cols],
                                start=(c == 0), stop=False), None)
                        p.wait("tensor", "act", p.st["setup_done"])
                        t_lg = p.emit("tensor", lambda e: e.matmul(
                            ps_lg[...], c16_b[...], ones_b[0:1, 0:512], start=False, stop=True), "pe")
                        p.wait("scalar", "pe", t_z)
                        p.wait("scalar", "dve", p.st.get("zpd", 0))
                        t_zev = p.emit("scalar", lambda e: e.activation(z_sb[...], ps_z[...], AF.Copy), "act")
                        p.st["zev"] = t_zev
                        p.wait("scalar", "pe", t_lg)
                        t_lgev = p.emit("scalar", lambda e: e.activation(lg_b[...], ps_lg[...], AF.Copy), "act")
                        p.st["lgev"] = t_lgev
                        p.wait("tensor", "act", t_lgev)
                        p.wait("tensor", "dve", p.st.get("tpc", 0))
                        t_tp = 0
                        for s4 in range(4):
                            t_tp = p.emit("tensor", lambda e, s4=s4: e.transpose(
                                ps_tp[:, s4 * 16:(s4 + 1) * 16], lg_b[:, s4 * 128:(s4 + 1) * 128],
                                eye_b[0:16, 0:16]), "pe")
                        p.wait("vector", "pe", t_tp)
                        t_c = p.emit("vector", lambda e: e.tensor_copy(lg_tm[...], ps_tp[...]), "dve")
                        p.st["tpc"] = t_c
                        p.wait("vector", "dve", t_c)
                        t_mx = p.emit("vector", lambda e: e.tensor_reduce(
                            mx4[...], lg_tm.ap().rearrange("p (s e) -> p s e", e=16),
                            AX.X, ALU.max, negate=True), "dve")
                        t_ex = [0] * 4
                        for s4 in range(4):
                            p.wait("scalar", "dve", t_mx)
                            t_ex[s4] = p.emit("scalar", lambda e, s4=s4: e.activation(
                                et[:, s4 * 16:(s4 + 1) * 16], lg_tm[:, s4 * 16:(s4 + 1) * 16],
                                AF.Exp, bias=mx4[:, s4:s4 + 1]), "act")
                        t_ct = [0] * 4
                        for s4 in range(4):
                            p.wait("vector", "act", t_ex[s4])
                            t_ct[s4] = p.emit("vector", lambda e, s4=s4: e.tensor_tensor(
                                et[:, s4 * 16:(s4 + 1) * 16], et[:, s4 * 16:(s4 + 1) * 16],
                                cnt_bc[...], ALU.mult), "dve")
                        p.wait("vector", "dve", t_ct[3])
                        t_sm = p.emit("vector", lambda e: e.tensor_reduce(
                            sm4[...], et.ap().rearrange("p (s e) -> p s e", e=16), AX.X, ALU.add), "dve")
                        p.wait("vector", "dve", t_sm)
                        t_rc = p.emit("vector", lambda e: e.reciprocal(sm4[...], sm4[...]), "dve")
                        t_al = [0] * 4
                        for s4 in range(4):
                            p.wait("vector", "dve", t_rc)
                            t_al[s4] = p.emit("vector", lambda e, s4=s4: e.tensor_scalar(
                                al_b[:, s4 * 16:(s4 + 1) * 16], et[:, s4 * 16:(s4 + 1) * 16],
                                sm4[:, s4:s4 + 1], 2.0, ALU.mult, ALU.mult), "dve")
                        p.wait("tensor", "dve", t_al[3])
                        p.wait("tensor", "dve", p.st.get("emc", 0))
                        t_tb = 0
                        for s4 in range(4):
                            t_tb = p.emit("tensor", lambda e, s4=s4: e.transpose(
                                ps_tb[:, s4 * 128:(s4 + 1) * 128], al_b[:, s4 * 16:(s4 + 1) * 16],
                                eye_b[...]), "pe")
                        p.wait("vector", "pe", t_tb)
                        t_em = p.emit("vector", lambda e: e.tensor_copy(al_em[...], ps_tb[...]), "dve")
                        p.st["emc"] = t_em
                        p.wait("tensor", "dve", t_em)
                        p.wait("tensor", "dve", p.st.get("zpd", 0))
                        t_ax = p.emit("tensor", lambda e: e.matmul(
                            ps_ax[...], emat_b[...], al_em[...], start=True, stop=True), "pe")
                        p.wait("vector", "pe", t_ax)
                        p.wait("vector", "act", t_zev)
                        t_zp = p.emit("vector", lambda e, cols=cols: e.tensor_tensor(
                            zp_sb[:, cols], ps_ax[...], z_sb[...], ALU.mult), "dve")
                        p.st["zpd"] = t_zp
                    p.st["zp_done"] = p.st["zpd"]

                pb.phase(fn, tag)
        return run

    def make_ffn_b(resid_in, resid_out, tag):
        def run():
            with ExitStack() as es:
                lb_sb = es.enter_context(nc.sbuf_tensor(f"lb{tag}", [128, 4096], BF16))
                bias_row = es.enter_context(nc.sbuf_tensor(f"bb{tag}", [1, 4096], BF16))
                wbufs = [es.enter_context(nc.sbuf_tensor(f"wb{tag}{i}", [128, 4096], BF16)) for i in range(2)]
                ps_t = [[es.enter_context(nc.psum_tensor(f"ps{tag}{i}_{t}", [128, 512], F32)) for t in range(NTT)]
                        for i in range(2)]
                rt = [es.enter_context(nc.sbuf_tensor(f"rt{tag}{i}", [128, 512], F32)) for i in range(2)]
                ot = [es.enter_context(nc.sbuf_tensor(f"ot{tag}{i}", [128, 512], F32)) for i in range(2)]

                def fn(p):
                    p.emit("gpsimd", lambda e: e.dma_start(out=lb_sb[...], in_=lora_b_f[...]), "d6", 16)
                    t_lb = p.emit("gpsimd", lambda e: e.dma_start(out=bias_row[...], in_=bbase_i[...]), "d6", 16)
                    consume = resid_consume(tag, rt, ot, resid_in, resid_out, ps_t)
                    t_mm = [0] * 32
                    t_cons = [None] * 32
                    for ob in range(32):
                        par = ob % 2
                        if ob >= 2:
                            p.wait("gpsimd", "pe", t_mm[ob - 2])
                        t_w = p.emit("gpsimd", lambda e, ob=ob, par=par: e.dma_start(
                            out=wbufs[par][...], in_=wbase_t[ob]), f"d{4 + par}", 16)
                        p.wait("tensor", f"d{4 + par}", t_w)
                        p.wait("tensor", "d6", t_lb)
                        p.wait("tensor", "dve", p.st["zp_done"])
                        if ob >= 2:
                            c2 = t_cons[ob - 2]
                            p.wait("tensor", "dve", c2.get("dve", 0))
                        for c in range(NC):
                            for tt in range(NTT):
                                p.emit("tensor", lambda e, c=c, tt=tt, par=par: e.matmul(
                                    ps_t[par][tt][...], wbufs[par][:, c * 128:(c + 1) * 128],
                                    big_sb[:, c, tt * 512:(tt + 1) * 512],
                                    start=(c == 0), stop=False), None)
                        for tt in range(NTT):
                            p.emit("tensor", lambda e, tt=tt, par=par, ob=ob: e.matmul(
                                ps_t[par][tt][...], lb_sb[:, ob * 128:(ob + 1) * 128],
                                zp_sb[:, tt * 512:(tt + 1) * 512], start=False, stop=False), None)
                            t_mm[ob] = p.emit("tensor", lambda e, tt=tt, par=par, ob=ob: e.matmul(
                                ps_t[par][tt][...], bias_row[0:1, ob * 128:(ob + 1) * 128],
                                ones_b[0:1, 0:512], start=False, stop=True),
                                "pe" if tt == NTT - 1 else None) or t_mm[ob]
                        t_cons[ob] = consume(p, ob, par, t_mm[ob])

                pb.phase(fn, tag)
        return run

    def make_copy(src_fn, dst, tag, rows=D, dt=F32, width=T):
        def run():
            with ExitStack() as es:
                buf = [es.enter_context(nc.sbuf_tensor(f"cp{tag}{i}", [128, width], dt)) for i in range(2)]
                src = src_fn() if callable(src_fn) else src_fn

                def fn(p):
                    p.wait("sync", "cc", p.tag("cc"))
                    for c in range(rows // 128):
                        cp = c % 2
                        p.wait("sync", f"d{8 + cp}", p.st.get(f"{tag}_s{cp}", 0))
                        t_l = p.emit("sync", lambda e, c=c, cp=cp: e.dma_start(
                            out=buf[cp][...], in_=src[c * 128:(c + 1) * 128, :]), f"d{cp}", 16)
                        p.wait("scalar", f"d{cp}", t_l)
                        t_s = p.emit("scalar", lambda e, c=c, cp=cp: e.dma_start(
                            out=dst[c * 128:(c + 1) * 128, :], in_=buf[cp][...]), f"d{8 + cp}", 16)
                        p.st[f"{tag}_s{cp}"] = t_s

                pb.phase(fn, tag)
        return run

    def make_mprep(variant):
        qsrc = q_buf
        ksrc = cka_in if variant == "a" else ckb_in
        nk = NG if variant == "a" else NH
        qcol0 = 0 if variant == "a" else 64
        kcol0 = 32 if variant == "a" else 96
        mcol0 = 128 if variant == "a" else 160

        def run():
            with ExitStack() as es:
                til = [es.enter_context(nc.sbuf_tensor(f"mp{variant}{i}", [128, 512], BF16)) for i in range(2)]
                sqt = [es.enter_context(nc.sbuf_tensor(f"mq{variant}{i}", [128, 512], BF16)) for i in range(2)]
                nrb = [es.enter_context(nc.sbuf_tensor(f"mn{variant}{i}", [1, 512], F32)) for i in range(2)]
                ps2 = [es.enter_context(nc.psum_tensor(f"mps{variant}{i}", [1, 512], F32)) for i in range(2)]

                def one_tensor(p, src, nh, col0):
                    for h in range(nh):
                        t_hmax = 0
                        for tt in range(NTT):
                            bp = (h * NTT + tt) % 2
                            p.wait("sync", "act", p.st.get(f"mp_sq{bp}", 0))
                            t_ld = p.emit("sync", lambda e, h=h, tt=tt, bp=bp, src=src: e.dma_start(
                                out=til[bp][...], in_=src[h * 128:(h + 1) * 128, tt * 512:(tt + 1) * 512]),
                                f"d{bp}", 16)
                            p.wait("scalar", f"d{bp}", t_ld)
                            p.wait("scalar", "pe", p.st.get(f"mp_mm{bp}", 0))
                            t_sq = p.emit("scalar", lambda e, bp=bp: e.activation(
                                sqt[bp][...], til[bp][...], AF.Square), "act")
                            p.st[f"mp_sq{bp}"] = t_sq
                            p.wait("tensor", "act", t_sq)
                            p.wait("tensor", "act", p.st.get(f"mp_ev{bp}", 0))
                            t_mm = p.emit("tensor", lambda e, bp=bp: e.matmul(
                                ps2[bp][0:1, :], ones_col_b[...], sqt[bp][...],
                                start=True, stop=True), "pe")
                            p.st[f"mp_mm{bp}"] = t_mm
                            p.wait("scalar", "pe", t_mm)
                            p.wait("scalar", "dve", p.st.get(f"mp_red{bp}", 0))
                            t_ev = p.emit("scalar", lambda e, bp=bp: e.activation(
                                nrb[bp][...], ps2[bp][0:1, :], AF.Copy), "act")
                            p.st[f"mp_ev{bp}"] = t_ev
                            p.wait("vector", "act", t_ev)
                            if tt == 0:
                                t_red = p.emit("vector", lambda e, h=h, col0=col0, bp=bp: e.tensor_reduce(
                                    nrm[0:1, col0 + h:col0 + h + 1], nrb[bp][...], AX.X, ALU.max), "dve")
                            else:
                                p.wait("vector", "dve", t_hmax)
                                t_red = p.emit("vector", lambda e, bp=bp: e.tensor_reduce(
                                    nrm[0:1, 192:193], nrb[bp][...], AX.X, ALU.max), "dve")
                                p.wait("vector", "dve", t_red)
                                t_red = p.emit("vector", lambda e, h=h, col0=col0: e.tensor_tensor(
                                    nrm[0:1, col0 + h:col0 + h + 1], nrm[0:1, col0 + h:col0 + h + 1],
                                    nrm[0:1, 192:193], ALU.max), "dve")
                            t_hmax = t_red
                            p.st[f"mp_red{bp}"] = t_red

                def fn(p):
                    one_tensor(p, qsrc, NH, qcol0)
                    one_tensor(p, ksrc, nk, kcol0)
                    a1 = nrm[0:1, 200:232]
                    a2 = nrm[0:1, 232:264] if False else nrm[0:1, 192 + 8:232] if False else None
                    t0 = p.st[f"mp_red0"]
                    t1 = p.st[f"mp_red1"]
                    p.wait("scalar", "dve", max(t0, t1))
                    tq = p.emit("scalar", lambda e: e.activation(a1, nrm[0:1, qcol0:qcol0 + 32], AF.Ln), "act")
                    if variant == "a":
                        tk = p.emit("scalar", lambda e: e.activation(
                            nrm[0:1, 192:200], nrm[0:1, kcol0:kcol0 + 8], AF.Ln), "act")
                        tcp = 0
                        for rr in range(4):
                            p.wait("vector", "act", tk)
                            tcp = p.emit("vector", lambda e, rr=rr: e.tensor_copy(
                                nrm.ap()[0:1, 232:264].rearrange("u (g r) -> u g r", r=4)[:, :, rr],
                                nrm[0:1, 192:200]), "dve")
                        p.wait("vector", "act", tq)
                        p.wait("vector", "dve", tcp)
                        t_add = p.emit("vector", lambda e: e.tensor_tensor(
                            a1, a1, nrm[0:1, 232:264], ALU.add), "dve")
                    else:
                        tk = p.emit("scalar", lambda e: e.activation(
                            nrm[0:1, 232:264], nrm[0:1, kcol0:kcol0 + 32], AF.Ln), "act")
                        p.wait("vector", "act", tk)
                        p.wait("vector", "act", tq)
                        t_add = p.emit("vector", lambda e: e.tensor_tensor(
                            a1, a1, nrm[0:1, 232:264], ALU.add), "dve")
                    p.wait("scalar", "dve", t_add)
                    t_exp = p.emit("scalar", lambda e: e.activation(a1, a1, AF.Exp, scale=0.5), "act")
                    p.wait("vector", "act", t_exp)
                    t_m = p.emit("vector", lambda e, mcol0=mcol0: e.tensor_scalar(
                        nrm[0:1, mcol0:mcol0 + 32], a1, -1.01, None, ALU.mult), "dve")
                    p.st[f"negM_{variant}"] = t_m

                pb.phase(fn, f"mprep_{variant}")
        return run

    phases = [
        ("setup", None),
        ("ln1", make_ln("xT", xT, "ln1", out_sb=big_sb)),
        ("ka", make_proj("ka", wk_t, 0, NG, "rope", cos_i=ropek_cos_i, sin_i=ropek_sin_i,
                         out_dram=cka_in, bias_i=bk_i, bias_n=1024)),
        ("va", make_proj("va", wv_t, 0, NG, "clip", out_dram=vstage, do_clip=False,
                         bias_i=bv_i, bias_n=1024)),
        ("vta", make_vtr(NG, vstage, cva_in, [(cka_in, cka_out), (cva_in, cva_out)], "vta")),
        ("qa", make_proj("qa", wq_t, 0, NH, "rope", cos_i=ropeq_cos_i, sin_i=ropeq_sin_i,
                         out_dram=q_buf, bias_i=bq_i, bias_n=4096)),
        ("mpa", make_mprep("a")),
        ("attn_a", make_attn("a")),
        ("out_a", make_proj("out_a", wo_t, 0, 32, "resid", resid_in=xT, resid_out=r1,
                            bias_i=bo_i, bias_n=4096)),
        ("ln2", make_ln("r1", r1, "ln2", out_sb=big_sb)),
        ("ffn1a", make_ffn_a("f1")),
        ("ffn1b", make_ffn_b(r1, r2, "f1b")),
        ("ln3", make_ln("r2", r2, "ln3", out_sb=big_sb)),
        ("kb", make_proj("kb", wqkv_t, 32, 32, "clip", out_dram=kstage)),
        ("klnb", make_ln("kstage", kstage, "klnb", src_f32=False,
                         affine=(klnw_c, klnb_c), out_dram=ckb_in)),
        ("vb", make_proj("vb", wqkv_t, 64, 32, "clip", out_dram=vstage)),
        ("vtb", make_vtr(NH, vstage, cvb_in, [(ckb_in, ckb_out), (cvb_in, cvb_out)], "vtb")),
        ("qb", make_proj("qb", wqkv_t, 0, 32, "clip", out_dram=qstage)),
        ("qlnb", make_ln("qstage", qstage, "qlnb", src_f32=False,
                         affine=(qlnw_c, qlnb_c), out_dram=q_buf)),
        ("mpb", make_mprep("b")),
        ("attn_b", make_attn("b")),
        ("out_b", make_proj("out_b", wout_t, 0, 32, "resid", resid_in=r2, resid_out=r3)),
        ("ln4", make_ln("r3", r3, "ln4", out_sb=big_sb)),
        ("ffn2a", make_ffn_a("f2")),
        ("ffn2b", make_ffn_b(r3, yT, "f2b")),
    ]

    resid_after = [xT, xT, xT, xT, xT, xT, xT, xT, xT, r1, r1, r1, r2, r2, r2,
                   r2, r2, r2, r2, r2, r2, r3, r3, r3, r3]

    nrun = min(upto, len(phases))
    for i in range(nrun):
        name, fn = phases[i]
        if name == "setup":
            with nc.psum_tensor("ps_set", [128, 16], F32) as ps:
                pb.phase(ph_setup_fn(ps), "setup")
        else:
            fn()
    if nrun < len(phases):
        make_copy(resid_after[nrun], yT, "fincp")()
    if dump is not None:
        if dump[0] == "big_sb":
            def dump_big(p):
                for c in range(NC):
                    t_s = p.emit("scalar", lambda e, c=c: e.dma_start(
                        out=dump_out[c * 128:(c + 1) * 128, :], in_=big_sb[:, c, :]), "st", 16)
                p.wait("scalar", "st", p.tag("st"))
            pb.phase(dump_big, "dump_big")
        elif dump[0] == "zp_sb":
            def dump_zp(p):
                t_s = p.emit("scalar", lambda e: e.dma_start(out=dump_out[...], in_=zp_sb[...]), "st", 16)
                p.wait("scalar", "st", p.tag("st"))
            pb.phase(dump_zp, "dump_zp")
        else:
            src = dram_map[dump[0]]
            make_copy(src, dump_out, "dmp", rows=dump[1][0], dt=dump[2], width=dump[1][1])()

    def fin(p):
        for s_ in ("st", "lx", "lw", "cc") + tuple(f"d{i}" for i in range(10)):
            p.wait("sync", s_, p.tag(s_))
    pb.phase(fin, "fin_wait")
    return nc



def _tile_w(w_in_out):
    i_, o_ = w_in_out.shape
    t = w_in_out.reshape(i_ // 128, 128, o_ // 128, 128)
    t = t.transpose(2, 1, 0, 3).reshape(o_ // 128, 128, (i_ // 128) * 128)
    return np.ascontiguousarray(t)


def prepare_in_maps(inputs):
    x = np.asarray(inputs["L0_attn_a_hidden_states"])
    f32 = np.float32

    wq_t = _tile_w(np.asarray(inputs["L0_attn_a_Wq"]))
    wk_t = _tile_w(np.asarray(inputs["L0_attn_a_Wk"]))
    wv_t = _tile_w(np.asarray(inputs["L0_attn_a_Wv"]))
    wo_t = _tile_w(np.asarray(inputs["L0_attn_a_Wo"]))
    wqkv_t = _tile_w(np.ascontiguousarray(np.asarray(inputs["L0_attn_b_Wqkv"]).T))
    wout_t = _tile_w(np.ascontiguousarray(np.asarray(inputs["L0_attn_b_Wout"]).T))
    wbase_t = _tile_w(np.ascontiguousarray(np.asarray(inputs["L0_ffn_W_base"]).T))
    lora_a = np.asarray(inputs["L0_ffn_lora_A"])
    lora_a_t = np.ascontiguousarray(
        lora_a.reshape(128, 4096).T.reshape(32, 128, 128).transpose(1, 0, 2))
    lora_b = np.asarray(inputs["L0_ffn_lora_B"])
    lora_b_f = np.ascontiguousarray(lora_b.transpose(0, 2, 1).reshape(128, 4096))
    router_w = np.asarray(inputs["L0_ffn_router_W"])
    router_t = np.ascontiguousarray(
        router_w.T.reshape(32, 128, 16).transpose(1, 0, 2))

    def row(v, n):
        return np.ascontiguousarray(np.asarray(v, dtype=f32).reshape(1, n))

    def col32(v):
        return np.ascontiguousarray(np.asarray(v, dtype=f32).reshape(32, 128).T)

    shared = dict(
        wq_t=wq_t, wk_t=wk_t, wv_t=wv_t, wo_t=wo_t, wqkv_t=wqkv_t,
        wout_t=wout_t, wbase_t=wbase_t, lora_a_t=lora_a_t, lora_b_f=lora_b_f,
        router_t=router_t,
        bq=row(inputs["L0_attn_a_bq"], 4096), bk=row(inputs["L0_attn_a_bk"], 1024),
        bv=row(inputs["L0_attn_a_bv"], 1024), bo=row(inputs["L0_attn_a_bo"], 4096),
        b_base=row(inputs["L0_ffn_b_base"], 4096),
        qlnw_col=col32(inputs["L0_attn_b_q_ln_w"]), qlnb_col=col32(inputs["L0_attn_b_q_ln_b"]),
        klnw_col=col32(inputs["L0_attn_b_k_ln_w"]), klnb_col=col32(inputs["L0_attn_b_k_ln_b"]),
        router_b=row(inputs["L0_ffn_router_b"], 16),
        prior=row(inputs["L0_ffn_domain_prior"], 16),
        active=np.ascontiguousarray(np.asarray(inputs["L0_ffn_active"], dtype=np.int32).reshape(1, 4)),
        iota16=np.arange(16, dtype=f32).reshape(1, 16),
        eye128=np.eye(128, dtype=f32),
    )
    emat = np.zeros((16, 128), dtype=f32)
    for e_ in range(16):
        emat[e_, e_ * 8:(e_ + 1) * 8] = 1.0
    shared["emat"] = emat

    half = 64
    inv_freq = 1.0 / (10000.0 ** (np.arange(half, dtype=np.float64) / half))
    m_ = np.arange(1, 33, dtype=np.float64) * (8.0 / 32)
    sl = 1.0 / (2.0 ** m_)

    al_k = np.zeros((128, 32, 8, 4), dtype=np.float64)
    pp = np.arange(128)
    for h in range(32):
        for j in range(4):
            for r in range(2):
                for d_ in range(4):
                    kpos = (2 * j + r) * 512 + d_ * 128 + pp
                    al_k[:, h, j * 2 + r, d_] = sl[h] * kpos
    shared["alibi_k"] = np.ascontiguousarray(al_k.reshape(128, 1024).astype(f32))

    NEG = -1.0e30

    def masks_for(parity):
        mk = np.zeros((128, 8, 512), dtype=f32)
        pcol = np.arange(128).reshape(128, 1)
        ccol = np.arange(512).reshape(1, 512)
        for r in range(2):
            for d_ in range(4):
                if r == parity:
                    mk[:, r * 4 + d_, :] = np.where(d_ * 128 + pcol <= ccol, 0.0, NEG)
                else:
                    mk[:, r * 4 + d_, :] = 0.0 if parity == 1 else NEG
        return np.ascontiguousarray(mk.reshape(128, 8 * 512))

    in_maps = []
    for core in range(8):
        b = core // 2
        parity = core % 2
        blocks = [2 * i + parity for i in range(4)]
        own_pos = np.concatenate([np.arange(g * 512, (g + 1) * 512) for g in blocks])
        xT_core = np.ascontiguousarray(x[b][own_pos, :].T.astype(f32))
        ang = own_pos.reshape(1, T).astype(np.float64) * inv_freq.reshape(half, 1)
        cos_t = np.cos(ang)
        sin_t = np.sin(ang)
        m = dict(shared)
        m.update(
            xT=xT_core,
            ropeq_cos=np.ascontiguousarray((cos_t * SCALE).astype(f32)),
            ropeq_sin=np.ascontiguousarray((sin_t * SCALE).astype(f32)),
            ropek_cos=np.ascontiguousarray(cos_t.astype(f32)),
            ropek_sin=np.ascontiguousarray(sin_t.astype(f32)),
            alibi_q=np.ascontiguousarray((-sl.reshape(32, 1) * own_pos.reshape(1, T)).astype(f32)),
            alibi_qb=np.ascontiguousarray(np.broadcast_to(
                (-sl.reshape(32, 1, 1) * own_pos.reshape(1, 1, T)).astype(f32),
                (32, 128, T)).reshape(32, 128 * T)),
            masks=masks_for(parity),
        )
        in_maps.append(m)
    return in_maps


def assemble_output(results, key="yT"):
    out = np.zeros((B, S, D), dtype=np.float32)
    for core in range(8):
        b = core // 2
        parity = core % 2
        yc = results[core][key]
        for i in range(4):
            g = 2 * i + parity
            out[b, g * 512:(g + 1) * 512, :] = yc[:, i * 512:(i + 1) * 512].T
    return out


_BUILD_CFG = {"upto": 99, "dump": None, "debug_ln": False}


def kernel(**inputs):
    nc = build_program(upto=_BUILD_CFG["upto"], dump=_BUILD_CFG["dump"],
                       debug_ln=_BUILD_CFG.get("debug_ln", False))
    in_maps = prepare_in_maps(inputs)
    res = run_bass_kernel_spmd(nc, in_maps, core_ids=list(range(8)))
    if _BUILD_CFG["dump"] is not None:
        return res.results
    return assemble_output(res.results)
